# version 06
# baseline (speedup 1.0000x reference)
import numpy as np
from contextlib import ExitStack

import concourse.bass as bass
import concourse.mybir as mybir
from concourse.bass_utils import run_bass_kernel_spmd

F32 = mybir.dt.float32
I32 = mybir.dt.int32
F16 = mybir.dt.float16
BF16 = mybir.dt.bfloat16
Alu = mybir.AluOpType
AF = mybir.ActivationFunctionType
AX = mybir.AxisListType

D = 1024
S = 2048
H = 16
I3 = 3072
KD = D // 128
KI = I3 // 128
TC = S // 512
NKT = S // 128
QMAX_A = 32767.0
EPS = 1e-5
TCW = KD * 512


def TX(k, tc):
    return tc * TCW + k * 512
PHASE_LIMIT = 99


class Sem:
    def __init__(self, nc, stack, name):
        self.h = stack.enter_context(nc.semaphore(name))
        self.n = 0

    def inc(self, inst, k=1):
        inst.then_inc(self.h, k)
        self.n += k
        return self.n


def build_program():
    nc = bass.Bass("TRN2", target_bir_lowering=False)
    gstack = ExitStack()
    G = gstack.enter_context

    xin = nc.dram_tensor("xin", [128, KD * S], F32, kind="ExternalInput").ap()
    wq_h = nc.dram_tensor("wq", [128, KD * D], F32, kind="ExternalInput").ap()
    wk_h = nc.dram_tensor("wk", [128, KD * D], F32, kind="ExternalInput").ap()
    wv_h = nc.dram_tensor("wv", [128, KD * D], F32, kind="ExternalInput").ap()
    wm_h = nc.dram_tensor("wm", [128, KD * D], F32, kind="ExternalInput").ap()
    bqc = nc.dram_tensor("bqc", [128, KD], F32, kind="ExternalInput").ap()
    bkc = nc.dram_tensor("bkc", [128, KD], F32, kind="ExternalInput").ap()
    bmc = nc.dram_tensor("bmc", [128, KD], F32, kind="ExternalInput").ap()
    bvr = nc.dram_tensor("bvr", [1, D], F32, kind="ExternalInput").ap()
    wupT = nc.dram_tensor("wupT", [128, KD * I3], F32, kind="ExternalInput").ap()
    wgT = nc.dram_tensor("wgT", [128, KD * I3], F32, kind="ExternalInput").ap()
    wdT = nc.dram_tensor("wdT", [128, KI * D], F32, kind="ExternalInput").ap()
    wup_src = nc.dram_tensor("wup_src", [128, 24576], F32, kind="ExternalInput").ap()
    wg_src = nc.dram_tensor("wg_src", [128, 24576], F32, kind="ExternalInput").ap()
    wd_src = nc.dram_tensor("wd_src", [128, 24576], F32, kind="ExternalInput").ap()
    yout = nc.dram_tensor("yout", [128, KD * S], F32, kind="ExternalOutput").ap()
    ln1_sp = nc.dram_tensor("ln1_sp", [128, KD * S], F32, kind="Internal").ap()
    up_sp = nc.dram_tensor("up_sp", [128, KI * S], F16, kind="Internal").ap()
    gate_sp = nc.dram_tensor("gate_sp", [128, KI * S], F16, kind="Internal").ap()
    cc_bufs = [(nc.dram_tensor(f"cc{i}i", [1, 2], F32, kind="Internal").ap(),
                nc.dram_tensor(f"cc{i}o", [1, 2], F32, kind="Internal",
                               addr_space="Shared").ap()) for i in range(4)]

    v, sc, te, gp, sy = nc.vector, nc.scalar, nc.tensor, nc.gpsimd, nc.sync

    p = ExitStack()
    ld = Sem(nc, gstack, "ld")
    ldg = Sem(nc, gstack, "ldg")
    lds = Sem(nc, gstack, "lds")
    st = Sem(nc, gstack, "st")
    sp0 = Sem(nc, gstack, "sp0")
    sp1 = Sem(nc, gstack, "sp1")
    pa = Sem(nc, gstack, "pa")
    pb = Sem(nc, gstack, "pb")
    pc = Sem(nc, gstack, "pc")
    pd = Sem(nc, gstack, "pd")
    sv = Sem(nc, gstack, "sv")
    ss = Sem(nc, gstack, "ss")
    tt = Sem(nc, gstack, "tt")
    gg = Sem(nc, gstack, "gg")
    gw = Sem(nc, gstack, "gw")
    ccs = Sem(nc, gstack, "ccs")

    def dma(out, in_):
        return ld.inc(sy.dma_start(out=out, in_=in_), 16)

    def dma_g(out, in_):
        return ldg.inc(gp.dma_start(out=out, in_=in_), 16)

    def dma_s(out, in_):
        return lds.inc(sc.dma_start(out=out, in_=in_), 16)

    def dma_out(out, in_):
        return st.inc(sy.dma_start(out=out, in_=in_), 16)

    def barrier():
        engs = [te, v, sc, gp, sy]
        base = gg.n
        for e in engs:
            gg.inc(e.nop(), 1)
        for e in engs:
            e.wait_ge(gg.h, base + len(engs))

    def debug_exit(src_ap, nfree, is_f32):
        barrier()
        dbg = nc.sbuf_tensor("dbgbuf", [128, 2048], F32, side="right")
        dbgh = dbg.__enter__()
        nch = min(nfree, KD * S) // 2048
        for c in range(nch):
            if c > 0:
                sy.wait_ge(sv.h, sv.n)
            v.wait_ge(st.h, st.n)
            v.tensor_copy(dbgh[:], src_ap[:, c * 2048:(c + 1) * 2048])
            v.drain()
            n_ = sv.inc(v.nop())
            sy.wait_ge(sv.h, n_)
            dma_out(yout[:, c * 2048:(c + 1) * 2048], dbgh[:])
        sy.wait_ge(st.h, st.n)

    small = G(nc.sbuf_tensor("small", [128, 40], F32))
    ccst = G(nc.sbuf_tensor("ccst", [1, 8], F32))
    grow = G(nc.sbuf_tensor("grow", [1, 256], F32))
    ones_col = G(nc.sbuf_tensor("ones_col", [128, 2], BF16))
    bstage = G(nc.sbuf_tensor("bstage", [128, 3 * KD], F32))

    pA = G(nc.psum_tensor("pA", [128, 512], F32))
    pB = G(nc.psum_tensor("pB", [128, 512], F32))
    pC = G(nc.psum_tensor("pC", [128, 512], F32))
    pD = G(nc.psum_tensor("pD", [128, 512], F32))
    pV0 = G(nc.psum_tensor("pV0", [66, 512], F32))
    pV1 = G(nc.psum_tensor("pV1", [66, 512], F32))
    pS1 = G(nc.psum_tensor("pS1", [1, 512], F32))
    pS2 = G(nc.psum_tensor("pS2", [1, 512], F32))

    dma(bstage[:, 0:KD], bqc)
    dma(bstage[:, KD:2 * KD], bkc)
    dma(bstage[:, 2 * KD:3 * KD], bmc)
    v.memset(ones_col[:], 1.0)
    v.memset(small[:, 34:35], EPS)
    v.memset(ccst[:], 0.0)
    v.drain()
    sv.inc(v.nop())

    attn_stack = ExitStack()
    attnT = attn_stack.enter_context(
        nc.sbuf_tensor("attnT", [128, KD * S], BF16, side="right"))
    xq = attnT

    qkv_stack = ExitStack()
    Q_ = qkv_stack.enter_context
    qT = Q_(nc.sbuf_tensor("qT", [128, KD * S], BF16))
    kT = Q_(nc.sbuf_tensor("kT", [128, KD * S], BF16))
    V66 = Q_(nc.sbuf_tensor("V66", [128, NKT * 1056], BF16))

    x_stack = ExitStack()
    X_ = x_stack.enter_context
    xbf = X_(nc.sbuf_tensor("xbf", [128, KD * S], BF16))
    wproj = X_(nc.sbuf_tensor("wproj", [128, KD * D], BF16))
    wstage = X_(nc.sbuf_tensor("wstage", [128, 2 * 1024], F32))
    bvrep = X_(nc.sbuf_tensor("bvrep", [128, D], F32))

    dma(bvrep[:], bvr.unsqueeze(1).broadcast_to([1, 128, D]))
    sy.wait_ge(ld.h, ld.n)

    def load_weight_bf16(src_ap, dest_sb, nfree, csz=1024):
        mile = []
        for c in range(nfree // csz):
            par = c % 2
            buf = wstage[:, par * 1024: par * 1024 + csz]
            eng = sy if par == 0 else sc
            sem = pa if par == 0 else pb
            if c >= 2:
                eng.wait_ge(sv.h, mile[c - 2])
            else:
                eng.wait_ge(sv.h, sv.n)
            n_ = sem.inc((sy if par == 0 else sc).dma_start(
                out=buf, in_=src_ap[:, c * csz:(c + 1) * csz]), 16)
            v.wait_ge(sem.h, n_)
            v.tensor_copy(dest_sb[:, c * csz:(c + 1) * csz], buf)
            v.drain()
            mile.append(sv.inc(v.nop()))
        return sv.n

    load_weight_bf16(xin, xbf, KD * S)

    def qk_proj(wsrc, bias_col, dest):
        v.wait_ge(tt.h, tt.n)
        wready = load_weight_bf16(wsrc, wproj, KD * D)
        te.wait_ge(sv.h, wready)
        ss_b = ss.n
        te.wait_ge(ss.h, ss_b)
        sc.wait_ge(ld.h, 4 * 16)
        tt_b = tt.n
        g = 0
        for do in range(KD):
            for tc in range(TC):
                ps = pA if g % 2 == 0 else pB
                if g >= 2:
                    te.wait_ge(ss.h, ss_b + g - 1)
                for ki in range(KD):
                    mi = te.matmul(ps[:],
                              wproj[:, ki * D + do * 128: ki * D + do * 128 + 128],
                              xbf[:, TX(ki, tc): TX(ki, tc) + 512],
                              start=(ki == 0), stop=(ki == KD - 1))
                tt.inc(mi)
                sc.wait_ge(tt.h, tt_b + g + 1)
                ss.inc(sc.activation(
                    dest[:, TX(do, tc): TX(do, tc) + 512], ps[:],
                    AF.Identity,
                    bias=bstage[:, bias_col + do: bias_col + do + 1], scale=1.0))
                g += 1

    qk_proj(wq_h, 0, qT)
    qk_proj(wk_h, KD, kT)

    v.wait_ge(tt.h, tt.n)
    wready = load_weight_bf16(wv_h, wproj, KD * D)
    gp.wait_ge(sv.h, sv.n)
    gw.inc(gp.memset(V66[:], 1.0))
    te.wait_ge(sv.h, wready)
    te.wait_ge(gw.h, gw.n)
    te.wait_ge(ss.h, ss.n)
    sv_b = sv.n
    tt_b = tt.n
    g = 0
    for tt_i in range(NKT):
        for dc in range(2):
            ps = pA if g % 2 == 0 else pB
            if g >= 2:
                te.wait_ge(sv.h, sv_b + g - 1)
            for ki in range(KD):
                mi = te.matmul(ps[:],
                          xbf[:, TX(ki, tt_i // 4) + (tt_i % 4) * 128:
                                 TX(ki, tt_i // 4) + (tt_i % 4) * 128 + 128],
                          wproj[:, ki * D + dc * 512: ki * D + dc * 512 + 512],
                          start=(ki == 0), stop=(ki == KD - 1))
            tt.inc(mi)
            v.wait_ge(tt.h, tt_b + g + 1)
            out_ap = V66[:, tt_i * 1056 + dc * 8 * 66: tt_i * 1056 + (dc * 8 + 8) * 66]
            out_ap = out_ap.rearrange("p (h c) -> p h c", c=66)[:, :, 0:64]
            ps3 = ps[:].rearrange("p (h c) -> p h c", c=64)
            bv3 = bvrep[:, dc * 512:(dc + 1) * 512].rearrange("p (h c) -> p h c", c=64)
            v.tensor_tensor(out_ap, ps3, bv3, op=Alu.add)
            v.drain()
            sv.inc(v.nop())
            g += 1

    barrier()
    if PHASE_LIMIT <= 1:
        debug_exit(qT[:], KD * S, False)
        return nc, gstack
    x_stack.close()

    a_stack = ExitStack()
    A_ = a_stack.enter_context
    probs = A_(nc.sbuf_tensor("probs", [128, 2 * NKT * 512], BF16))
    recband = A_(nc.sbuf_tensor("recband", [64, 2 * 512], F32))
    pvrec = A_(nc.sbuf_tensor("pvrec", [1, 2 * 512], F32))

    sbanks = [pA, pB, pC, pD]
    pv_norm_sv = [0, 0]
    pv_tt_par = [tt.n, tt.n]
    rec_sv_par = [sv.n, sv.n]
    ss_b4 = ss.n
    gexp = [0]
    exp_mile = []
    iters = [(h, qc) for h in range(H) for qc in range(TC)]

    def p4_scores(it):
        h, qc = iters[it]
        ko, po = h // 2, (h % 2) * 64
        par = it % 2
        pbase = par * (NKT * 512)
        kmax = 4 * (qc + 1)
        sc.wait_ge(tt.h, pv_tt_par[par])
        for kb in range(kmax):
            m = gexp[0] + kb
            psb = sbanks[m % 4]
            if m >= 4:
                te.wait_ge(ss.h, ss_b4 + m - 3)
            kx = TX(ko, kb // 4) + (kb % 4) * 128
            mi = te.matmul(psb[:],
                      kT[po:po + 64, kx: kx + 128],
                      qT[po:po + 64, TX(ko, qc): TX(ko, qc) + 512],
                      start=True, stop=True)
            tt.inc(mi)
            sc.wait_ge(tt.h, tt.n)
            ss.inc(sc.activation(probs[:, pbase + kb * 512: pbase + (kb + 1) * 512],
                                 psb[:], AF.Exp, bias=0.0, scale=0.125))
        gexp[0] += kmax
        exp_mile.append(ss_b4 + gexp[0])
        gp.wait_ge(ss.h, exp_mile[-1])
        for kb in [kb for kb in range(kmax) if kb * 128 + 127 > qc * 512]:
            mi_g = gp.affine_select(
                probs[:, pbase + kb * 512: pbase + (kb + 1) * 512],
                probs[:, pbase + kb * 512: pbase + (kb + 1) * 512],
                pattern=[[1, 512]], compare_op=Alu.is_ge, fill=0.0,
                base=qc * 512 - kb * 128, channel_multiplier=-1)
        return gw.inc(mi_g)

    def p4_pv(it, gw_mask):
        h, qc = iters[it]
        ko, po = h // 2, (h % 2) * 64
        par = it % 2
        pbase = par * (NKT * 512)
        kmax = 4 * (qc + 1)
        ps_pv = pV0 if par == 0 else pV1
        te.wait_ge(gw.h, gw_mask)
        te.wait_ge(sv.h, pv_norm_sv[par])
        for kb in range(kmax):
            mi = te.matmul(ps_pv[:],
                      V66[:, kb * 1056 + h * 66: kb * 1056 + h * 66 + 66],
                      probs[:, pbase + kb * 512: pbase + (kb + 1) * 512],
                      start=(kb == 0), stop=(kb == kmax - 1))
        pv_tt_par[par] = tt.inc(mi)
        rsl = pvrec[0:1, par * 512: par * 512 + 512]
        v.wait_ge(tt.h, pv_tt_par[par])
        v.reciprocal(rsl, ps_pv[64:65, :])
        v.drain()
        nrec = sv.inc(v.nop())
        gp.wait_ge(sv.h, nrec)
        gp.wait_ge(sv.h, rec_sv_par[par])
        rb = recband[:, par * 512: par * 512 + 512]
        nb = dma_g(rb, rsl.unsqueeze(1).broadcast_to([1, 64, 512]))
        v.wait_ge(ldg.h, nb)
        v.tensor_tensor(
            attnT[po:po + 64, TX(ko, qc): TX(ko, qc) + 512],
            ps_pv[0:64, :], rb, op=Alu.mult)
        v.drain()
        rec_sv_par[par] = pv_norm_sv[par] = sv.inc(v.nop())

    gw_prev = p4_scores(0)
    for it in range(len(iters)):
        gw_next = p4_scores(it + 1) if it + 1 < len(iters) else None
        p4_pv(it, gw_prev)
        gw_prev = gw_next

    barrier()
    if PHASE_LIMIT <= 2:
        debug_exit(attnT[:], KD * S, False)
        return nc, gstack
    a_stack.close()
    qkv_stack.close()

    p5_stack = ExitStack()
    P5 = p5_stack.enter_context
    wproj2 = P5(nc.sbuf_tensor("wproj2", [128, KD * D], BF16))
    wstage2 = P5(nc.sbuf_tensor("wstage2", [128, 2048], F32))
    rbuf = P5(nc.sbuf_tensor("rbuf", [128, KD * S], F32))
    xres = P5(nc.sbuf_tensor("xres", [128, 2 * TCW], F32))
    rbf = P5(nc.sbuf_tensor("rbf", [128, 512], BF16))
    rsq = P5(nc.sbuf_tensor("rsq", [128, 512], BF16))
    arep = P5(nc.sbuf_tensor("arep", [128, 512], F32))
    brep = P5(nc.sbuf_tensor("brep", [128, 512], F32))
    q32 = P5(nc.sbuf_tensor("q32", [128, TCW], F32))
    qi32 = P5(nc.sbuf_tensor("qi32", [128, TCW], I32))
    lrow = P5(nc.sbuf_tensor("lrow", [1, 1536], F32))

    def load_weight_bf16_p5(src_ap, dest_sb, nfree, csz=1024):
        mile = []
        for c in range(nfree // csz):
            par = c % 2
            buf = wstage2[:, par * 1024: par * 1024 + csz]
            eng = sy if par == 0 else sc
            sem = pa if par == 0 else pb
            if c >= 2:
                eng.wait_ge(sv.h, mile[c - 2])
            else:
                eng.wait_ge(sv.h, sv.n)
            n_ = sem.inc(eng.dma_start(out=buf, in_=src_ap[:, c * csz:(c + 1) * csz]), 16)
            v.wait_ge(sem.h, n_)
            v.tensor_copy(dest_sb[:, c * csz:(c + 1) * csz], buf)
            v.drain()
            mile.append(sv.inc(v.nop()))
        return sv.n

    load_weight_bf16_p5(wm_h, wproj2, KD * D)
    te.wait_ge(sv.h, sv.n)

    xmile = []
    for tc in range(TC):
        xeng = sy if tc % 2 == 0 else sc
        xsem = pa if tc % 2 == 0 else pb
        if tc >= 2:
            xeng.wait_ge(sv.h, xmile[tc - 2])
        else:
            xeng.wait_ge(sv.h, sv.n)
        nx = xsem.inc(xeng.dma_start(
            out=xres[:, (tc % 2) * TCW: (tc % 2) * TCW + TCW],
            in_=xin[:, tc * TCW: (tc + 1) * TCW]), 16)
        for do in range(KD):
            ps = pA if do % 2 == 0 else pB
            if do >= 2:
                te.wait_ge(sv.h, sv.n)
            for ki in range(KD):
                mi = te.matmul(ps[:],
                          wproj2[:, ki * D + do * 128: ki * D + do * 128 + 128],
                          attnT[:, TX(ki, tc): TX(ki, tc) + 512],
                          start=(ki == 0), stop=(ki == KD - 1))
            n_grp = tt.inc(mi)
            sc.wait_ge(tt.h, n_grp)
            n_ev = ss.inc(sc.activation(
                rbuf[:, TX(do, tc): TX(do, tc) + 512], ps[:],
                AF.Identity, bias=bstage[:, 2 * KD + do: 2 * KD + do + 1], scale=1.0))
            v.wait_ge(ss.h, n_ev)
            v.wait_ge(xsem.h, nx)
            v.wait_ge(tt.h, tt.n)
            rsl = rbuf[:, TX(do, tc): TX(do, tc) + 512]
            v.tensor_tensor(rsl, rsl,
                            xres[:, (tc % 2) * TCW + do * 512:
                                    (tc % 2) * TCW + do * 512 + 512], op=Alu.add)
            v.drain()
            v.tensor_copy(rbf[:], rsl)
            v.tensor_tensor(rsq[:], rsl, rsl, op=Alu.mult)
            v.drain()
            n_r = sv.inc(v.nop())
            te.wait_ge(sv.h, n_r)
            te.matmul(pS1[:], ones_col[:, 0:1], rbf[:],
                      start=(do == 0), stop=(do == KD - 1))
            mi = te.matmul(pS2[:], ones_col[:, 0:1], rsq[:],
                      start=(do == 0), stop=(do == KD - 1))
            tt.inc(mi)
        v.wait_ge(tt.h, tt.n)
        mrow = lrow[0:1, 0:512]
        urow = lrow[0:1, 512:1024]
        brow = lrow[0:1, 1024:1536]
        v.tensor_scalar(mrow, pS1[:], 1.0 / D, None, op0=Alu.mult)
        v.drain()
        v.tensor_tensor(urow, mrow, mrow, op=Alu.mult)
        v.drain()
        v.scalar_tensor_tensor(urow, pS2[:], 1.0 / D, urow,
                               op0=Alu.mult, op1=Alu.subtract)
        v.drain()
        n_v = sv.inc(v.nop())
        sc.wait_ge(sv.h, n_v)
        n_s = ss.inc(sc.activation(urow, urow, AF.Sqrt, bias=small[0:1, 34:35], scale=1.0))
        v.wait_ge(ss.h, n_s)
        v.reciprocal(urow, urow)
        v.drain()
        v.scalar_tensor_tensor(brow, mrow, -1.0, urow, op0=Alu.mult, op1=Alu.mult)
        v.drain()
        n_ab = sv.inc(v.nop())
        gp.wait_ge(sv.h, n_ab)
        dma_g(arep[:], urow.unsqueeze(1).broadcast_to([1, 128, 512]))
        nbb = dma_g(brep[:], brow.unsqueeze(1).broadcast_to([1, 128, 512]))
        v.wait_ge(ldg.h, nbb)
        win = rbuf[:, tc * TCW: (tc + 1) * TCW]
        win3 = win.rearrange("p (k f) -> p k f", f=512)
        v.tensor_tensor(win3, win3,
                        arep[:].unsqueeze(1).broadcast_to([128, KD, 512]),
                        op=Alu.mult)
        v.drain()
        v.tensor_tensor(win3, win3,
                        brep[:].unsqueeze(1).broadcast_to([128, KD, 512]),
                        op=Alu.add)
        v.drain()
        v.tensor_reduce(small[:, tc:tc + 1], win, axis=AX.X, op=Alu.max,
                        apply_absolute_value=True)
        v.drain()
        xmile.append(sv.inc(v.nop()))

    v.tensor_reduce(small[:, 8:9], small[:, 0:TC], axis=AX.X, op=Alu.max)
    v.drain()
    n_m = sv.inc(v.nop())
    sy.wait_ge(sv.h, n_m)
    ng = dma(grow[0:1, 0:128], small[:, 8:9])
    v.wait_ge(ld.h, ng)
    v.tensor_reduce(ccst[0:1, 0:1], grow[0:1, 0:128], axis=AX.X, op=Alu.max)
    v.drain()
    sv.inc(v.nop())

    def exchange(idx, nvals):
        ci, co = cc_bufs[idx]
        sy.wait_ge(sv.h, sv.n)
        nin = dma(ci[0:1, 0:2], ccst[0:1, 0:2])
        gp.wait_ge(ld.h, nin)
        ccs.inc(gp.collective_compute(
            "AllReduce", Alu.max, replica_groups=[[0, 1, 2, 3, 4, 5, 6, 7]],
            ins=[ci], outs=[co]))
        sy.wait_ge(ccs.h, ccs.n)
        nout = dma(ccst[0:1, 0:2], co[0:1, 0:2])
        v.wait_ge(ld.h, nout)

    def make_scale(src_ap, qmax, s_col, r_col):
        v.tensor_scalar(ccst[0:1, 4:5], src_ap, 1.0 / qmax, 1e-12,
                        op0=Alu.mult, op1=Alu.max)
        v.drain()
        v.reciprocal(ccst[0:1, 5:6], ccst[0:1, 4:5])
        v.drain()
        n_ = sv.inc(v.nop())
        sy.wait_ge(sv.h, n_)
        dma(small[:, s_col:s_col + 1],
            ccst[0:1, 4:5].unsqueeze(1).broadcast_to([1, 128, 1]))
        n2 = dma(small[:, r_col:r_col + 1],
                 ccst[0:1, 5:6].unsqueeze(1).broadcast_to([1, 128, 1]))
        v.wait_ge(ld.h, n2)
        sv.inc(v.nop())

    exchange(0, 1)
    make_scale(ccst[0:1, 0:1], QMAX_A, 16, 17)

    for tc in range(TC):
        rsl = rbuf[:, tc * TCW: (tc + 1) * TCW]
        v.tensor_scalar(q32[:], rsl, small[:, 17:18], None, op0=Alu.mult)
        v.drain()
        v.tensor_copy(qi32[:], q32[:])
        v.drain()
        v.tensor_scalar(xq[:, tc * TCW: (tc + 1) * TCW], qi32[:], small[:, 16:17],
                        None, op0=Alu.mult)
        v.drain()
    n_xq = sv.inc(v.nop())
    sy.wait_ge(sv.h, n_xq)
    n_ln1sp = dma_out(ln1_sp, rbuf[:])
    sy.wait_ge(st.h, n_ln1sp)

    barrier()
    if PHASE_LIMIT <= 3:
        debug_exit(rbuf[:], KD * S, True)
        return nc, gstack
    p5_stack.close()

    ffn_stack = ExitStack()
    FS = ffn_stack.enter_context
    wx_bf = FS(nc.sbuf_tensor("wx_bf", [128, KD * I3], BF16))

    def make_dq_bufs(dstack, osz, tag):
        DQ = dstack.enter_context
        return dict(
            sblk_s=DQ(nc.sbuf_tensor(f"sblk_s{tag}", [128, 768], F32)),
            sblk_r=DQ(nc.sbuf_tensor(f"sblk_r{tag}", [128, 768], F32)),
            stage0=DQ(nc.sbuf_tensor(f"stage0{tag}", [128, 3072], F32)),
            stage1=DQ(nc.sbuf_tensor(f"stage1{tag}", [128, 3072], F32)),
            wtb0=DQ(nc.sbuf_tensor(f"wtb0{tag}", [128, osz], F32)),
            wtb1=DQ(nc.sbuf_tensor(f"wtb1{tag}", [128, osz], F32)),
            rrep0=DQ(nc.sbuf_tensor(f"rrep0{tag}", [128, osz], F32)),
            rrep1=DQ(nc.sbuf_tensor(f"rrep1{tag}", [128, osz], F32)),
        )

    def dequant_weight(src_ap, wT_ap, dest_sb, n_it, osz, bufs, dest_wait_tt=None):
        sblk_s, sblk_r = bufs["sblk_s"], bufs["sblk_r"]
        stages = [bufs["stage0"], bufs["stage1"]]
        wtbs = [bufs["wtb0"], bufs["wtb1"]]
        rreps = [bufs["rrep0"], bufs["rrep1"]]
        o4 = osz // 4
        mile = []
        for ch in range(8):
            par = ch % 2
            buf = stages[par]
            eng = sy if par == 0 else sc
            sem = pa if par == 0 else pb
            if ch >= 2:
                eng.wait_ge(sv.h, mile[ch - 2])
            else:
                eng.wait_ge(sv.h, sv.n)
            n_ = sem.inc((sy if par == 0 else sc).dma_start(
                out=buf[:], in_=src_ap[:, ch * 3072:(ch + 1) * 3072]), 16)
            v.wait_ge(sem.h, n_)
            v.tensor_reduce(sblk_s[:, ch * 96:(ch + 1) * 96],
                            buf[:].rearrange("p (b c) -> p b c", c=32),
                            axis=AX.X, op=Alu.max, apply_absolute_value=True)
            v.drain()
            mile.append(sv.inc(v.nop()))
        v.tensor_scalar(sblk_s[:], sblk_s[:], 1.0 / 7.0, 1e-12,
                        op0=Alu.mult, op1=Alu.max)
        v.drain()
        v.reciprocal(sblk_r[:], sblk_s[:])
        v.drain()
        sv.inc(v.nop())
        tmile = []
        for t in range(n_it):
            par = t % 2
            t3 = 0 if n_it == KD else t // 8
            trow = t if n_it == KD else t % 8
            srep = stages[par]
            wtb = wtbs[par]
            rrep = rreps[par]
            semA = pa if par == 0 else pc
            semB = pb if par == 0 else pd
            if t >= 2:
                sy.wait_ge(sv.h, tmile[t - 2])
                sc.wait_ge(sv.h, tmile[t - 2])
            elif t == 0:
                sy.wait_ge(sv.h, sv.n)
                sc.wait_ge(sv.h, sv.n)
            nA = nB = None
            for s4 in range(4):
                rb = 16 * trow + 4 * s4
                fo = t3 * o4
                srcs = sblk_s[rb:rb + 4, fo:fo + o4].unsqueeze(1).broadcast_to([4, 32, o4])
                srcr = sblk_r[rb:rb + 4, fo:fo + o4].unsqueeze(1).broadcast_to([4, 32, o4])
                nA = semA.inc(sy.dma_start(
                    out=srep[:, s4 * o4:(s4 + 1) * o4], in_=srcs), 16)
                nB = semB.inc(sc.dma_start(
                    out=rrep[:, s4 * o4:(s4 + 1) * o4], in_=srcr), 16)
            nA = semA.inc(sy.dma_start(
                out=wtb[:], in_=wT_ap[:, t * osz:(t + 1) * osz]), 16)
            v.wait_ge(semA.h, nA)
            v.wait_ge(semB.h, nB)
            v.tensor_tensor(wtb[:], wtb[:], rrep[:, 0:osz], op=Alu.mult)
            v.drain()
            v.tensor_copy(wtb[:].bitcast(I32), wtb[:])
            v.drain()
            if t == 0 and dest_wait_tt is not None:
                v.wait_ge(tt.h, dest_wait_tt)
            v.tensor_tensor(dest_sb[:, t * osz:(t + 1) * osz],
                            wtb[:].bitcast(I32), srep[:, 0:osz], op=Alu.mult)
            v.drain()
            tmile.append(sv.inc(v.nop()))

    dq_stack = ExitStack()
    dq_bufs = make_dq_bufs(dq_stack, I3, "a")

    p8_stack = ExitStack()
    P8 = p8_stack.enter_context
    fstage = P8(nc.sbuf_tensor("fstage", [128, 2 * 4 * S], F16))
    maxcols = P8(nc.sbuf_tensor("maxcols", [128, 2 * KI * TC], F32))

    def up_gate_pass(wsb, spill, maxbase):
        tt_b, ss_b, sv_b = tt.n, ss.n, sv.n
        sp_b = [sp0.n, sp1.n]
        te.wait_ge(ss.h, ss_b)
        te.wait_ge(sv.h, sv_b)
        sc.wait_ge(sp0.h, sp0.n)
        sc.wait_ge(sp1.h, sp1.n)
        sc.wait_ge(sv.h, sv_b)
        g = 0
        NG = KI // 4
        for g4 in range(NG):
            par = g4 % 2
            spar = sp0 if par == 0 else sp1
            stg = fstage[:, par * 4 * S: par * 4 * S + 4 * S]
            for oti in range(4):
                ot = g4 * 4 + oti
                for tc in range(TC):
                    ps = pA if g % 2 == 0 else pB
                    if g >= 2:
                        te.wait_ge(ss.h, ss_b + g - 1)
                    for ki in range(KD):
                        mi = te.matmul(ps[:],
                                  wsb[:, ki * I3 + ot * 128: ki * I3 + ot * 128 + 128],
                                  xq[:, TX(ki, tc): TX(ki, tc) + 512],
                                  start=(ki == 0), stop=(ki == KD - 1))
                    tt.inc(mi)
                    sc.wait_ge(tt.h, tt_b + g + 1)
                    if g4 >= 2 and oti == 0 and tc == 0:
                        sc.wait_ge(spar.h, sp_b[par] + 16 * (g4 // 2))
                        sc.wait_ge(sv.h, sv_b + (g4 - 1) * 4 * TC)
                    ss.inc(sc.copy(stg[:, oti * S + tc * 512: oti * S + (tc + 1) * 512],
                                   ps[:]))
                    v.wait_ge(ss.h, ss_b + g + 1)
                    v.tensor_reduce(
                        maxcols[:, maxbase + ot * TC + tc: maxbase + ot * TC + tc + 1],
                        stg[:, oti * S + tc * 512: oti * S + (tc + 1) * 512],
                        axis=AX.X, op=Alu.max, apply_absolute_value=True)
                    v.drain()
                    sv.inc(v.nop())
                    g += 1
            sy.wait_ge(ss.h, ss_b + g)
            spar.inc(sy.dma_start(out=spill[:, g4 * 4 * S: (g4 + 1) * 4 * S],
                                  in_=stg[:]), 16)

    dequant_weight(wup_src, wupT, wx_bf, KD, I3, dq_bufs)
    if PHASE_LIMIT <= 4:
        debug_exit(wx_bf[:], KD * S, False)
        return nc, gstack
    up_gate_pass(wx_bf, up_sp, 0)
    dequant_weight(wg_src, wgT, wx_bf, KD, I3, dq_bufs, dest_wait_tt=tt.n)
    up_gate_pass(wx_bf, gate_sp, KI * TC)

    v.wait_ge(sp0.h, sp0.n)
    v.wait_ge(sp1.h, sp1.n)
    v.tensor_reduce(small[:, 20:21], maxcols[:, 0:KI * TC], axis=AX.X, op=Alu.max)
    v.tensor_reduce(small[:, 21:22], maxcols[:, KI * TC:], axis=AX.X, op=Alu.max)
    v.drain()
    n_ = sv.inc(v.nop())
    sy.wait_ge(sv.h, n_)
    dma(grow[0:1, 0:128], small[:, 20:21])
    n2 = dma(grow[0:1, 128:256], small[:, 21:22])
    v.wait_ge(ld.h, n2)
    v.tensor_reduce(ccst[0:1, 0:1], grow[0:1, 0:128], axis=AX.X, op=Alu.max)
    v.tensor_reduce(ccst[0:1, 1:2], grow[0:1, 128:256], axis=AX.X, op=Alu.max)
    v.drain()
    sv.inc(v.nop())
    exchange(1, 2)
    make_scale(ccst[0:1, 0:1], QMAX_A, 24, 25)
    make_scale(ccst[0:1, 1:2], QMAX_A, 26, 27)

    barrier()
    if PHASE_LIMIT <= 5:
        debug_exit(xq[:], KD * S, False)
        return nc, gstack
    p8_stack.close()
    dq_stack.close()
    ffn_stack.close()
    attn_stack.close()

    pass_stack = ExitStack()
    PS_ = pass_stack.enter_context
    gate2 = PS_(nc.sbuf_tensor("gate2", [128, KI * S], F16))
    ew_stack = ExitStack()
    EW = ew_stack.enter_context
    g16 = EW(nc.sbuf_tensor("g16", [128, 2 * 4 * S], F16))
    t32a = EW(nc.sbuf_tensor("t32a", [128, 2 * S], F32))
    t32b = EW(nc.sbuf_tensor("t32b", [128, S], F32))
    ti32 = EW(nc.sbuf_tensor("ti32", [128, S], I32))

    def quant3(dst_ap, src_ap, r_col, s_col, tmp_f, tmp_i):
        v.tensor_scalar(tmp_f, src_ap, small[:, r_col:r_col + 1], None, op0=Alu.mult)
        v.drain()
        v.tensor_copy(tmp_i, tmp_f)
        v.drain()
        v.tensor_scalar(dst_ap, tmp_i, small[:, s_col:s_col + 1], None, op0=Alu.mult)
        v.drain()

    def spill_load(g4, spill, pmile):
        par = g4 % 2
        buf = g16[:, par * 4 * S: par * 4 * S + 4 * S]
        eng = sy if par == 0 else sc
        sem = pa if par == 0 else pb
        if g4 >= 2:
            eng.wait_ge(sv.h, pmile[g4 - 2])
        else:
            eng.wait_ge(sv.h, sv.n)
        n_ = sem.inc(eng.dma_start(
            out=buf, in_=spill[:, g4 * 4 * S: (g4 + 1) * 4 * S]), 16)
        return buf, sem, n_

    pmile = []
    for g4 in range(KI // 4):
        gbuf, gsem, gn = spill_load(g4, gate_sp, pmile)
        v.wait_ge(gsem.h, gn)
        for oti in range(4):
            ot = g4 * 4 + oti
            gsl_in = gbuf[:, oti * S: (oti + 1) * S]
            apar = ot % 2
            ta = t32a[:, apar * S: apar * S + S]
            quant3(ta, gsl_in, 27, 26, t32b[:], ti32[:])
            n_ = sv.inc(v.nop())
            sc.wait_ge(sv.h, n_)
            n_s = ss.inc(sc.activation(t32b[:], ta, AF.Sigmoid, bias=0.0, scale=1.0))
            v.wait_ge(ss.h, n_s)
            v.tensor_scalar(t32b[:], t32b[:], 65536.0, None, op0=Alu.mult)
            v.drain()
            v.tensor_copy(ti32[:], t32b[:])
            v.drain()
            v.tensor_scalar(t32b[:], ti32[:], 65535.0, 1.0 / 65536.0,
                            op0=Alu.min, op1=Alu.mult)
            v.drain()
            v.tensor_tensor(gate2[:, ot * S:(ot + 1) * S], ta, t32b[:], op=Alu.mult)
            v.drain()
            v.tensor_reduce(small[:, 0:1], gate2[:, ot * S:(ot + 1) * S],
                            axis=AX.X, op=Alu.max, apply_absolute_value=True)
            v.drain()
            if ot == 0:
                v.tensor_copy(small[:, 1:2], small[:, 0:1])
            else:
                v.tensor_tensor(small[:, 1:2], small[:, 1:2], small[:, 0:1], op=Alu.max)
            v.drain()
        pmile.append(sv.inc(v.nop()))
    sy.wait_ge(sv.h, sv.n)
    ng = dma(grow[0:1, 0:128], small[:, 1:2])
    v.wait_ge(ld.h, ng)
    v.tensor_reduce(ccst[0:1, 0:1], grow[0:1, 0:128], axis=AX.X, op=Alu.max)
    v.drain()
    sv.inc(v.nop())
    exchange(2, 1)
    make_scale(ccst[0:1, 0:1], QMAX_A, 28, 29)

    pmile = []
    for g4 in range(KI // 4):
        gbuf, gsem, gn = spill_load(g4, up_sp, pmile)
        v.wait_ge(gsem.h, gn)
        for oti in range(4):
            ot = g4 * 4 + oti
            apar = ot % 2
            ta = t32a[:, apar * S: apar * S + S]
            quant3(ta, gbuf[:, oti * S: (oti + 1) * S], 25, 24, t32b[:], ti32[:])
            gsl = gate2[:, ot * S:(ot + 1) * S]
            v.tensor_scalar(t32b[:], gsl, small[:, 29:30], None, op0=Alu.mult)
            v.drain()
            v.tensor_copy(ti32[:], t32b[:])
            v.drain()
            v.tensor_scalar(t32b[:], ti32[:], small[:, 28:29], None, op0=Alu.mult)
            v.drain()
            v.tensor_tensor(gsl, ta, t32b[:], op=Alu.mult)
            v.drain()
            v.tensor_reduce(small[:, 0:1], gsl, axis=AX.X, op=Alu.max,
                            apply_absolute_value=True)
            v.drain()
            if ot == 0:
                v.tensor_copy(small[:, 2:3], small[:, 0:1])
            else:
                v.tensor_tensor(small[:, 2:3], small[:, 2:3], small[:, 0:1], op=Alu.max)
            v.drain()
        pmile.append(sv.inc(v.nop()))
    sy.wait_ge(sv.h, sv.n)
    ng = dma(grow[0:1, 0:128], small[:, 2:3])
    v.wait_ge(ld.h, ng)
    v.tensor_reduce(ccst[0:1, 0:1], grow[0:1, 0:128], axis=AX.X, op=Alu.max)
    v.drain()
    sv.inc(v.nop())
    exchange(3, 1)
    make_scale(ccst[0:1, 0:1], QMAX_A, 30, 31)

    for ot in range(KI):
        gsl = gate2[:, ot * S:(ot + 1) * S]
        quant3(gsl.bitcast(BF16), gsl, 31, 30, t32a[:, 0:S], ti32[:])
    sv.inc(v.nop())
    oq_all = gate2

    barrier()
    if PHASE_LIMIT <= 6:
        debug_exit(gate2[:].bitcast(BF16), KD * S, False)
        return nc, gstack
    ew_stack.close()

    wd_bf = PS_(nc.sbuf_tensor("wd_bf", [128, KI * D], BF16))
    dq2_stack = ExitStack()
    dq2_bufs = make_dq_bufs(dq2_stack, D, "b")
    dequant_weight(wd_src, wdT, wd_bf, KI, D, dq2_bufs)
    barrier()
    dq2_stack.close()

    out_stack = ExitStack()
    O_ = out_stack.enter_context
    ln1b = O_(nc.sbuf_tensor("ln1b", [128, TCW], F32))
    r2 = O_(nc.sbuf_tensor("r2", [128, KD * 512], F32))
    rbf2 = O_(nc.sbuf_tensor("rbf2", [128, 512], BF16))
    rsq2 = O_(nc.sbuf_tensor("rsq2", [128, 512], BF16))
    a2 = O_(nc.sbuf_tensor("a2", [128, 512], F32))
    b2 = O_(nc.sbuf_tensor("b2", [128, 512], F32))
    lrow2 = O_(nc.sbuf_tensor("lrow2", [1, 1536], F32))

    lmile = []
    for tc in range(TC):
        v.wait_ge(st.h, st.n)
        lpar = 0
        lsl = ln1b[:]
        lsem = pa
        sy.wait_ge(sv.h, sv.n)
        nl1 = lsem.inc(sy.dma_start(
            out=lsl, in_=ln1_sp[:, tc * TCW: (tc + 1) * TCW]), 16)
        for do in range(KD):
            ps = pA if do % 2 == 0 else pB
            if do >= 2:
                te.wait_ge(sv.h, sv.n)
            for ot in range(KI):
                mi = te.matmul(ps[:],
                          wd_bf[:, ot * D + do * 128: ot * D + do * 128 + 128],
                          oq_all[:, ot * S + tc * 512: ot * S + tc * 512 + 512]
                          .bitcast(BF16),
                          start=(ot == 0), stop=(ot == KI - 1))
            n_grp = tt.inc(mi)
            v.wait_ge(tt.h, n_grp)
            v.wait_ge(lsem.h, nl1)
            rsl = r2[:, do * 512:(do + 1) * 512]
            v.tensor_tensor(rsl, ps[:],
                            ln1b[:, do * 512: do * 512 + 512], op=Alu.add)
            v.drain()
            v.tensor_copy(rbf2[:], rsl)
            v.tensor_tensor(rsq2[:], rsl, rsl, op=Alu.mult)
            v.drain()
            n_r = sv.inc(v.nop())
            te.wait_ge(sv.h, n_r)
            te.matmul(pS1[:], ones_col[:, 0:1], rbf2[:],
                      start=(do == 0), stop=(do == KD - 1))
            mi = te.matmul(pS2[:], ones_col[:, 0:1], rsq2[:],
                      start=(do == 0), stop=(do == KD - 1))
            tt.inc(mi)
        v.wait_ge(tt.h, tt.n)
        mrow = lrow2[0:1, 0:512]
        urow = lrow2[0:1, 512:1024]
        brow = lrow2[0:1, 1024:1536]
        v.tensor_scalar(mrow, pS1[:], 1.0 / D, None, op0=Alu.mult)
        v.drain()
        v.tensor_tensor(urow, mrow, mrow, op=Alu.mult)
        v.drain()
        v.scalar_tensor_tensor(urow, pS2[:], 1.0 / D, urow,
                               op0=Alu.mult, op1=Alu.subtract)
        v.drain()
        n_v = sv.inc(v.nop())
        sc.wait_ge(sv.h, n_v)
        n_s = ss.inc(sc.activation(urow, urow, AF.Sqrt, bias=small[0:1, 34:35], scale=1.0))
        v.wait_ge(ss.h, n_s)
        v.reciprocal(urow, urow)
        v.drain()
        v.scalar_tensor_tensor(brow, mrow, -1.0, urow, op0=Alu.mult, op1=Alu.mult)
        v.drain()
        n_ab = sv.inc(v.nop())
        gp.wait_ge(sv.h, n_ab)
        dma_g(a2[:], urow.unsqueeze(1).broadcast_to([1, 128, 512]))
        nbb = dma_g(b2[:], brow.unsqueeze(1).broadcast_to([1, 128, 512]))
        v.wait_ge(ldg.h, nbb)
        r23 = r2[:].rearrange("p (k f) -> p k f", f=512)
        v.tensor_tensor(r23, r23, a2[:].unsqueeze(1).broadcast_to([128, KD, 512]),
                        op=Alu.mult)
        v.drain()
        v.tensor_tensor(r23, r23, b2[:].unsqueeze(1).broadcast_to([128, KD, 512]),
                        op=Alu.add)
        v.drain()
        n_fin = sv.inc(v.nop())
        lmile.append(n_fin)
        sy.wait_ge(sv.h, n_fin)
        dma_out(yout[:, tc * TCW: (tc + 1) * TCW], r2[:])

    sy.wait_ge(st.h, st.n)
    sy.wait_ge(lds.h, lds.n)
    sy.wait_ge(ldg.h, ldg.n)
    sy.wait_ge(ld.h, ld.n)
    sy.wait_ge(sp0.h, sp0.n)
    sy.wait_ge(sp1.h, sp1.n)

    out_stack.close()
    pass_stack.close()
    return nc, gstack


def _prep_inputs(L0_attn_x, L0_attn_Wq, L0_attn_bq, L0_attn_Wk, L0_attn_bk,
                 L0_attn_Wv, L0_attn_bv, L0_attn_Wm, L0_attn_bm,
                 L0_ffn_w_gate, L0_ffn_w_up, L0_ffn_w_down):
    def feat_major(w):
        din, n = w.shape
        return np.ascontiguousarray(
            w.reshape(KD, 128, n).transpose(1, 0, 2).reshape(128, KD * n))

    def wT_layout(w, nk):
        O, I = w.shape
        wt = np.ascontiguousarray(np.asarray(w).T)
        return np.ascontiguousarray(
            wt.reshape(nk, 128, O).transpose(1, 0, 2).reshape(128, nk * O))

    def src_layout(w):
        O, I = w.shape
        nblk = I // 32
        o4 = O // 4
        wb = np.asarray(w, np.float32).reshape(O, nblk, 32)
        out = np.zeros((128, 24576), np.float32)
        for ib in range(nblk):
            t, g = ib // 4, ib % 4
            t3, trow = t // 8, t % 8
            for s4 in range(4):
                p_src = 16 * trow + 4 * s4 + g
                seg = wb[s4 * o4:(s4 + 1) * o4, ib, :]
                base = t3 * (o4 * 32)
                out[p_src, base:base + o4 * 32] = seg.reshape(-1)
        return out

    f32 = lambda a: np.asarray(a, np.float32)
    common = dict(
        wq=feat_major(f32(L0_attn_Wq)), wk=feat_major(f32(L0_attn_Wk)),
        wv=feat_major(f32(L0_attn_Wv)), wm=feat_major(f32(L0_attn_Wm)),
        bqc=np.ascontiguousarray(f32(L0_attn_bq).reshape(KD, 128).T),
        bkc=np.ascontiguousarray(f32(L0_attn_bk).reshape(KD, 128).T),
        bmc=np.ascontiguousarray(f32(L0_attn_bm).reshape(KD, 128).T),
        bvr=f32(L0_attn_bv).reshape(1, D),
        wupT=wT_layout(f32(L0_ffn_w_up), KD),
        wgT=wT_layout(f32(L0_ffn_w_gate), KD),
        wdT=wT_layout(f32(L0_ffn_w_down), KI),
        wup_src=src_layout(f32(L0_ffn_w_up)),
        wg_src=src_layout(f32(L0_ffn_w_gate)),
        wd_src=src_layout(f32(L0_ffn_w_down)),
    )
    x = f32(L0_attn_x)
    in_maps = []
    for b in range(8):
        xb = np.ascontiguousarray(
            x[b].T.reshape(KD, 128, TC, 512).transpose(1, 2, 0, 3)
            .reshape(128, KD * S))
        m = dict(common)
        m["xin"] = xb
        in_maps.append(m)
    return in_maps


def kernel(L0_attn_x, L0_attn_Wq, L0_attn_bq, L0_attn_Wk, L0_attn_bk,
           L0_attn_Wv, L0_attn_bv, L0_attn_Wm, L0_attn_bm,
           L0_ffn_w_gate, L0_ffn_w_up, L0_ffn_w_down):
    in_maps = _prep_inputs(L0_attn_x, L0_attn_Wq, L0_attn_bq, L0_attn_Wk, L0_attn_bk,
                           L0_attn_Wv, L0_attn_bv, L0_attn_Wm, L0_attn_bm,
                           L0_ffn_w_gate, L0_ffn_w_up, L0_ffn_w_down)
    nc, gstack = build_program()
    try:
        res = run_bass_kernel_spmd(nc, in_maps, core_ids=list(range(8)))
    finally:
        try:
            gstack.close()
        except Exception:
            pass
    outs = []
    for b in range(8):
        y = res.results[b]["yout"]
        y = (y.reshape(128, TC, KD, 512).transpose(1, 3, 2, 0)
             .reshape(S, D))
        outs.append(y)
    return np.ascontiguousarray(np.stack(outs, 0)).astype(np.float32)


# version 07
# speedup vs baseline: 1.2307x; 1.2307x over previous
import numpy as np
from contextlib import ExitStack

import concourse.bass as bass
import concourse.mybir as mybir
from concourse.bass_utils import run_bass_kernel_spmd

F32 = mybir.dt.float32
I32 = mybir.dt.int32
F16 = mybir.dt.float16
BF16 = mybir.dt.bfloat16
Alu = mybir.AluOpType
AF = mybir.ActivationFunctionType
AX = mybir.AxisListType

D = 1024
S = 2048
H = 16
I3 = 3072
KD = D // 128
KI = I3 // 128
TC = S // 512
NKT = S // 128
QMAX_A = 32767.0
EPS = 1e-5
TCW = KD * 512


def TX(k, tc):
    return tc * TCW + k * 512
PHASE_LIMIT = 99


class Sem:
    def __init__(self, nc, stack, name):
        self.h = stack.enter_context(nc.semaphore(name))
        self.n = 0

    def inc(self, inst, k=1):
        inst.then_inc(self.h, k)
        self.n += k
        return self.n


def build_program():
    nc = bass.Bass("TRN2", target_bir_lowering=False)
    gstack = ExitStack()
    G = gstack.enter_context

    xin = nc.dram_tensor("xin", [128, KD * S], F32, kind="ExternalInput").ap()
    wq_h = nc.dram_tensor("wq", [128, KD * D], F32, kind="ExternalInput").ap()
    wk_h = nc.dram_tensor("wk", [128, KD * D], F32, kind="ExternalInput").ap()
    wv_h = nc.dram_tensor("wv", [128, KD * D], F32, kind="ExternalInput").ap()
    wm_h = nc.dram_tensor("wm", [128, KD * D], F32, kind="ExternalInput").ap()
    bqc = nc.dram_tensor("bqc", [128, KD], F32, kind="ExternalInput").ap()
    bkc = nc.dram_tensor("bkc", [128, KD], F32, kind="ExternalInput").ap()
    bmc = nc.dram_tensor("bmc", [128, KD], F32, kind="ExternalInput").ap()
    bvr = nc.dram_tensor("bvr", [1, D], F32, kind="ExternalInput").ap()
    wupT = nc.dram_tensor("wupT", [128, KD * I3], F32, kind="ExternalInput").ap()
    wgT = nc.dram_tensor("wgT", [128, KD * I3], F32, kind="ExternalInput").ap()
    wdT = nc.dram_tensor("wdT", [128, KI * D], F32, kind="ExternalInput").ap()
    wup_src = nc.dram_tensor("wup_src", [128, 24576], F32, kind="ExternalInput").ap()
    wg_src = nc.dram_tensor("wg_src", [128, 24576], F32, kind="ExternalInput").ap()
    wd_src = nc.dram_tensor("wd_src", [128, 24576], F32, kind="ExternalInput").ap()
    yout = nc.dram_tensor("yout", [128, KD * S], F32, kind="ExternalOutput").ap()
    ln1_sp = nc.dram_tensor("ln1_sp", [128, KD * S], F32, kind="Internal").ap()
    up_sp = nc.dram_tensor("up_sp", [128, KI * S], F16, kind="Internal").ap()
    gate_sp = nc.dram_tensor("gate_sp", [128, KI * S], F16, kind="Internal").ap()
    cc_bufs = [(nc.dram_tensor(f"cc{i}i", [1, 2], F32, kind="Internal").ap(),
                nc.dram_tensor(f"cc{i}o", [1, 2], F32, kind="Internal",
                               addr_space="Shared").ap()) for i in range(4)]

    v, sc, te, gp, sy = nc.vector, nc.scalar, nc.tensor, nc.gpsimd, nc.sync

    p = ExitStack()
    ld = Sem(nc, gstack, "ld")
    ldg = Sem(nc, gstack, "ldg")
    lds = Sem(nc, gstack, "lds")
    st = Sem(nc, gstack, "st")
    sp0 = Sem(nc, gstack, "sp0")
    sp1 = Sem(nc, gstack, "sp1")
    pa = Sem(nc, gstack, "pa")
    pb = Sem(nc, gstack, "pb")
    pc = Sem(nc, gstack, "pc")
    pd = Sem(nc, gstack, "pd")
    sv = Sem(nc, gstack, "sv")
    ss = Sem(nc, gstack, "ss")
    tt = Sem(nc, gstack, "tt")
    gg = Sem(nc, gstack, "gg")
    gw = Sem(nc, gstack, "gw")
    ccs = Sem(nc, gstack, "ccs")

    def dma(out, in_):
        return ld.inc(sy.dma_start(out=out, in_=in_), 16)

    def dma_g(out, in_):
        return ldg.inc(gp.dma_start(out=out, in_=in_), 16)

    def dma_s(out, in_):
        return lds.inc(sc.dma_start(out=out, in_=in_), 16)

    def dma_out(out, in_):
        return st.inc(sy.dma_start(out=out, in_=in_), 16)

    def barrier():
        engs = [te, v, sc, gp, sy]
        base = gg.n
        for e in engs:
            gg.inc(e.nop(), 1)
        for e in engs:
            e.wait_ge(gg.h, base + len(engs))

    def debug_exit(src_ap, nfree, is_f32):
        barrier()
        dbg = nc.sbuf_tensor("dbgbuf", [128, 2048], F32, side="right")
        dbgh = dbg.__enter__()
        nch = min(nfree, KD * S) // 2048
        for c in range(nch):
            if c > 0:
                sy.wait_ge(sv.h, sv.n)
            v.wait_ge(st.h, st.n)
            v.tensor_copy(dbgh[:], src_ap[:, c * 2048:(c + 1) * 2048])
            v.drain()
            n_ = sv.inc(v.nop())
            sy.wait_ge(sv.h, n_)
            dma_out(yout[:, c * 2048:(c + 1) * 2048], dbgh[:])
        sy.wait_ge(st.h, st.n)

    small = G(nc.sbuf_tensor("small", [128, 40], F32))
    ccst = G(nc.sbuf_tensor("ccst", [1, 8], F32))
    grow = G(nc.sbuf_tensor("grow", [1, 256], F32))
    ones_col = G(nc.sbuf_tensor("ones_col", [128, 2], BF16))
    bstage = G(nc.sbuf_tensor("bstage", [128, 3 * KD], F32))

    pA = G(nc.psum_tensor("pA", [128, 512], F32))
    pB = G(nc.psum_tensor("pB", [128, 512], F32))
    pC = G(nc.psum_tensor("pC", [128, 512], F32))
    pD = G(nc.psum_tensor("pD", [128, 512], F32))
    pV0 = G(nc.psum_tensor("pV0", [66, 512], F32))
    pV1 = G(nc.psum_tensor("pV1", [66, 512], F32))
    pS1 = G(nc.psum_tensor("pS1", [1, 512], F32))
    pS2 = G(nc.psum_tensor("pS2", [1, 512], F32))

    dma(bstage[:, 0:KD], bqc)
    dma(bstage[:, KD:2 * KD], bkc)
    dma(bstage[:, 2 * KD:3 * KD], bmc)
    v.memset(ones_col[:], 1.0)
    v.memset(small[:, 34:35], EPS)
    v.memset(ccst[:], 0.0)
    v.drain()
    sv.inc(v.nop())

    import ml_dtypes as _mld
    _e = np.zeros((128, 8 * 128), np.float32)
    for i4 in range(4):
        for k32 in range(32):
            for vv in range(8):
                for pp in range(128):
                    if k32 == 4 * vv + pp // 32:
                        _e[32 * i4 + k32, vv * 128 + pp] = 1.0
    e_dram = nc.inline_tensor(_e.astype(_mld.bfloat16), name="e_repl").ap()
    e_all = G(nc.sbuf_tensor("e_all", [128, 8 * 128], BF16))
    dma(e_all[:], e_dram)

    attn_stack = ExitStack()
    attnT = attn_stack.enter_context(
        nc.sbuf_tensor("attnT", [128, KD * S], BF16, side="right"))
    xq = attnT

    qkv_stack = ExitStack()
    Q_ = qkv_stack.enter_context
    qT = Q_(nc.sbuf_tensor("qT", [128, KD * S], BF16))
    kT = Q_(nc.sbuf_tensor("kT", [128, KD * S], BF16))
    V66 = Q_(nc.sbuf_tensor("V66", [128, NKT * 1056], BF16))

    x_stack = ExitStack()
    X_ = x_stack.enter_context
    xbf = X_(nc.sbuf_tensor("xbf", [128, KD * S], BF16))
    wproj = X_(nc.sbuf_tensor("wproj", [128, KD * D], BF16))
    wstage = X_(nc.sbuf_tensor("wstage", [128, 2 * 1024], F32))
    bvrep = X_(nc.sbuf_tensor("bvrep", [128, D], F32))

    dma(bvrep[:], bvr.unsqueeze(1).broadcast_to([1, 128, D]))
    sy.wait_ge(ld.h, ld.n)

    def load_weight_bf16(src_ap, dest_sb, nfree, csz=1024):
        mile = []
        for c in range(nfree // csz):
            par = c % 2
            buf = wstage[:, par * 1024: par * 1024 + csz]
            eng = sy if par == 0 else sc
            sem = pa if par == 0 else pb
            if c >= 2:
                eng.wait_ge(sv.h, mile[c - 2])
            else:
                eng.wait_ge(sv.h, sv.n)
            n_ = sem.inc((sy if par == 0 else sc).dma_start(
                out=buf, in_=src_ap[:, c * csz:(c + 1) * csz]), 16)
            v.wait_ge(sem.h, n_)
            v.tensor_copy(dest_sb[:, c * csz:(c + 1) * csz], buf)
            v.drain()
            mile.append(sv.inc(v.nop()))
        return sv.n

    load_weight_bf16(xin, xbf, KD * S)

    def qk_proj(wsrc, bias_col, dest):
        v.wait_ge(tt.h, tt.n)
        wready = load_weight_bf16(wsrc, wproj, KD * D)
        te.wait_ge(sv.h, wready)
        ss_b = ss.n
        te.wait_ge(ss.h, ss_b)
        sc.wait_ge(ld.h, 5 * 16)
        tt_b = tt.n
        g = 0
        for do in range(KD):
            for tc in range(TC):
                ps = pA if g % 2 == 0 else pB
                if g >= 2:
                    te.wait_ge(ss.h, ss_b + g - 1)
                for ki in range(KD):
                    mi = te.matmul(ps[:],
                              wproj[:, ki * D + do * 128: ki * D + do * 128 + 128],
                              xbf[:, TX(ki, tc): TX(ki, tc) + 512],
                              start=(ki == 0), stop=(ki == KD - 1))
                tt.inc(mi)
                sc.wait_ge(tt.h, tt_b + g + 1)
                ss.inc(sc.activation(
                    dest[:, TX(do, tc): TX(do, tc) + 512], ps[:],
                    AF.Identity,
                    bias=bstage[:, bias_col + do: bias_col + do + 1], scale=1.0))
                g += 1

    qk_proj(wq_h, 0, qT)
    qk_proj(wk_h, KD, kT)

    v.wait_ge(tt.h, tt.n)
    wready = load_weight_bf16(wv_h, wproj, KD * D)
    gp.wait_ge(sv.h, sv.n)
    gw.inc(gp.memset(V66[:], 1.0))
    te.wait_ge(sv.h, wready)
    te.wait_ge(gw.h, gw.n)
    te.wait_ge(ss.h, ss.n)
    sv_b = sv.n
    tt_b = tt.n
    g = 0
    for tt_i in range(NKT):
        for dc in range(2):
            ps = pA if g % 2 == 0 else pB
            if g >= 2:
                te.wait_ge(sv.h, sv_b + g - 1)
            for ki in range(KD):
                mi = te.matmul(ps[:],
                          xbf[:, TX(ki, tt_i // 4) + (tt_i % 4) * 128:
                                 TX(ki, tt_i // 4) + (tt_i % 4) * 128 + 128],
                          wproj[:, ki * D + dc * 512: ki * D + dc * 512 + 512],
                          start=(ki == 0), stop=(ki == KD - 1))
            tt.inc(mi)
            v.wait_ge(tt.h, tt_b + g + 1)
            out_ap = V66[:, tt_i * 1056 + dc * 8 * 66: tt_i * 1056 + (dc * 8 + 8) * 66]
            out_ap = out_ap.rearrange("p (h c) -> p h c", c=66)[:, :, 0:64]
            ps3 = ps[:].rearrange("p (h c) -> p h c", c=64)
            bv3 = bvrep[:, dc * 512:(dc + 1) * 512].rearrange("p (h c) -> p h c", c=64)
            v.tensor_tensor(out_ap, ps3, bv3, op=Alu.add)
            v.drain()
            sv.inc(v.nop())
            g += 1

    barrier()
    if PHASE_LIMIT <= 1:
        debug_exit(qT[:], KD * S, False)
        return nc, gstack
    x_stack.close()

    a_stack = ExitStack()
    A_ = a_stack.enter_context
    probs = A_(nc.sbuf_tensor("probs", [128, 2 * NKT * 512], BF16))
    recband = A_(nc.sbuf_tensor("recband", [64, 2 * 512], F32))
    pvrec = A_(nc.sbuf_tensor("pvrec", [1, 2 * 512], F32))

    sbanks = [pA, pB, pC, pD]
    pv_norm_sv = [0, 0]
    pv_tt_par = [tt.n, tt.n]
    rec_sv_par = [sv.n, sv.n]
    ss_b4 = ss.n
    gexp = [0]
    exp_mile = []
    iters = [(h, qc) for h in range(H) for qc in range(TC)]

    def p4_scores(it):
        h, qc = iters[it]
        ko, po = h // 2, (h % 2) * 64
        par = it % 2
        pbase = par * (NKT * 512)
        kmax = 4 * (qc + 1)
        sc.wait_ge(tt.h, pv_tt_par[par])
        for kb in range(kmax):
            m = gexp[0] + kb
            psb = sbanks[m % 4]
            if m >= 4:
                te.wait_ge(ss.h, ss_b4 + m - 3)
            kx = TX(ko, kb // 4) + (kb % 4) * 128
            mi = te.matmul(psb[:],
                      kT[po:po + 64, kx: kx + 128],
                      qT[po:po + 64, TX(ko, qc): TX(ko, qc) + 512],
                      start=True, stop=True)
            tt.inc(mi)
            sc.wait_ge(tt.h, tt.n)
            ss.inc(sc.activation(probs[:, pbase + kb * 512: pbase + (kb + 1) * 512],
                                 psb[:], AF.Exp, bias=0.0, scale=0.125))
        gexp[0] += kmax
        exp_mile.append(ss_b4 + gexp[0])
        gp.wait_ge(ss.h, exp_mile[-1])
        for kb in [kb for kb in range(kmax) if kb * 128 + 127 > qc * 512]:
            mi_g = gp.affine_select(
                probs[:, pbase + kb * 512: pbase + (kb + 1) * 512],
                probs[:, pbase + kb * 512: pbase + (kb + 1) * 512],
                pattern=[[1, 512]], compare_op=Alu.is_ge, fill=0.0,
                base=qc * 512 - kb * 128, channel_multiplier=-1)
        return gw.inc(mi_g)

    def p4_pv(it, gw_mask):
        h, qc = iters[it]
        ko, po = h // 2, (h % 2) * 64
        par = it % 2
        pbase = par * (NKT * 512)
        kmax = 4 * (qc + 1)
        ps_pv = pV0 if par == 0 else pV1
        te.wait_ge(gw.h, gw_mask)
        te.wait_ge(sv.h, pv_norm_sv[par])
        for kb in range(kmax):
            mi = te.matmul(ps_pv[:],
                      V66[:, kb * 1056 + h * 66: kb * 1056 + h * 66 + 66],
                      probs[:, pbase + kb * 512: pbase + (kb + 1) * 512],
                      start=(kb == 0), stop=(kb == kmax - 1))
        pv_tt_par[par] = tt.inc(mi)
        rsl = pvrec[0:1, par * 512: par * 512 + 512]
        v.wait_ge(tt.h, pv_tt_par[par])
        v.reciprocal(rsl, ps_pv[64:65, :])
        v.drain()
        nrec = sv.inc(v.nop())
        gp.wait_ge(sv.h, nrec)
        gp.wait_ge(sv.h, rec_sv_par[par])
        rb = recband[:, par * 512: par * 512 + 512]
        nb = dma_g(rb, rsl.unsqueeze(1).broadcast_to([1, 64, 512]))
        v.wait_ge(ldg.h, nb)
        v.tensor_tensor(
            attnT[po:po + 64, TX(ko, qc): TX(ko, qc) + 512],
            ps_pv[0:64, :], rb, op=Alu.mult)
        v.drain()
        rec_sv_par[par] = pv_norm_sv[par] = sv.inc(v.nop())

    gw_prev = p4_scores(0)
    for it in range(len(iters)):
        gw_next = p4_scores(it + 1) if it + 1 < len(iters) else None
        p4_pv(it, gw_prev)
        gw_prev = gw_next

    barrier()
    if PHASE_LIMIT <= 2:
        debug_exit(attnT[:], KD * S, False)
        return nc, gstack
    a_stack.close()
    qkv_stack.close()

    p5_stack = ExitStack()
    P5 = p5_stack.enter_context
    wproj2 = P5(nc.sbuf_tensor("wproj2", [128, KD * D], BF16))
    wstage2 = P5(nc.sbuf_tensor("wstage2", [128, 2048], F32))
    rbuf = P5(nc.sbuf_tensor("rbuf", [128, KD * S], F32))
    xres = P5(nc.sbuf_tensor("xres", [128, 2 * TCW], F32))
    rbf = P5(nc.sbuf_tensor("rbf", [128, 512], BF16))
    rsq = P5(nc.sbuf_tensor("rsq", [128, 512], BF16))
    arep = P5(nc.sbuf_tensor("arep", [128, 512], F32))
    brep = P5(nc.sbuf_tensor("brep", [128, 512], F32))
    q32 = P5(nc.sbuf_tensor("q32", [128, TCW], F32))
    qi32 = P5(nc.sbuf_tensor("qi32", [128, TCW], I32))
    lrow = P5(nc.sbuf_tensor("lrow", [1, 1536], F32))

    def load_weight_bf16_p5(src_ap, dest_sb, nfree, csz=1024):
        mile = []
        for c in range(nfree // csz):
            par = c % 2
            buf = wstage2[:, par * 1024: par * 1024 + csz]
            eng = sy if par == 0 else sc
            sem = pa if par == 0 else pb
            if c >= 2:
                eng.wait_ge(sv.h, mile[c - 2])
            else:
                eng.wait_ge(sv.h, sv.n)
            n_ = sem.inc(eng.dma_start(out=buf, in_=src_ap[:, c * csz:(c + 1) * csz]), 16)
            v.wait_ge(sem.h, n_)
            v.tensor_copy(dest_sb[:, c * csz:(c + 1) * csz], buf)
            v.drain()
            mile.append(sv.inc(v.nop()))
        return sv.n

    load_weight_bf16_p5(wm_h, wproj2, KD * D)
    te.wait_ge(sv.h, sv.n)

    xmile = []
    for tc in range(TC):
        xeng = sy if tc % 2 == 0 else sc
        xsem = pa if tc % 2 == 0 else pb
        if tc >= 2:
            xeng.wait_ge(sv.h, xmile[tc - 2])
        else:
            xeng.wait_ge(sv.h, sv.n)
        nx = xsem.inc(xeng.dma_start(
            out=xres[:, (tc % 2) * TCW: (tc % 2) * TCW + TCW],
            in_=xin[:, tc * TCW: (tc + 1) * TCW]), 16)
        for do in range(KD):
            ps = pA if do % 2 == 0 else pB
            if do >= 2:
                te.wait_ge(sv.h, sv.n)
            for ki in range(KD):
                mi = te.matmul(ps[:],
                          wproj2[:, ki * D + do * 128: ki * D + do * 128 + 128],
                          attnT[:, TX(ki, tc): TX(ki, tc) + 512],
                          start=(ki == 0), stop=(ki == KD - 1))
            n_grp = tt.inc(mi)
            sc.wait_ge(tt.h, n_grp)
            n_ev = ss.inc(sc.activation(
                rbuf[:, TX(do, tc): TX(do, tc) + 512], ps[:],
                AF.Identity, bias=bstage[:, 2 * KD + do: 2 * KD + do + 1], scale=1.0))
            v.wait_ge(ss.h, n_ev)
            v.wait_ge(xsem.h, nx)
            v.wait_ge(tt.h, tt.n)
            rsl = rbuf[:, TX(do, tc): TX(do, tc) + 512]
            v.tensor_tensor(rsl, rsl,
                            xres[:, (tc % 2) * TCW + do * 512:
                                    (tc % 2) * TCW + do * 512 + 512], op=Alu.add)
            v.drain()
            v.tensor_copy(rbf[:], rsl)
            v.tensor_tensor(rsq[:], rsl, rsl, op=Alu.mult)
            v.drain()
            n_r = sv.inc(v.nop())
            te.wait_ge(sv.h, n_r)
            te.matmul(pS1[:], ones_col[:, 0:1], rbf[:],
                      start=(do == 0), stop=(do == KD - 1))
            mi = te.matmul(pS2[:], ones_col[:, 0:1], rsq[:],
                      start=(do == 0), stop=(do == KD - 1))
            tt.inc(mi)
        v.wait_ge(tt.h, tt.n)
        mrow = lrow[0:1, 0:512]
        urow = lrow[0:1, 512:1024]
        brow = lrow[0:1, 1024:1536]
        v.tensor_scalar(mrow, pS1[:], 1.0 / D, None, op0=Alu.mult)
        v.drain()
        v.tensor_tensor(urow, mrow, mrow, op=Alu.mult)
        v.drain()
        v.scalar_tensor_tensor(urow, pS2[:], 1.0 / D, urow,
                               op0=Alu.mult, op1=Alu.subtract)
        v.drain()
        n_v = sv.inc(v.nop())
        sc.wait_ge(sv.h, n_v)
        n_s = ss.inc(sc.activation(urow, urow, AF.Sqrt, bias=small[0:1, 34:35], scale=1.0))
        v.wait_ge(ss.h, n_s)
        v.reciprocal(urow, urow)
        v.drain()
        v.scalar_tensor_tensor(brow, mrow, -1.0, urow, op0=Alu.mult, op1=Alu.mult)
        v.drain()
        n_ab = sv.inc(v.nop())
        gp.wait_ge(sv.h, n_ab)
        dma_g(arep[:], urow.unsqueeze(1).broadcast_to([1, 128, 512]))
        nbb = dma_g(brep[:], brow.unsqueeze(1).broadcast_to([1, 128, 512]))
        v.wait_ge(ldg.h, nbb)
        win = rbuf[:, tc * TCW: (tc + 1) * TCW]
        win3 = win.rearrange("p (k f) -> p k f", f=512)
        v.tensor_tensor(win3, win3,
                        arep[:].unsqueeze(1).broadcast_to([128, KD, 512]),
                        op=Alu.mult)
        v.drain()
        v.tensor_tensor(win3, win3,
                        brep[:].unsqueeze(1).broadcast_to([128, KD, 512]),
                        op=Alu.add)
        v.drain()
        v.tensor_reduce(small[:, tc:tc + 1], win, axis=AX.X, op=Alu.max,
                        apply_absolute_value=True)
        v.drain()
        xmile.append(sv.inc(v.nop()))

    v.tensor_reduce(small[:, 8:9], small[:, 0:TC], axis=AX.X, op=Alu.max)
    v.drain()
    n_m = sv.inc(v.nop())
    sy.wait_ge(sv.h, n_m)
    ng = dma(grow[0:1, 0:128], small[:, 8:9])
    v.wait_ge(ld.h, ng)
    v.tensor_reduce(ccst[0:1, 0:1], grow[0:1, 0:128], axis=AX.X, op=Alu.max)
    v.drain()
    sv.inc(v.nop())

    def exchange(idx, nvals):
        ci, co = cc_bufs[idx]
        sy.wait_ge(sv.h, sv.n)
        nin = dma(ci[0:1, 0:2], ccst[0:1, 0:2])
        gp.wait_ge(ld.h, nin)
        ccs.inc(gp.collective_compute(
            "AllReduce", Alu.max, replica_groups=[[0, 1, 2, 3, 4, 5, 6, 7]],
            ins=[ci], outs=[co]))
        sy.wait_ge(ccs.h, ccs.n)
        nout = dma(ccst[0:1, 0:2], co[0:1, 0:2])
        v.wait_ge(ld.h, nout)

    def make_scale(src_ap, qmax, s_col, r_col):
        v.tensor_scalar(ccst[0:1, 4:5], src_ap, 1.0 / qmax, 1e-12,
                        op0=Alu.mult, op1=Alu.max)
        v.drain()
        v.reciprocal(ccst[0:1, 5:6], ccst[0:1, 4:5])
        v.drain()
        n_ = sv.inc(v.nop())
        sy.wait_ge(sv.h, n_)
        dma(small[:, s_col:s_col + 1],
            ccst[0:1, 4:5].unsqueeze(1).broadcast_to([1, 128, 1]))
        n2 = dma(small[:, r_col:r_col + 1],
                 ccst[0:1, 5:6].unsqueeze(1).broadcast_to([1, 128, 1]))
        v.wait_ge(ld.h, n2)
        sv.inc(v.nop())

    exchange(0, 1)
    make_scale(ccst[0:1, 0:1], QMAX_A, 16, 17)

    for tc in range(TC):
        rsl = rbuf[:, tc * TCW: (tc + 1) * TCW]
        v.tensor_scalar(q32[:], rsl, small[:, 17:18], None, op0=Alu.mult)
        v.drain()
        v.tensor_copy(qi32[:], q32[:])
        v.drain()
        v.tensor_scalar(xq[:, tc * TCW: (tc + 1) * TCW], qi32[:], small[:, 16:17],
                        None, op0=Alu.mult)
        v.drain()
    n_xq = sv.inc(v.nop())
    sy.wait_ge(sv.h, n_xq)
    n_ln1sp = dma_out(ln1_sp, rbuf[:])
    sy.wait_ge(st.h, n_ln1sp)

    barrier()
    if PHASE_LIMIT <= 3:
        debug_exit(rbuf[:], KD * S, True)
        return nc, gstack
    p5_stack.close()

    ffn_stack = ExitStack()
    FS = ffn_stack.enter_context
    wx_bf = FS(nc.sbuf_tensor("wx_bf", [128, KD * I3], BF16))

    def make_dq_bufs(dstack, osz, tag):
        DQ = dstack.enter_context
        return dict(
            sblk_s=DQ(nc.sbuf_tensor(f"sblk_s{tag}", [128, 768], F32)),
            sblk_r=DQ(nc.sbuf_tensor(f"sblk_r{tag}", [128, 768], F32)),
            dscr=DQ(nc.sbuf_tensor(f"dscr{tag}", [128, 768], F32)),
            shml=DQ(nc.sbuf_tensor(f"shml{tag}", [128, 3 * 768], BF16)),
            rhml=DQ(nc.sbuf_tensor(f"rhml{tag}", [128, 3 * 768], BF16)),
            wtb0=DQ(nc.sbuf_tensor(f"wtb0{tag}", [128, 3072], F32)),
            wtb1=DQ(nc.sbuf_tensor(f"wtb1{tag}", [128, 3072], F32)),
            srep0=DQ(nc.sbuf_tensor(f"srep0{tag}", [128, osz], F32)),
            srep1=(DQ(nc.sbuf_tensor(f"srep1{tag}", [128, osz], F32))
                   if osz >= 3072 else None),
            rrep0=DQ(nc.sbuf_tensor(f"rrep0{tag}", [128, osz], F32)),
            rrep1=(DQ(nc.sbuf_tensor(f"rrep1{tag}", [128, osz], F32))
                   if osz >= 3072 else None),
        )

    def dequant_weight(src_ap, wT_ap, dest_sb, n_it, osz, bufs, dest_wait_tt=None):
        sblk_s, sblk_r, dscr = bufs["sblk_s"], bufs["sblk_r"], bufs["dscr"]
        shml, rhml = bufs["shml"], bufs["rhml"]
        wtbs = [bufs["wtb0"], bufs["wtb1"]]
        dbl = bufs["srep1"] is not None
        sreps = [bufs["srep0"], bufs["srep1"] if dbl else bufs["srep0"]]
        rreps = [bufs["rrep0"], bufs["rrep1"] if dbl else bufs["rrep0"]]
        o4 = osz // 4
        mile = []
        for ch in range(8):
            par = ch % 2
            buf = wtbs[par]
            eng = sy if par == 0 else sc
            sem = pa if par == 0 else pb
            if ch >= 2:
                eng.wait_ge(sv.h, mile[ch - 2])
            else:
                eng.wait_ge(sv.h, sv.n)
            n_ = sem.inc(eng.dma_start(
                out=buf[:], in_=src_ap[:, ch * 3072:(ch + 1) * 3072]), 16)
            v.wait_ge(sem.h, n_)
            v.tensor_reduce(sblk_s[:, ch * 96:(ch + 1) * 96],
                            buf[:].rearrange("p (b c) -> p b c", c=32),
                            axis=AX.X, op=Alu.max, apply_absolute_value=True)
            v.drain()
            mile.append(sv.inc(v.nop()))
        v.tensor_scalar(sblk_s[:], sblk_s[:], 1.0 / 7.0, 1e-12,
                        op0=Alu.mult, op1=Alu.max)
        v.drain()
        v.reciprocal(sblk_r[:], sblk_s[:])
        v.drain()
        for x_f32, hml in ((sblk_s, shml), (sblk_r, rhml)):
            v.tensor_copy(hml[:, 0:768], x_f32[:])
            v.drain()
            v.tensor_tensor(dscr[:], x_f32[:], hml[:, 0:768], op=Alu.subtract)
            v.drain()
            v.tensor_copy(hml[:, 768:1536], dscr[:])
            v.drain()
            v.tensor_tensor(dscr[:], dscr[:], hml[:, 768:1536], op=Alu.subtract)
            v.drain()
            v.tensor_copy(hml[:, 1536:2304], dscr[:])
            v.drain()
        n_hml = sv.inc(v.nop())
        te.wait_ge(sv.h, n_hml)
        tmile = []
        ss_db = ss.n
        nch = osz // 512
        evac_cnt = 0
        for t in range(n_it):
            par = t % 2
            t3 = 0 if n_it == KD else t // 8
            trow = t if n_it == KD else t % 8
            srep, rrep, wtb = sreps[par if dbl else 0], rreps[par if dbl else 0], wtbs[par]
            wsem = pa if par == 0 else pb
            weng = sy if par == 0 else sc
            back = 2 if dbl else 1
            if t >= back:
                weng.wait_ge(sv.h, tmile[t - back])
                sc.wait_ge(sv.h, tmile[t - back])
            else:
                weng.wait_ge(sv.h, sv.n)
                sc.wait_ge(sv.h, sv.n)
            nw = wsem.inc(weng.dma_start(
                out=wtb[:, 0:osz], in_=wT_ap[:, t * osz:(t + 1) * osz]), 16)
            for ch in range(nch):
                o_lo = ch * 512
                for hml, rep in ((shml, srep), (rhml, rrep)):
                    psr = pC if evac_cnt % 2 == 0 else pD
                    if evac_cnt >= 2:
                        te.wait_ge(ss.h, ss_db + evac_cnt - 1)
                    specs = []
                    for xi in range(3):
                        pos = o_lo
                        while pos < o_lo + 512:
                            s4 = pos // o4
                            seg = min(o_lo + 512, (s4 + 1) * o4) - pos
                            row = 16 * trow + 4 * s4
                            specs.append((xi, pos, seg, row // 32, row % 32,
                                          t3 * o4 + (pos - s4 * o4)))
                            pos += seg
                    for si, (xi, pos, seg, w32, dlt, fo) in enumerate(specs):
                        mi = te.matmul(
                            psr[:, pos - o_lo: pos - o_lo + seg],
                            e_all[32 * w32: 32 * w32 + 32,
                                  (dlt // 4) * 128: (dlt // 4) * 128 + 128],
                            hml[32 * w32: 32 * w32 + 32,
                                xi * 768 + fo: xi * 768 + fo + seg],
                            start=(si == 0), stop=(si == len(specs) - 1),
                            tile_position=(32 * w32, 0))
                    tt.inc(mi)
                    sc.wait_ge(tt.h, tt.n)
                    ss.inc(sc.copy(rep[:, o_lo:o_lo + 512], psr[:]))
                    evac_cnt += 1
            v.wait_ge(wsem.h, nw)
            v.wait_ge(ss.h, ss_db + evac_cnt)
            v.tensor_tensor(wtb[:, 0:osz], wtb[:, 0:osz], rrep[:, 0:osz], op=Alu.mult)
            v.drain()
            v.tensor_copy(wtb[:, 0:osz].bitcast(I32), wtb[:, 0:osz])
            v.drain()
            if t == 0 and dest_wait_tt is not None:
                v.wait_ge(tt.h, dest_wait_tt)
            v.tensor_tensor(dest_sb[:, t * osz:(t + 1) * osz],
                            wtb[:, 0:osz].bitcast(I32), srep[:, 0:osz], op=Alu.mult)
            v.drain()
            tmile.append(sv.inc(v.nop()))

    dq_stack = ExitStack()
    dq_bufs = make_dq_bufs(dq_stack, I3, "a")

    p8_stack = ExitStack()
    P8 = p8_stack.enter_context
    fstage = P8(nc.sbuf_tensor("fstage", [128, 2 * 4 * S], F16))
    maxcols = P8(nc.sbuf_tensor("maxcols", [128, 2 * KI * TC], F32))

    def up_gate_pass(wsb, spill, maxbase):
        tt_b, ss_b, sv_b = tt.n, ss.n, sv.n
        sp_b = [sp0.n, sp1.n]
        te.wait_ge(ss.h, ss_b)
        te.wait_ge(sv.h, sv_b)
        sc.wait_ge(sp0.h, sp0.n)
        sc.wait_ge(sp1.h, sp1.n)
        sc.wait_ge(sv.h, sv_b)
        g = 0
        NG = KI // 4
        for g4 in range(NG):
            par = g4 % 2
            spar = sp0 if par == 0 else sp1
            stg = fstage[:, par * 4 * S: par * 4 * S + 4 * S]
            for oti in range(4):
                ot = g4 * 4 + oti
                for tc in range(TC):
                    ps = pA if g % 2 == 0 else pB
                    if g >= 2:
                        te.wait_ge(ss.h, ss_b + g - 1)
                    for ki in range(KD):
                        mi = te.matmul(ps[:],
                                  wsb[:, ki * I3 + ot * 128: ki * I3 + ot * 128 + 128],
                                  xq[:, TX(ki, tc): TX(ki, tc) + 512],
                                  start=(ki == 0), stop=(ki == KD - 1))
                    tt.inc(mi)
                    sc.wait_ge(tt.h, tt_b + g + 1)
                    if g4 >= 2 and oti == 0 and tc == 0:
                        sc.wait_ge(spar.h, sp_b[par] + 16 * (g4 // 2))
                        sc.wait_ge(sv.h, sv_b + (g4 - 1) * 4 * TC)
                    ss.inc(sc.copy(stg[:, oti * S + tc * 512: oti * S + (tc + 1) * 512],
                                   ps[:]))
                    v.wait_ge(ss.h, ss_b + g + 1)
                    v.tensor_reduce(
                        maxcols[:, maxbase + ot * TC + tc: maxbase + ot * TC + tc + 1],
                        stg[:, oti * S + tc * 512: oti * S + (tc + 1) * 512],
                        axis=AX.X, op=Alu.max, apply_absolute_value=True)
                    v.drain()
                    sv.inc(v.nop())
                    g += 1
            sy.wait_ge(ss.h, ss_b + g)
            spar.inc(sy.dma_start(out=spill[:, g4 * 4 * S: (g4 + 1) * 4 * S],
                                  in_=stg[:]), 16)

    dequant_weight(wup_src, wupT, wx_bf, KD, I3, dq_bufs)
    if PHASE_LIMIT <= 4:
        debug_exit(wx_bf[:], KD * S, False)
        return nc, gstack
    up_gate_pass(wx_bf, up_sp, 0)
    dequant_weight(wg_src, wgT, wx_bf, KD, I3, dq_bufs, dest_wait_tt=tt.n)
    up_gate_pass(wx_bf, gate_sp, KI * TC)

    v.wait_ge(sp0.h, sp0.n)
    v.wait_ge(sp1.h, sp1.n)
    v.tensor_reduce(small[:, 20:21], maxcols[:, 0:KI * TC], axis=AX.X, op=Alu.max)
    v.tensor_reduce(small[:, 21:22], maxcols[:, KI * TC:], axis=AX.X, op=Alu.max)
    v.drain()
    n_ = sv.inc(v.nop())
    sy.wait_ge(sv.h, n_)
    dma(grow[0:1, 0:128], small[:, 20:21])
    n2 = dma(grow[0:1, 128:256], small[:, 21:22])
    v.wait_ge(ld.h, n2)
    v.tensor_reduce(ccst[0:1, 0:1], grow[0:1, 0:128], axis=AX.X, op=Alu.max)
    v.tensor_reduce(ccst[0:1, 1:2], grow[0:1, 128:256], axis=AX.X, op=Alu.max)
    v.drain()
    sv.inc(v.nop())
    exchange(1, 2)
    make_scale(ccst[0:1, 0:1], QMAX_A, 24, 25)
    make_scale(ccst[0:1, 1:2], QMAX_A, 26, 27)

    barrier()
    if PHASE_LIMIT <= 5:
        debug_exit(xq[:], KD * S, False)
        return nc, gstack
    p8_stack.close()
    dq_stack.close()
    ffn_stack.close()
    attn_stack.close()

    pass_stack = ExitStack()
    PS_ = pass_stack.enter_context
    gate2 = PS_(nc.sbuf_tensor("gate2", [128, KI * S], F16))
    ew_stack = ExitStack()
    EW = ew_stack.enter_context
    g16 = EW(nc.sbuf_tensor("g16", [128, 2 * 4 * S], F16))
    t32a = EW(nc.sbuf_tensor("t32a", [128, 2 * S], F32))
    t32b = EW(nc.sbuf_tensor("t32b", [128, S], F32))
    ti32 = EW(nc.sbuf_tensor("ti32", [128, S], I32))

    def quant3(dst_ap, src_ap, r_col, s_col, tmp_f, tmp_i):
        v.tensor_scalar(tmp_f, src_ap, small[:, r_col:r_col + 1], None, op0=Alu.mult)
        v.drain()
        v.tensor_copy(tmp_i, tmp_f)
        v.drain()
        v.tensor_scalar(dst_ap, tmp_i, small[:, s_col:s_col + 1], None, op0=Alu.mult)
        v.drain()

    def spill_load(g4, spill, pmile):
        par = g4 % 2
        buf = g16[:, par * 4 * S: par * 4 * S + 4 * S]
        eng = sy if par == 0 else sc
        sem = pa if par == 0 else pb
        if g4 >= 2:
            eng.wait_ge(sv.h, pmile[g4 - 2])
        else:
            eng.wait_ge(sv.h, sv.n)
        n_ = sem.inc(eng.dma_start(
            out=buf, in_=spill[:, g4 * 4 * S: (g4 + 1) * 4 * S]), 16)
        return buf, sem, n_

    pmile = []
    for g4 in range(KI // 4):
        gbuf, gsem, gn = spill_load(g4, gate_sp, pmile)
        v.wait_ge(gsem.h, gn)
        for oti in range(4):
            ot = g4 * 4 + oti
            gsl_in = gbuf[:, oti * S: (oti + 1) * S]
            apar = ot % 2
            ta = t32a[:, apar * S: apar * S + S]
            quant3(ta, gsl_in, 27, 26, t32b[:], ti32[:])
            n_ = sv.inc(v.nop())
            sc.wait_ge(sv.h, n_)
            n_s = ss.inc(sc.activation(t32b[:], ta, AF.Sigmoid, bias=0.0, scale=1.0))
            v.wait_ge(ss.h, n_s)
            v.tensor_scalar(t32b[:], t32b[:], 65536.0, None, op0=Alu.mult)
            v.drain()
            v.tensor_copy(ti32[:], t32b[:])
            v.drain()
            v.tensor_scalar(t32b[:], ti32[:], 65535.0, 1.0 / 65536.0,
                            op0=Alu.min, op1=Alu.mult)
            v.drain()
            v.tensor_tensor(gate2[:, ot * S:(ot + 1) * S], ta, t32b[:], op=Alu.mult)
            v.drain()
            v.tensor_reduce(small[:, 0:1], gate2[:, ot * S:(ot + 1) * S],
                            axis=AX.X, op=Alu.max, apply_absolute_value=True)
            v.drain()
            if ot == 0:
                v.tensor_copy(small[:, 1:2], small[:, 0:1])
            else:
                v.tensor_tensor(small[:, 1:2], small[:, 1:2], small[:, 0:1], op=Alu.max)
            v.drain()
        pmile.append(sv.inc(v.nop()))
    sy.wait_ge(sv.h, sv.n)
    ng = dma(grow[0:1, 0:128], small[:, 1:2])
    v.wait_ge(ld.h, ng)
    v.tensor_reduce(ccst[0:1, 0:1], grow[0:1, 0:128], axis=AX.X, op=Alu.max)
    v.drain()
    sv.inc(v.nop())
    exchange(2, 1)
    make_scale(ccst[0:1, 0:1], QMAX_A, 28, 29)

    pmile = []
    for g4 in range(KI // 4):
        gbuf, gsem, gn = spill_load(g4, up_sp, pmile)
        v.wait_ge(gsem.h, gn)
        for oti in range(4):
            ot = g4 * 4 + oti
            apar = ot % 2
            ta = t32a[:, apar * S: apar * S + S]
            quant3(ta, gbuf[:, oti * S: (oti + 1) * S], 25, 24, t32b[:], ti32[:])
            gsl = gate2[:, ot * S:(ot + 1) * S]
            v.tensor_scalar(t32b[:], gsl, small[:, 29:30], None, op0=Alu.mult)
            v.drain()
            v.tensor_copy(ti32[:], t32b[:])
            v.drain()
            v.tensor_scalar(t32b[:], ti32[:], small[:, 28:29], None, op0=Alu.mult)
            v.drain()
            v.tensor_tensor(gsl, ta, t32b[:], op=Alu.mult)
            v.drain()
            v.tensor_reduce(small[:, 0:1], gsl, axis=AX.X, op=Alu.max,
                            apply_absolute_value=True)
            v.drain()
            if ot == 0:
                v.tensor_copy(small[:, 2:3], small[:, 0:1])
            else:
                v.tensor_tensor(small[:, 2:3], small[:, 2:3], small[:, 0:1], op=Alu.max)
            v.drain()
        pmile.append(sv.inc(v.nop()))
    sy.wait_ge(sv.h, sv.n)
    ng = dma(grow[0:1, 0:128], small[:, 2:3])
    v.wait_ge(ld.h, ng)
    v.tensor_reduce(ccst[0:1, 0:1], grow[0:1, 0:128], axis=AX.X, op=Alu.max)
    v.drain()
    sv.inc(v.nop())
    exchange(3, 1)
    make_scale(ccst[0:1, 0:1], QMAX_A, 30, 31)

    for ot in range(KI):
        gsl = gate2[:, ot * S:(ot + 1) * S]
        quant3(gsl.bitcast(BF16), gsl, 31, 30, t32a[:, 0:S], ti32[:])
    sv.inc(v.nop())
    oq_all = gate2

    barrier()
    if PHASE_LIMIT <= 6:
        debug_exit(gate2[:].bitcast(BF16), KD * S, False)
        return nc, gstack
    ew_stack.close()

    wd_bf = PS_(nc.sbuf_tensor("wd_bf", [128, KI * D], BF16))
    dq2_stack = ExitStack()
    dq2_bufs = make_dq_bufs(dq2_stack, D, "b")
    dequant_weight(wd_src, wdT, wd_bf, KI, D, dq2_bufs)
    barrier()
    dq2_stack.close()

    out_stack = ExitStack()
    O_ = out_stack.enter_context
    ln1b = O_(nc.sbuf_tensor("ln1b", [128, TCW], F32))
    r2 = O_(nc.sbuf_tensor("r2", [128, KD * 512], F32))
    rbf2 = O_(nc.sbuf_tensor("rbf2", [128, 512], BF16))
    rsq2 = O_(nc.sbuf_tensor("rsq2", [128, 512], BF16))
    a2 = O_(nc.sbuf_tensor("a2", [128, 512], F32))
    b2 = O_(nc.sbuf_tensor("b2", [128, 512], F32))
    lrow2 = O_(nc.sbuf_tensor("lrow2", [1, 1536], F32))

    lmile = []
    for tc in range(TC):
        v.wait_ge(st.h, st.n)
        lpar = 0
        lsl = ln1b[:]
        lsem = pa
        sy.wait_ge(sv.h, sv.n)
        nl1 = lsem.inc(sy.dma_start(
            out=lsl, in_=ln1_sp[:, tc * TCW: (tc + 1) * TCW]), 16)
        for do in range(KD):
            ps = pA if do % 2 == 0 else pB
            if do >= 2:
                te.wait_ge(sv.h, sv.n)
            for ot in range(KI):
                mi = te.matmul(ps[:],
                          wd_bf[:, ot * D + do * 128: ot * D + do * 128 + 128],
                          oq_all[:, ot * S + tc * 512: ot * S + tc * 512 + 512]
                          .bitcast(BF16),
                          start=(ot == 0), stop=(ot == KI - 1))
            n_grp = tt.inc(mi)
            v.wait_ge(tt.h, n_grp)
            v.wait_ge(lsem.h, nl1)
            rsl = r2[:, do * 512:(do + 1) * 512]
            v.tensor_tensor(rsl, ps[:],
                            ln1b[:, do * 512: do * 512 + 512], op=Alu.add)
            v.drain()
            v.tensor_copy(rbf2[:], rsl)
            v.tensor_tensor(rsq2[:], rsl, rsl, op=Alu.mult)
            v.drain()
            n_r = sv.inc(v.nop())
            te.wait_ge(sv.h, n_r)
            te.matmul(pS1[:], ones_col[:, 0:1], rbf2[:],
                      start=(do == 0), stop=(do == KD - 1))
            mi = te.matmul(pS2[:], ones_col[:, 0:1], rsq2[:],
                      start=(do == 0), stop=(do == KD - 1))
            tt.inc(mi)
        v.wait_ge(tt.h, tt.n)
        mrow = lrow2[0:1, 0:512]
        urow = lrow2[0:1, 512:1024]
        brow = lrow2[0:1, 1024:1536]
        v.tensor_scalar(mrow, pS1[:], 1.0 / D, None, op0=Alu.mult)
        v.drain()
        v.tensor_tensor(urow, mrow, mrow, op=Alu.mult)
        v.drain()
        v.scalar_tensor_tensor(urow, pS2[:], 1.0 / D, urow,
                               op0=Alu.mult, op1=Alu.subtract)
        v.drain()
        n_v = sv.inc(v.nop())
        sc.wait_ge(sv.h, n_v)
        n_s = ss.inc(sc.activation(urow, urow, AF.Sqrt, bias=small[0:1, 34:35], scale=1.0))
        v.wait_ge(ss.h, n_s)
        v.reciprocal(urow, urow)
        v.drain()
        v.scalar_tensor_tensor(brow, mrow, -1.0, urow, op0=Alu.mult, op1=Alu.mult)
        v.drain()
        n_ab = sv.inc(v.nop())
        gp.wait_ge(sv.h, n_ab)
        dma_g(a2[:], urow.unsqueeze(1).broadcast_to([1, 128, 512]))
        nbb = dma_g(b2[:], brow.unsqueeze(1).broadcast_to([1, 128, 512]))
        v.wait_ge(ldg.h, nbb)
        r23 = r2[:].rearrange("p (k f) -> p k f", f=512)
        v.tensor_tensor(r23, r23, a2[:].unsqueeze(1).broadcast_to([128, KD, 512]),
                        op=Alu.mult)
        v.drain()
        v.tensor_tensor(r23, r23, b2[:].unsqueeze(1).broadcast_to([128, KD, 512]),
                        op=Alu.add)
        v.drain()
        n_fin = sv.inc(v.nop())
        lmile.append(n_fin)
        sy.wait_ge(sv.h, n_fin)
        dma_out(yout[:, tc * TCW: (tc + 1) * TCW], r2[:])

    sy.wait_ge(st.h, st.n)
    sy.wait_ge(lds.h, lds.n)
    sy.wait_ge(ldg.h, ldg.n)
    sy.wait_ge(ld.h, ld.n)
    sy.wait_ge(sp0.h, sp0.n)
    sy.wait_ge(sp1.h, sp1.n)

    out_stack.close()
    pass_stack.close()
    return nc, gstack


def _prep_inputs(L0_attn_x, L0_attn_Wq, L0_attn_bq, L0_attn_Wk, L0_attn_bk,
                 L0_attn_Wv, L0_attn_bv, L0_attn_Wm, L0_attn_bm,
                 L0_ffn_w_gate, L0_ffn_w_up, L0_ffn_w_down):
    def feat_major(w):
        din, n = w.shape
        return np.ascontiguousarray(
            w.reshape(KD, 128, n).transpose(1, 0, 2).reshape(128, KD * n))

    def wT_layout(w, nk):
        O, I = w.shape
        wt = np.ascontiguousarray(np.asarray(w).T)
        return np.ascontiguousarray(
            wt.reshape(nk, 128, O).transpose(1, 0, 2).reshape(128, nk * O))

    def src_layout(w):
        O, I = w.shape
        nblk = I // 32
        o4 = O // 4
        wb = np.asarray(w, np.float32).reshape(O, nblk, 32)
        out = np.zeros((128, 24576), np.float32)
        for ib in range(nblk):
            t, g = ib // 4, ib % 4
            t3, trow = t // 8, t % 8
            for s4 in range(4):
                p_src = 16 * trow + 4 * s4 + g
                seg = wb[s4 * o4:(s4 + 1) * o4, ib, :]
                base = t3 * (o4 * 32)
                out[p_src, base:base + o4 * 32] = seg.reshape(-1)
        return out

    f32 = lambda a: np.asarray(a, np.float32)
    common = dict(
        wq=feat_major(f32(L0_attn_Wq)), wk=feat_major(f32(L0_attn_Wk)),
        wv=feat_major(f32(L0_attn_Wv)), wm=feat_major(f32(L0_attn_Wm)),
        bqc=np.ascontiguousarray(f32(L0_attn_bq).reshape(KD, 128).T),
        bkc=np.ascontiguousarray(f32(L0_attn_bk).reshape(KD, 128).T),
        bmc=np.ascontiguousarray(f32(L0_attn_bm).reshape(KD, 128).T),
        bvr=f32(L0_attn_bv).reshape(1, D),
        wupT=wT_layout(f32(L0_ffn_w_up), KD),
        wgT=wT_layout(f32(L0_ffn_w_gate), KD),
        wdT=wT_layout(f32(L0_ffn_w_down), KI),
        wup_src=src_layout(f32(L0_ffn_w_up)),
        wg_src=src_layout(f32(L0_ffn_w_gate)),
        wd_src=src_layout(f32(L0_ffn_w_down)),
    )
    x = f32(L0_attn_x)
    in_maps = []
    for b in range(8):
        xb = np.ascontiguousarray(
            x[b].T.reshape(KD, 128, TC, 512).transpose(1, 2, 0, 3)
            .reshape(128, KD * S))
        m = dict(common)
        m["xin"] = xb
        in_maps.append(m)
    return in_maps


def kernel(L0_attn_x, L0_attn_Wq, L0_attn_bq, L0_attn_Wk, L0_attn_bk,
           L0_attn_Wv, L0_attn_bv, L0_attn_Wm, L0_attn_bm,
           L0_ffn_w_gate, L0_ffn_w_up, L0_ffn_w_down):
    in_maps = _prep_inputs(L0_attn_x, L0_attn_Wq, L0_attn_bq, L0_attn_Wk, L0_attn_bk,
                           L0_attn_Wv, L0_attn_bv, L0_attn_Wm, L0_attn_bm,
                           L0_ffn_w_gate, L0_ffn_w_up, L0_ffn_w_down)
    nc, gstack = build_program()
    try:
        res = run_bass_kernel_spmd(nc, in_maps, core_ids=list(range(8)))
    finally:
        try:
            gstack.close()
        except Exception:
            pass
    outs = []
    for b in range(8):
        y = res.results[b]["yout"]
        y = (y.reshape(128, TC, KD, 512).transpose(1, 3, 2, 0)
             .reshape(S, D))
        outs.append(y)
    return np.ascontiguousarray(np.stack(outs, 0)).astype(np.float32)


# version 08
# speedup vs baseline: 1.2463x; 1.0126x over previous
import numpy as np
from contextlib import ExitStack

import concourse.bass as bass
import concourse.mybir as mybir
from concourse.bass_utils import run_bass_kernel_spmd

F32 = mybir.dt.float32
I32 = mybir.dt.int32
F16 = mybir.dt.float16
BF16 = mybir.dt.bfloat16
Alu = mybir.AluOpType
AF = mybir.ActivationFunctionType
AX = mybir.AxisListType

D = 1024
S = 2048
H = 16
I3 = 3072
KD = D // 128
KI = I3 // 128
TC = S // 512
NKT = S // 128
QMAX_A = 32767.0
EPS = 1e-5
TCW = KD * 512


def TX(k, tc):
    return tc * TCW + k * 512
PHASE_LIMIT = 99


class Sem:
    def __init__(self, nc, stack, name):
        self.h = stack.enter_context(nc.semaphore(name))
        self.n = 0

    def inc(self, inst, k=1):
        inst.then_inc(self.h, k)
        self.n += k
        return self.n


def build_program():
    nc = bass.Bass("TRN2", target_bir_lowering=False)
    gstack = ExitStack()
    G = gstack.enter_context

    xin = nc.dram_tensor("xin", [128, KD * S], F32, kind="ExternalInput").ap()
    wq_h = nc.dram_tensor("wq", [128, KD * D], F32, kind="ExternalInput").ap()
    wk_h = nc.dram_tensor("wk", [128, KD * D], F32, kind="ExternalInput").ap()
    wv_h = nc.dram_tensor("wv", [128, KD * D], F32, kind="ExternalInput").ap()
    wm_h = nc.dram_tensor("wm", [128, KD * D], F32, kind="ExternalInput").ap()
    bqc = nc.dram_tensor("bqc", [128, KD], F32, kind="ExternalInput").ap()
    bkc = nc.dram_tensor("bkc", [128, KD], F32, kind="ExternalInput").ap()
    bmc = nc.dram_tensor("bmc", [128, KD], F32, kind="ExternalInput").ap()
    bvr = nc.dram_tensor("bvr", [1, D], F32, kind="ExternalInput").ap()
    wupT = nc.dram_tensor("wupT", [128, KD * I3], F32, kind="ExternalInput").ap()
    wgT = nc.dram_tensor("wgT", [128, KD * I3], F32, kind="ExternalInput").ap()
    wdT = nc.dram_tensor("wdT", [128, KI * D], F32, kind="ExternalInput").ap()
    wup_src = nc.dram_tensor("wup_src", [128, 24576], F32, kind="ExternalInput").ap()
    wg_src = nc.dram_tensor("wg_src", [128, 24576], F32, kind="ExternalInput").ap()
    wd_src = nc.dram_tensor("wd_src", [128, 24576], F32, kind="ExternalInput").ap()
    yout = nc.dram_tensor("yout", [128, KD * S], F32, kind="ExternalOutput").ap()
    ln1_sp = nc.dram_tensor("ln1_sp", [128, KD * S], F32, kind="Internal").ap()
    up_sp = nc.dram_tensor("up_sp", [128, KI * S], F16, kind="Internal").ap()
    gate_sp = nc.dram_tensor("gate_sp", [128, KI * S], F16, kind="Internal").ap()
    cc_bufs = [(nc.dram_tensor(f"cc{i}i", [1, 2], F32, kind="Internal").ap(),
                nc.dram_tensor(f"cc{i}o", [1, 2], F32, kind="Internal",
                               addr_space="Shared").ap()) for i in range(4)]

    v, sc, te, gp, sy = nc.vector, nc.scalar, nc.tensor, nc.gpsimd, nc.sync

    p = ExitStack()
    ld = Sem(nc, gstack, "ld")
    ldg = Sem(nc, gstack, "ldg")
    lds = Sem(nc, gstack, "lds")
    st = Sem(nc, gstack, "st")
    sp0 = Sem(nc, gstack, "sp0")
    sp1 = Sem(nc, gstack, "sp1")
    pa = Sem(nc, gstack, "pa")
    pb = Sem(nc, gstack, "pb")
    pc = Sem(nc, gstack, "pc")
    pd = Sem(nc, gstack, "pd")
    sv = Sem(nc, gstack, "sv")
    ss = Sem(nc, gstack, "ss")
    tt = Sem(nc, gstack, "tt")
    gg = Sem(nc, gstack, "gg")
    gw = Sem(nc, gstack, "gw")
    ccs = Sem(nc, gstack, "ccs")

    def dma(out, in_):
        return ld.inc(sy.dma_start(out=out, in_=in_), 16)

    def dma_g(out, in_):
        return ldg.inc(gp.dma_start(out=out, in_=in_), 16)

    def dma_s(out, in_):
        return lds.inc(sc.dma_start(out=out, in_=in_), 16)

    def dma_out(out, in_):
        return st.inc(sy.dma_start(out=out, in_=in_), 16)

    def barrier():
        engs = [te, v, sc, gp, sy]
        base = gg.n
        for e in engs:
            gg.inc(e.nop(), 1)
        for e in engs:
            e.wait_ge(gg.h, base + len(engs))

    def debug_exit(src_ap, nfree, is_f32):
        barrier()
        dbg = nc.sbuf_tensor("dbgbuf", [128, 2048], F32, side="right")
        dbgh = dbg.__enter__()
        nch = min(nfree, KD * S) // 2048
        for c in range(nch):
            if c > 0:
                sy.wait_ge(sv.h, sv.n)
            v.wait_ge(st.h, st.n)
            v.tensor_copy(dbgh[:], src_ap[:, c * 2048:(c + 1) * 2048])
            v.drain()
            n_ = sv.inc(v.nop())
            sy.wait_ge(sv.h, n_)
            dma_out(yout[:, c * 2048:(c + 1) * 2048], dbgh[:])
        sy.wait_ge(st.h, st.n)

    small = G(nc.sbuf_tensor("small", [128, 40], F32))
    ccst = G(nc.sbuf_tensor("ccst", [1, 8], F32))
    grow = G(nc.sbuf_tensor("grow", [1, 256], F32))
    ones_col = G(nc.sbuf_tensor("ones_col", [128, 2], BF16))
    bstage = G(nc.sbuf_tensor("bstage", [128, 3 * KD], F32))

    pA = G(nc.psum_tensor("pA", [128, 512], F32))
    pB = G(nc.psum_tensor("pB", [128, 512], F32))
    pC = G(nc.psum_tensor("pC", [128, 512], F32))
    pD = G(nc.psum_tensor("pD", [128, 512], F32))
    pV0 = G(nc.psum_tensor("pV0", [66, 512], F32))
    pV1 = G(nc.psum_tensor("pV1", [66, 512], F32))
    pS1 = G(nc.psum_tensor("pS1", [1, 512], F32))
    pS2 = G(nc.psum_tensor("pS2", [1, 512], F32))

    dma(bstage[:, 0:KD], bqc)
    dma(bstage[:, KD:2 * KD], bkc)
    dma(bstage[:, 2 * KD:3 * KD], bmc)
    v.memset(ones_col[:], 1.0)
    v.memset(small[:, 34:35], EPS)
    v.memset(ccst[:], 0.0)
    v.drain()
    sv.inc(v.nop())

    import ml_dtypes as _mld
    _e = np.zeros((128, 32 * 128), np.float32)
    for vv in range(32):
        for pp in range(128):
            kk = 4 * vv + pp // 32
            if kk < 128:
                _e[kk, vv * 128 + pp] = 1.0
    e_dram = nc.inline_tensor(_e.astype(_mld.bfloat16), name="e_repl").ap()
    e_stack = ExitStack()
    e_all = e_stack.enter_context(
        nc.sbuf_tensor("e_all", [128, 32 * 128], BF16, side="right"))
    dma(e_all[:], e_dram)

    attn_stack = ExitStack()
    attnT = attn_stack.enter_context(
        nc.sbuf_tensor("attnT", [128, KD * S], BF16, side="right"))
    xq = attnT

    qkv_stack = ExitStack()
    Q_ = qkv_stack.enter_context
    qT = Q_(nc.sbuf_tensor("qT", [128, KD * S], BF16))
    kT = Q_(nc.sbuf_tensor("kT", [128, KD * S], BF16))
    V66 = Q_(nc.sbuf_tensor("V66", [128, NKT * 1056], BF16))

    x_stack = ExitStack()
    X_ = x_stack.enter_context
    xbf = X_(nc.sbuf_tensor("xbf", [128, KD * S], BF16))
    wproj = X_(nc.sbuf_tensor("wproj", [128, KD * D], BF16))
    wstage = X_(nc.sbuf_tensor("wstage", [128, 2 * 1024], F32))
    bvrep = X_(nc.sbuf_tensor("bvrep", [128, D], F32))

    dma(bvrep[:], bvr.unsqueeze(1).broadcast_to([1, 128, D]))
    sy.wait_ge(ld.h, ld.n)

    def load_weight_bf16(src_ap, dest_sb, nfree, csz=1024):
        mile = []
        for c in range(nfree // csz):
            par = c % 2
            buf = wstage[:, par * 1024: par * 1024 + csz]
            eng = sy if par == 0 else sc
            sem = pa if par == 0 else pb
            if c >= 2:
                eng.wait_ge(sv.h, mile[c - 2])
            else:
                eng.wait_ge(sv.h, sv.n)
            n_ = sem.inc((sy if par == 0 else sc).dma_start(
                out=buf, in_=src_ap[:, c * csz:(c + 1) * csz]), 16)
            v.wait_ge(sem.h, n_)
            v.tensor_copy(dest_sb[:, c * csz:(c + 1) * csz], buf)
            v.drain()
            mile.append(sv.inc(v.nop()))
        return sv.n

    load_weight_bf16(xin, xbf, KD * S)

    def qk_proj(wsrc, bias_col, dest):
        v.wait_ge(tt.h, tt.n)
        wready = load_weight_bf16(wsrc, wproj, KD * D)
        te.wait_ge(sv.h, wready)
        ss_b = ss.n
        te.wait_ge(ss.h, ss_b)
        sc.wait_ge(ld.h, 5 * 16)
        tt_b = tt.n
        g = 0
        for do in range(KD):
            for tc in range(TC):
                ps = pA if g % 2 == 0 else pB
                if g >= 2:
                    te.wait_ge(ss.h, ss_b + g - 1)
                for ki in range(KD):
                    mi = te.matmul(ps[:],
                              wproj[:, ki * D + do * 128: ki * D + do * 128 + 128],
                              xbf[:, TX(ki, tc): TX(ki, tc) + 512],
                              start=(ki == 0), stop=(ki == KD - 1))
                tt.inc(mi)
                sc.wait_ge(tt.h, tt_b + g + 1)
                ss.inc(sc.activation(
                    dest[:, TX(do, tc): TX(do, tc) + 512], ps[:],
                    AF.Identity,
                    bias=bstage[:, bias_col + do: bias_col + do + 1], scale=1.0))
                g += 1

    qk_proj(wq_h, 0, qT)
    qk_proj(wk_h, KD, kT)

    v.wait_ge(tt.h, tt.n)
    wready = load_weight_bf16(wv_h, wproj, KD * D)
    gp.wait_ge(sv.h, sv.n)
    gw.inc(gp.memset(V66[:], 1.0))
    te.wait_ge(sv.h, wready)
    te.wait_ge(gw.h, gw.n)
    te.wait_ge(ss.h, ss.n)
    sv_b = sv.n
    tt_b = tt.n
    g = 0
    for tt_i in range(NKT):
        for dc in range(2):
            ps = pA if g % 2 == 0 else pB
            if g >= 2:
                te.wait_ge(sv.h, sv_b + g - 1)
            for ki in range(KD):
                mi = te.matmul(ps[:],
                          xbf[:, TX(ki, tt_i // 4) + (tt_i % 4) * 128:
                                 TX(ki, tt_i // 4) + (tt_i % 4) * 128 + 128],
                          wproj[:, ki * D + dc * 512: ki * D + dc * 512 + 512],
                          start=(ki == 0), stop=(ki == KD - 1))
            tt.inc(mi)
            v.wait_ge(tt.h, tt_b + g + 1)
            out_ap = V66[:, tt_i * 1056 + dc * 8 * 66: tt_i * 1056 + (dc * 8 + 8) * 66]
            out_ap = out_ap.rearrange("p (h c) -> p h c", c=66)[:, :, 0:64]
            ps3 = ps[:].rearrange("p (h c) -> p h c", c=64)
            bv3 = bvrep[:, dc * 512:(dc + 1) * 512].rearrange("p (h c) -> p h c", c=64)
            v.tensor_tensor(out_ap, ps3, bv3, op=Alu.add)
            v.drain()
            sv.inc(v.nop())
            g += 1

    barrier()
    if PHASE_LIMIT <= 1:
        debug_exit(qT[:], KD * S, False)
        return nc, gstack
    x_stack.close()

    a_stack = ExitStack()
    A_ = a_stack.enter_context
    probs = A_(nc.sbuf_tensor("probs", [128, 2 * NKT * 512], BF16))
    recband = A_(nc.sbuf_tensor("recband", [64, 2 * 512], F32))
    pvrec = A_(nc.sbuf_tensor("pvrec", [1, 2 * 512], F32))

    sbanks = [pA, pB, pC, pD]
    pv_norm_sv = [0, 0]
    pv_tt_par = [tt.n, tt.n]
    rec_sv_par = [sv.n, sv.n]
    ss_b4 = ss.n
    gexp = [0]
    exp_mile = []
    iters = [(h, qc) for h in range(H) for qc in range(TC)]

    def p4_scores(it):
        h, qc = iters[it]
        ko, po = h // 2, (h % 2) * 64
        par = it % 2
        pbase = par * (NKT * 512)
        kmax = 4 * (qc + 1)
        sc.wait_ge(tt.h, pv_tt_par[par])
        for kb in range(kmax):
            m = gexp[0] + kb
            psb = sbanks[m % 4]
            if m >= 4:
                te.wait_ge(ss.h, ss_b4 + m - 3)
            kx = TX(ko, kb // 4) + (kb % 4) * 128
            mi = te.matmul(psb[:],
                      kT[po:po + 64, kx: kx + 128],
                      qT[po:po + 64, TX(ko, qc): TX(ko, qc) + 512],
                      start=True, stop=True)
            tt.inc(mi)
            sc.wait_ge(tt.h, tt.n)
            ss.inc(sc.activation(probs[:, pbase + kb * 512: pbase + (kb + 1) * 512],
                                 psb[:], AF.Exp, bias=0.0, scale=0.125))
        gexp[0] += kmax
        exp_mile.append(ss_b4 + gexp[0])
        gp.wait_ge(ss.h, exp_mile[-1])
        for kb in [kb for kb in range(kmax) if kb * 128 + 127 > qc * 512]:
            mi_g = gp.affine_select(
                probs[:, pbase + kb * 512: pbase + (kb + 1) * 512],
                probs[:, pbase + kb * 512: pbase + (kb + 1) * 512],
                pattern=[[1, 512]], compare_op=Alu.is_ge, fill=0.0,
                base=qc * 512 - kb * 128, channel_multiplier=-1)
        return gw.inc(mi_g)

    def p4_pv(it, gw_mask):
        h, qc = iters[it]
        ko, po = h // 2, (h % 2) * 64
        par = it % 2
        pbase = par * (NKT * 512)
        kmax = 4 * (qc + 1)
        ps_pv = pV0 if par == 0 else pV1
        te.wait_ge(gw.h, gw_mask)
        te.wait_ge(sv.h, pv_norm_sv[par])
        for kb in range(kmax):
            mi = te.matmul(ps_pv[:],
                      V66[:, kb * 1056 + h * 66: kb * 1056 + h * 66 + 66],
                      probs[:, pbase + kb * 512: pbase + (kb + 1) * 512],
                      start=(kb == 0), stop=(kb == kmax - 1))
        pv_tt_par[par] = tt.inc(mi)
        rsl = pvrec[0:1, par * 512: par * 512 + 512]
        v.wait_ge(tt.h, pv_tt_par[par])
        v.reciprocal(rsl, ps_pv[64:65, :])
        v.drain()
        nrec = sv.inc(v.nop())
        gp.wait_ge(sv.h, nrec)
        gp.wait_ge(sv.h, rec_sv_par[par])
        rb = recband[:, par * 512: par * 512 + 512]
        nb = dma_g(rb, rsl.unsqueeze(1).broadcast_to([1, 64, 512]))
        v.wait_ge(ldg.h, nb)
        v.tensor_tensor(
            attnT[po:po + 64, TX(ko, qc): TX(ko, qc) + 512],
            ps_pv[0:64, :], rb, op=Alu.mult)
        v.drain()
        rec_sv_par[par] = pv_norm_sv[par] = sv.inc(v.nop())

    gw_prev = p4_scores(0)
    for it in range(len(iters)):
        gw_next = p4_scores(it + 1) if it + 1 < len(iters) else None
        p4_pv(it, gw_prev)
        gw_prev = gw_next

    barrier()
    if PHASE_LIMIT <= 2:
        debug_exit(attnT[:], KD * S, False)
        return nc, gstack
    a_stack.close()
    qkv_stack.close()

    p5_stack = ExitStack()
    P5 = p5_stack.enter_context
    wproj2 = P5(nc.sbuf_tensor("wproj2", [128, KD * D], BF16))
    wstage2 = P5(nc.sbuf_tensor("wstage2", [128, 2048], F32))
    rbuf = P5(nc.sbuf_tensor("rbuf", [128, KD * S], F32))
    xres = P5(nc.sbuf_tensor("xres", [128, 2 * TCW], F32))
    rbf = P5(nc.sbuf_tensor("rbf", [128, 512], BF16))
    rsq = P5(nc.sbuf_tensor("rsq", [128, 512], BF16))
    arep = P5(nc.sbuf_tensor("arep", [128, 512], F32))
    brep = P5(nc.sbuf_tensor("brep", [128, 512], F32))
    q32 = P5(nc.sbuf_tensor("q32", [128, TCW], F32))
    qi32 = P5(nc.sbuf_tensor("qi32", [128, TCW], I32))
    lrow = P5(nc.sbuf_tensor("lrow", [1, 1536], F32))

    def load_weight_bf16_p5(src_ap, dest_sb, nfree, csz=1024):
        mile = []
        for c in range(nfree // csz):
            par = c % 2
            buf = wstage2[:, par * 1024: par * 1024 + csz]
            eng = sy if par == 0 else sc
            sem = pa if par == 0 else pb
            if c >= 2:
                eng.wait_ge(sv.h, mile[c - 2])
            else:
                eng.wait_ge(sv.h, sv.n)
            n_ = sem.inc(eng.dma_start(out=buf, in_=src_ap[:, c * csz:(c + 1) * csz]), 16)
            v.wait_ge(sem.h, n_)
            v.tensor_copy(dest_sb[:, c * csz:(c + 1) * csz], buf)
            v.drain()
            mile.append(sv.inc(v.nop()))
        return sv.n

    load_weight_bf16_p5(wm_h, wproj2, KD * D)
    te.wait_ge(sv.h, sv.n)

    xmile = []
    for tc in range(TC):
        xeng = sy if tc % 2 == 0 else sc
        xsem = pa if tc % 2 == 0 else pb
        if tc >= 2:
            xeng.wait_ge(sv.h, xmile[tc - 2])
        else:
            xeng.wait_ge(sv.h, sv.n)
        nx = xsem.inc(xeng.dma_start(
            out=xres[:, (tc % 2) * TCW: (tc % 2) * TCW + TCW],
            in_=xin[:, tc * TCW: (tc + 1) * TCW]), 16)
        for do in range(KD):
            ps = pA if do % 2 == 0 else pB
            if do >= 2:
                te.wait_ge(sv.h, sv.n)
            for ki in range(KD):
                mi = te.matmul(ps[:],
                          wproj2[:, ki * D + do * 128: ki * D + do * 128 + 128],
                          attnT[:, TX(ki, tc): TX(ki, tc) + 512],
                          start=(ki == 0), stop=(ki == KD - 1))
            n_grp = tt.inc(mi)
            sc.wait_ge(tt.h, n_grp)
            n_ev = ss.inc(sc.activation(
                rbuf[:, TX(do, tc): TX(do, tc) + 512], ps[:],
                AF.Identity, bias=bstage[:, 2 * KD + do: 2 * KD + do + 1], scale=1.0))
            v.wait_ge(ss.h, n_ev)
            v.wait_ge(xsem.h, nx)
            v.wait_ge(tt.h, tt.n)
            rsl = rbuf[:, TX(do, tc): TX(do, tc) + 512]
            v.tensor_tensor(rsl, rsl,
                            xres[:, (tc % 2) * TCW + do * 512:
                                    (tc % 2) * TCW + do * 512 + 512], op=Alu.add)
            v.drain()
            v.tensor_copy(rbf[:], rsl)
            v.tensor_tensor(rsq[:], rsl, rsl, op=Alu.mult)
            v.drain()
            n_r = sv.inc(v.nop())
            te.wait_ge(sv.h, n_r)
            te.matmul(pS1[:], ones_col[:, 0:1], rbf[:],
                      start=(do == 0), stop=(do == KD - 1))
            mi = te.matmul(pS2[:], ones_col[:, 0:1], rsq[:],
                      start=(do == 0), stop=(do == KD - 1))
            tt.inc(mi)
        v.wait_ge(tt.h, tt.n)
        mrow = lrow[0:1, 0:512]
        urow = lrow[0:1, 512:1024]
        brow = lrow[0:1, 1024:1536]
        v.tensor_scalar(mrow, pS1[:], 1.0 / D, None, op0=Alu.mult)
        v.drain()
        v.tensor_tensor(urow, mrow, mrow, op=Alu.mult)
        v.drain()
        v.scalar_tensor_tensor(urow, pS2[:], 1.0 / D, urow,
                               op0=Alu.mult, op1=Alu.subtract)
        v.drain()
        n_v = sv.inc(v.nop())
        sc.wait_ge(sv.h, n_v)
        n_s = ss.inc(sc.activation(urow, urow, AF.Sqrt, bias=small[0:1, 34:35], scale=1.0))
        v.wait_ge(ss.h, n_s)
        v.reciprocal(urow, urow)
        v.drain()
        v.scalar_tensor_tensor(brow, mrow, -1.0, urow, op0=Alu.mult, op1=Alu.mult)
        v.drain()
        n_ab = sv.inc(v.nop())
        gp.wait_ge(sv.h, n_ab)
        dma_g(arep[:], urow.unsqueeze(1).broadcast_to([1, 128, 512]))
        nbb = dma_g(brep[:], brow.unsqueeze(1).broadcast_to([1, 128, 512]))
        v.wait_ge(ldg.h, nbb)
        win = rbuf[:, tc * TCW: (tc + 1) * TCW]
        win3 = win.rearrange("p (k f) -> p k f", f=512)
        v.tensor_tensor(win3, win3,
                        arep[:].unsqueeze(1).broadcast_to([128, KD, 512]),
                        op=Alu.mult)
        v.drain()
        v.tensor_tensor(win3, win3,
                        brep[:].unsqueeze(1).broadcast_to([128, KD, 512]),
                        op=Alu.add)
        v.drain()
        v.tensor_reduce(small[:, tc:tc + 1], win, axis=AX.X, op=Alu.max,
                        apply_absolute_value=True)
        v.drain()
        xmile.append(sv.inc(v.nop()))

    v.tensor_reduce(small[:, 8:9], small[:, 0:TC], axis=AX.X, op=Alu.max)
    v.drain()
    n_m = sv.inc(v.nop())
    sy.wait_ge(sv.h, n_m)
    ng = dma(grow[0:1, 0:128], small[:, 8:9])
    v.wait_ge(ld.h, ng)
    v.tensor_reduce(ccst[0:1, 0:1], grow[0:1, 0:128], axis=AX.X, op=Alu.max)
    v.drain()
    sv.inc(v.nop())

    def exchange(idx, nvals):
        ci, co = cc_bufs[idx]
        sy.wait_ge(sv.h, sv.n)
        nin = dma(ci[0:1, 0:2], ccst[0:1, 0:2])
        gp.wait_ge(ld.h, nin)
        ccs.inc(gp.collective_compute(
            "AllReduce", Alu.max, replica_groups=[[0, 1, 2, 3, 4, 5, 6, 7]],
            ins=[ci], outs=[co]))
        sy.wait_ge(ccs.h, ccs.n)
        nout = dma(ccst[0:1, 0:2], co[0:1, 0:2])
        v.wait_ge(ld.h, nout)

    def make_scale(src_ap, qmax, s_col, r_col):
        v.tensor_scalar(ccst[0:1, 4:5], src_ap, 1.0 / qmax, 1e-12,
                        op0=Alu.mult, op1=Alu.max)
        v.drain()
        v.reciprocal(ccst[0:1, 5:6], ccst[0:1, 4:5])
        v.drain()
        n_ = sv.inc(v.nop())
        sy.wait_ge(sv.h, n_)
        dma(small[:, s_col:s_col + 1],
            ccst[0:1, 4:5].unsqueeze(1).broadcast_to([1, 128, 1]))
        n2 = dma(small[:, r_col:r_col + 1],
                 ccst[0:1, 5:6].unsqueeze(1).broadcast_to([1, 128, 1]))
        v.wait_ge(ld.h, n2)
        sv.inc(v.nop())

    exchange(0, 1)
    make_scale(ccst[0:1, 0:1], QMAX_A, 16, 17)

    for tc in range(TC):
        rsl = rbuf[:, tc * TCW: (tc + 1) * TCW]
        v.tensor_scalar(q32[:], rsl, small[:, 17:18], None, op0=Alu.mult)
        v.drain()
        v.tensor_copy(qi32[:], q32[:])
        v.drain()
        v.tensor_scalar(xq[:, tc * TCW: (tc + 1) * TCW], qi32[:], small[:, 16:17],
                        None, op0=Alu.mult)
        v.drain()
    n_xq = sv.inc(v.nop())
    sy.wait_ge(sv.h, n_xq)
    n_ln1sp = dma_out(ln1_sp, rbuf[:])
    sy.wait_ge(st.h, n_ln1sp)

    barrier()
    if PHASE_LIMIT <= 3:
        debug_exit(rbuf[:], KD * S, True)
        return nc, gstack
    p5_stack.close()

    ffn_stack = ExitStack()
    FS = ffn_stack.enter_context
    wx_bf = FS(nc.sbuf_tensor("wx_bf", [128, KD * I3], BF16))

    def make_dq_bufs(dstack, osz, tag):
        DQ = dstack.enter_context
        return dict(
            sblk_s=DQ(nc.sbuf_tensor(f"sblk_s{tag}", [128, 768], F32)),
            sblk_r=DQ(nc.sbuf_tensor(f"sblk_r{tag}", [128, 768], F32)),
            dscr=DQ(nc.sbuf_tensor(f"dscr{tag}", [128, 768], F32)),
            shml=DQ(nc.sbuf_tensor(f"shml{tag}", [128, 3 * 768], BF16)),
            rhml=DQ(nc.sbuf_tensor(f"rhml{tag}", [128, 3 * 768], BF16)),
            wtb0=DQ(nc.sbuf_tensor(f"wtb0{tag}", [128, 3072], F32)),
            wtb1=DQ(nc.sbuf_tensor(f"wtb1{tag}", [128, 3072], F32)),
            srep0=DQ(nc.sbuf_tensor(f"srep0{tag}", [128, osz], F32)),
            srep1=(DQ(nc.sbuf_tensor(f"srep1{tag}", [128, osz], F32))
                   if osz >= 3072 else None),
            rrep0=DQ(nc.sbuf_tensor(f"rrep0{tag}", [128, osz], F32)),
            rrep1=(DQ(nc.sbuf_tensor(f"rrep1{tag}", [128, osz], F32))
                   if osz >= 3072 else None),
        )

    def dequant_weight(src_ap, wT_ap, dest_sb, n_it, osz, bufs, dest_wait_tt=None):
        sblk_s, sblk_r, dscr = bufs["sblk_s"], bufs["sblk_r"], bufs["dscr"]
        shml, rhml = bufs["shml"], bufs["rhml"]
        wtbs = [bufs["wtb0"], bufs["wtb1"]]
        dbl = bufs["srep1"] is not None
        sreps = [bufs["srep0"], bufs["srep1"] if dbl else bufs["srep0"]]
        rreps = [bufs["rrep0"], bufs["rrep1"] if dbl else bufs["rrep0"]]
        o4 = osz // 4
        mile = []
        for ch in range(8):
            par = ch % 2
            buf = wtbs[par]
            eng = sy if par == 0 else sc
            sem = pa if par == 0 else pb
            if ch >= 2:
                eng.wait_ge(sv.h, mile[ch - 2])
            else:
                eng.wait_ge(sv.h, sv.n)
            n_ = sem.inc(eng.dma_start(
                out=buf[:], in_=src_ap[:, ch * 3072:(ch + 1) * 3072]), 16)
            v.wait_ge(sem.h, n_)
            v.tensor_reduce(sblk_s[:, ch * 96:(ch + 1) * 96],
                            buf[:].rearrange("p (b c) -> p b c", c=32),
                            axis=AX.X, op=Alu.max, apply_absolute_value=True)
            v.drain()
            mile.append(sv.inc(v.nop()))
        v.tensor_scalar(sblk_s[:], sblk_s[:], 1.0 / 7.0, 1e-12,
                        op0=Alu.mult, op1=Alu.max)
        v.drain()
        v.reciprocal(sblk_r[:], sblk_s[:])
        v.drain()
        for x_f32, hml in ((sblk_s, shml), (sblk_r, rhml)):
            v.tensor_copy(hml[:, 0:768], x_f32[:])
            v.drain()
            v.tensor_tensor(dscr[:], x_f32[:], hml[:, 0:768], op=Alu.subtract)
            v.drain()
            v.tensor_copy(hml[:, 768:1536], dscr[:])
            v.drain()
            v.tensor_tensor(dscr[:], dscr[:], hml[:, 768:1536], op=Alu.subtract)
            v.drain()
            v.tensor_copy(hml[:, 1536:2304], dscr[:])
            v.drain()
        n_hml = sv.inc(v.nop())
        te.wait_ge(sv.h, n_hml)
        tmile = []
        ss_db = ss.n
        nch = osz // 512
        evac_cnt = 0
        for t in range(n_it):
            par = t % 2
            t3 = 0 if n_it == KD else t // 8
            trow = t if n_it == KD else t % 8
            srep, rrep, wtb = sreps[par if dbl else 0], rreps[par if dbl else 0], wtbs[par]
            wsem = pa if par == 0 else pb
            weng = sy if par == 0 else sc
            back = 2 if dbl else 1
            if t >= back:
                weng.wait_ge(sv.h, tmile[t - back])
                sc.wait_ge(sv.h, tmile[t - back])
            else:
                weng.wait_ge(sv.h, sv.n)
                sc.wait_ge(sv.h, sv.n)
            nw = wsem.inc(weng.dma_start(
                out=wtb[:, 0:osz], in_=wT_ap[:, t * osz:(t + 1) * osz]), 16)
            for ch in range(nch):
                o_lo = ch * 512
                for hml, rep in ((shml, srep), (rhml, rrep)):
                    psr = pC if evac_cnt % 2 == 0 else pD
                    if evac_cnt >= 2:
                        te.wait_ge(ss.h, ss_db + evac_cnt - 1)
                    specs = []
                    for xi in range(3):
                        pos = o_lo
                        while pos < o_lo + 512:
                            s4 = pos // o4
                            seg = min(o_lo + 512, (s4 + 1) * o4) - pos
                            row = 16 * trow + 4 * s4
                            specs.append((xi, pos, seg, row // 4,
                                          t3 * o4 + (pos - s4 * o4)))
                            pos += seg
                    for si, (xi, pos, seg, v32, fo) in enumerate(specs):
                        mi = te.matmul(
                            psr[:, pos - o_lo: pos - o_lo + seg],
                            e_all[:, v32 * 128: v32 * 128 + 128],
                            hml[:, xi * 768 + fo: xi * 768 + fo + seg],
                            start=(si == 0), stop=(si == len(specs) - 1))
                    tt.inc(mi)
                    sc.wait_ge(tt.h, tt.n)
                    ss.inc(sc.copy(rep[:, o_lo:o_lo + 512], psr[:]))
                    evac_cnt += 1
            v.wait_ge(wsem.h, nw)
            v.wait_ge(ss.h, ss_db + evac_cnt)
            v.tensor_tensor(wtb[:, 0:osz], wtb[:, 0:osz], rrep[:, 0:osz], op=Alu.mult)
            v.drain()
            v.tensor_copy(wtb[:, 0:osz].bitcast(I32), wtb[:, 0:osz])
            v.drain()
            if t == 0 and dest_wait_tt is not None:
                v.wait_ge(tt.h, dest_wait_tt)
            v.tensor_tensor(dest_sb[:, t * osz:(t + 1) * osz],
                            wtb[:, 0:osz].bitcast(I32), srep[:, 0:osz], op=Alu.mult)
            v.drain()
            tmile.append(sv.inc(v.nop()))

    dq_stack = ExitStack()
    dq_bufs = make_dq_bufs(dq_stack, I3, "a")

    p8_stack = ExitStack()
    P8 = p8_stack.enter_context
    fstage = P8(nc.sbuf_tensor("fstage", [128, 2 * 2 * S], F16))
    maxcols = P8(nc.sbuf_tensor("maxcols", [128, 2 * KI * TC], F32))

    def up_gate_pass(wsb, spill, maxbase):
        tt_b, ss_b, sv_b = tt.n, ss.n, sv.n
        sp_b = [sp0.n, sp1.n]
        te.wait_ge(ss.h, ss_b)
        te.wait_ge(sv.h, sv_b)
        sc.wait_ge(sp0.h, sp0.n)
        sc.wait_ge(sp1.h, sp1.n)
        sc.wait_ge(sv.h, sv_b)
        g = 0
        GSZ = 2
        NG = KI // GSZ
        for g4 in range(NG):
            par = g4 % 2
            spar = sp0 if par == 0 else sp1
            stg = fstage[:, par * GSZ * S: (par + 1) * GSZ * S]
            for oti in range(GSZ):
                ot = g4 * GSZ + oti
                for tc in range(TC):
                    ps = pA if g % 2 == 0 else pB
                    if g >= 2:
                        te.wait_ge(ss.h, ss_b + g - 1)
                    for ki in range(KD):
                        mi = te.matmul(ps[:],
                                  wsb[:, ki * I3 + ot * 128: ki * I3 + ot * 128 + 128],
                                  xq[:, TX(ki, tc): TX(ki, tc) + 512],
                                  start=(ki == 0), stop=(ki == KD - 1))
                    tt.inc(mi)
                    sc.wait_ge(tt.h, tt_b + g + 1)
                    if g4 >= 2 and oti == 0 and tc == 0:
                        sc.wait_ge(spar.h, sp_b[par] + 16 * (g4 // 2))
                        sc.wait_ge(sv.h, sv_b + (g4 - 1) * GSZ * TC)
                    ss.inc(sc.copy(stg[:, oti * S + tc * 512: oti * S + (tc + 1) * 512],
                                   ps[:]))
                    v.wait_ge(ss.h, ss_b + g + 1)
                    v.tensor_reduce(
                        maxcols[:, maxbase + ot * TC + tc: maxbase + ot * TC + tc + 1],
                        stg[:, oti * S + tc * 512: oti * S + (tc + 1) * 512],
                        axis=AX.X, op=Alu.max, apply_absolute_value=True)
                    v.drain()
                    sv.inc(v.nop())
                    g += 1
            sy.wait_ge(ss.h, ss_b + g)
            spar.inc(sy.dma_start(out=spill[:, g4 * GSZ * S: (g4 + 1) * GSZ * S],
                                  in_=stg[:]), 16)

    dequant_weight(wup_src, wupT, wx_bf, KD, I3, dq_bufs)
    if PHASE_LIMIT <= 4:
        debug_exit(wx_bf[:], KD * S, False)
        return nc, gstack
    up_gate_pass(wx_bf, up_sp, 0)
    dequant_weight(wg_src, wgT, wx_bf, KD, I3, dq_bufs, dest_wait_tt=tt.n)
    up_gate_pass(wx_bf, gate_sp, KI * TC)

    v.wait_ge(sp0.h, sp0.n)
    v.wait_ge(sp1.h, sp1.n)
    v.tensor_reduce(small[:, 20:21], maxcols[:, 0:KI * TC], axis=AX.X, op=Alu.max)
    v.tensor_reduce(small[:, 21:22], maxcols[:, KI * TC:], axis=AX.X, op=Alu.max)
    v.drain()
    n_ = sv.inc(v.nop())
    sy.wait_ge(sv.h, n_)
    dma(grow[0:1, 0:128], small[:, 20:21])
    n2 = dma(grow[0:1, 128:256], small[:, 21:22])
    v.wait_ge(ld.h, n2)
    v.tensor_reduce(ccst[0:1, 0:1], grow[0:1, 0:128], axis=AX.X, op=Alu.max)
    v.tensor_reduce(ccst[0:1, 1:2], grow[0:1, 128:256], axis=AX.X, op=Alu.max)
    v.drain()
    sv.inc(v.nop())
    exchange(1, 2)
    make_scale(ccst[0:1, 0:1], QMAX_A, 24, 25)
    make_scale(ccst[0:1, 1:2], QMAX_A, 26, 27)

    barrier()
    if PHASE_LIMIT <= 5:
        debug_exit(xq[:], KD * S, False)
        return nc, gstack
    p8_stack.close()
    dq_stack.close()
    ffn_stack.close()
    attn_stack.close()

    pass_stack = ExitStack()
    PS_ = pass_stack.enter_context
    gate2 = PS_(nc.sbuf_tensor("gate2", [128, KI * S], F16))
    ew_stack = ExitStack()
    EW = ew_stack.enter_context
    g16 = EW(nc.sbuf_tensor("g16", [128, 2 * 4 * S], F16))
    t32a = EW(nc.sbuf_tensor("t32a", [128, 2 * S], F32))
    t32b = EW(nc.sbuf_tensor("t32b", [128, S], F32))
    ti32 = EW(nc.sbuf_tensor("ti32", [128, S], I32))

    def quant3(dst_ap, src_ap, r_col, s_col, tmp_f, tmp_i):
        v.tensor_scalar(tmp_f, src_ap, small[:, r_col:r_col + 1], None, op0=Alu.mult)
        v.drain()
        v.tensor_copy(tmp_i, tmp_f)
        v.drain()
        v.tensor_scalar(dst_ap, tmp_i, small[:, s_col:s_col + 1], None, op0=Alu.mult)
        v.drain()

    def spill_load(g4, spill, pmile):
        par = g4 % 2
        buf = g16[:, par * 4 * S: par * 4 * S + 4 * S]
        eng = sy if par == 0 else sc
        sem = pa if par == 0 else pb
        if g4 >= 2:
            eng.wait_ge(sv.h, pmile[g4 - 2])
        else:
            eng.wait_ge(sv.h, sv.n)
        n_ = sem.inc(eng.dma_start(
            out=buf, in_=spill[:, g4 * 4 * S: (g4 + 1) * 4 * S]), 16)
        return buf, sem, n_

    pmile = []
    for g4 in range(KI // 4):
        gbuf, gsem, gn = spill_load(g4, gate_sp, pmile)
        v.wait_ge(gsem.h, gn)
        for oti in range(4):
            ot = g4 * 4 + oti
            gsl_in = gbuf[:, oti * S: (oti + 1) * S]
            apar = ot % 2
            ta = t32a[:, apar * S: apar * S + S]
            quant3(ta, gsl_in, 27, 26, t32b[:], ti32[:])
            n_ = sv.inc(v.nop())
            sc.wait_ge(sv.h, n_)
            n_s = ss.inc(sc.activation(t32b[:], ta, AF.Sigmoid, bias=0.0, scale=1.0))
            v.wait_ge(ss.h, n_s)
            v.tensor_scalar(t32b[:], t32b[:], 65536.0, None, op0=Alu.mult)
            v.drain()
            v.tensor_copy(ti32[:], t32b[:])
            v.drain()
            v.tensor_scalar(t32b[:], ti32[:], 65535.0, 1.0 / 65536.0,
                            op0=Alu.min, op1=Alu.mult)
            v.drain()
            v.tensor_tensor(gate2[:, ot * S:(ot + 1) * S], ta, t32b[:], op=Alu.mult)
            v.drain()
            v.tensor_reduce(small[:, 0:1], gate2[:, ot * S:(ot + 1) * S],
                            axis=AX.X, op=Alu.max, apply_absolute_value=True)
            v.drain()
            if ot == 0:
                v.tensor_copy(small[:, 1:2], small[:, 0:1])
            else:
                v.tensor_tensor(small[:, 1:2], small[:, 1:2], small[:, 0:1], op=Alu.max)
            v.drain()
        pmile.append(sv.inc(v.nop()))
    sy.wait_ge(sv.h, sv.n)
    ng = dma(grow[0:1, 0:128], small[:, 1:2])
    v.wait_ge(ld.h, ng)
    v.tensor_reduce(ccst[0:1, 0:1], grow[0:1, 0:128], axis=AX.X, op=Alu.max)
    v.drain()
    sv.inc(v.nop())
    exchange(2, 1)
    make_scale(ccst[0:1, 0:1], QMAX_A, 28, 29)

    pmile = []
    for g4 in range(KI // 4):
        gbuf, gsem, gn = spill_load(g4, up_sp, pmile)
        v.wait_ge(gsem.h, gn)
        for oti in range(4):
            ot = g4 * 4 + oti
            apar = ot % 2
            ta = t32a[:, apar * S: apar * S + S]
            quant3(ta, gbuf[:, oti * S: (oti + 1) * S], 25, 24, t32b[:], ti32[:])
            gsl = gate2[:, ot * S:(ot + 1) * S]
            v.tensor_scalar(t32b[:], gsl, small[:, 29:30], None, op0=Alu.mult)
            v.drain()
            v.tensor_copy(ti32[:], t32b[:])
            v.drain()
            v.tensor_scalar(t32b[:], ti32[:], small[:, 28:29], None, op0=Alu.mult)
            v.drain()
            v.tensor_tensor(gsl, ta, t32b[:], op=Alu.mult)
            v.drain()
            v.tensor_reduce(small[:, 0:1], gsl, axis=AX.X, op=Alu.max,
                            apply_absolute_value=True)
            v.drain()
            if ot == 0:
                v.tensor_copy(small[:, 2:3], small[:, 0:1])
            else:
                v.tensor_tensor(small[:, 2:3], small[:, 2:3], small[:, 0:1], op=Alu.max)
            v.drain()
        pmile.append(sv.inc(v.nop()))
    sy.wait_ge(sv.h, sv.n)
    ng = dma(grow[0:1, 0:128], small[:, 2:3])
    v.wait_ge(ld.h, ng)
    v.tensor_reduce(ccst[0:1, 0:1], grow[0:1, 0:128], axis=AX.X, op=Alu.max)
    v.drain()
    sv.inc(v.nop())
    exchange(3, 1)
    make_scale(ccst[0:1, 0:1], QMAX_A, 30, 31)

    for ot in range(KI):
        gsl = gate2[:, ot * S:(ot + 1) * S]
        quant3(gsl.bitcast(BF16), gsl, 31, 30, t32a[:, 0:S], ti32[:])
    sv.inc(v.nop())
    oq_all = gate2

    barrier()
    if PHASE_LIMIT <= 6:
        debug_exit(gate2[:].bitcast(BF16), KD * S, False)
        return nc, gstack
    ew_stack.close()

    wd_bf = PS_(nc.sbuf_tensor("wd_bf", [128, KI * D], BF16))
    dq2_stack = ExitStack()
    dq2_bufs = make_dq_bufs(dq2_stack, D, "b")
    dequant_weight(wd_src, wdT, wd_bf, KI, D, dq2_bufs)
    barrier()
    dq2_stack.close()

    out_stack = ExitStack()
    O_ = out_stack.enter_context
    ln1b = O_(nc.sbuf_tensor("ln1b", [128, TCW], F32))
    r2 = O_(nc.sbuf_tensor("r2", [128, KD * 512], F32))
    rbf2 = O_(nc.sbuf_tensor("rbf2", [128, 512], BF16))
    rsq2 = O_(nc.sbuf_tensor("rsq2", [128, 512], BF16))
    a2 = O_(nc.sbuf_tensor("a2", [128, 512], F32))
    b2 = O_(nc.sbuf_tensor("b2", [128, 512], F32))
    lrow2 = O_(nc.sbuf_tensor("lrow2", [1, 1536], F32))

    lmile = []
    for tc in range(TC):
        v.wait_ge(st.h, st.n)
        lpar = 0
        lsl = ln1b[:]
        lsem = pa
        sy.wait_ge(sv.h, sv.n)
        nl1 = lsem.inc(sy.dma_start(
            out=lsl, in_=ln1_sp[:, tc * TCW: (tc + 1) * TCW]), 16)
        for do in range(KD):
            ps = pA if do % 2 == 0 else pB
            if do >= 2:
                te.wait_ge(sv.h, sv.n)
            for ot in range(KI):
                mi = te.matmul(ps[:],
                          wd_bf[:, ot * D + do * 128: ot * D + do * 128 + 128],
                          oq_all[:, ot * S + tc * 512: ot * S + tc * 512 + 512]
                          .bitcast(BF16),
                          start=(ot == 0), stop=(ot == KI - 1))
            n_grp = tt.inc(mi)
            v.wait_ge(tt.h, n_grp)
            v.wait_ge(lsem.h, nl1)
            rsl = r2[:, do * 512:(do + 1) * 512]
            v.tensor_tensor(rsl, ps[:],
                            ln1b[:, do * 512: do * 512 + 512], op=Alu.add)
            v.drain()
            v.tensor_copy(rbf2[:], rsl)
            v.tensor_tensor(rsq2[:], rsl, rsl, op=Alu.mult)
            v.drain()
            n_r = sv.inc(v.nop())
            te.wait_ge(sv.h, n_r)
            te.matmul(pS1[:], ones_col[:, 0:1], rbf2[:],
                      start=(do == 0), stop=(do == KD - 1))
            mi = te.matmul(pS2[:], ones_col[:, 0:1], rsq2[:],
                      start=(do == 0), stop=(do == KD - 1))
            tt.inc(mi)
        v.wait_ge(tt.h, tt.n)
        mrow = lrow2[0:1, 0:512]
        urow = lrow2[0:1, 512:1024]
        brow = lrow2[0:1, 1024:1536]
        v.tensor_scalar(mrow, pS1[:], 1.0 / D, None, op0=Alu.mult)
        v.drain()
        v.tensor_tensor(urow, mrow, mrow, op=Alu.mult)
        v.drain()
        v.scalar_tensor_tensor(urow, pS2[:], 1.0 / D, urow,
                               op0=Alu.mult, op1=Alu.subtract)
        v.drain()
        n_v = sv.inc(v.nop())
        sc.wait_ge(sv.h, n_v)
        n_s = ss.inc(sc.activation(urow, urow, AF.Sqrt, bias=small[0:1, 34:35], scale=1.0))
        v.wait_ge(ss.h, n_s)
        v.reciprocal(urow, urow)
        v.drain()
        v.scalar_tensor_tensor(brow, mrow, -1.0, urow, op0=Alu.mult, op1=Alu.mult)
        v.drain()
        n_ab = sv.inc(v.nop())
        gp.wait_ge(sv.h, n_ab)
        dma_g(a2[:], urow.unsqueeze(1).broadcast_to([1, 128, 512]))
        nbb = dma_g(b2[:], brow.unsqueeze(1).broadcast_to([1, 128, 512]))
        v.wait_ge(ldg.h, nbb)
        r23 = r2[:].rearrange("p (k f) -> p k f", f=512)
        v.tensor_tensor(r23, r23, a2[:].unsqueeze(1).broadcast_to([128, KD, 512]),
                        op=Alu.mult)
        v.drain()
        v.tensor_tensor(r23, r23, b2[:].unsqueeze(1).broadcast_to([128, KD, 512]),
                        op=Alu.add)
        v.drain()
        n_fin = sv.inc(v.nop())
        lmile.append(n_fin)
        sy.wait_ge(sv.h, n_fin)
        dma_out(yout[:, tc * TCW: (tc + 1) * TCW], r2[:])

    sy.wait_ge(st.h, st.n)
    sy.wait_ge(lds.h, lds.n)
    sy.wait_ge(ldg.h, ldg.n)
    sy.wait_ge(ld.h, ld.n)
    sy.wait_ge(sp0.h, sp0.n)
    sy.wait_ge(sp1.h, sp1.n)

    out_stack.close()
    pass_stack.close()
    return nc, gstack


def _prep_inputs(L0_attn_x, L0_attn_Wq, L0_attn_bq, L0_attn_Wk, L0_attn_bk,
                 L0_attn_Wv, L0_attn_bv, L0_attn_Wm, L0_attn_bm,
                 L0_ffn_w_gate, L0_ffn_w_up, L0_ffn_w_down):
    def feat_major(w):
        din, n = w.shape
        return np.ascontiguousarray(
            w.reshape(KD, 128, n).transpose(1, 0, 2).reshape(128, KD * n))

    def wT_layout(w, nk):
        O, I = w.shape
        wt = np.ascontiguousarray(np.asarray(w).T)
        return np.ascontiguousarray(
            wt.reshape(nk, 128, O).transpose(1, 0, 2).reshape(128, nk * O))

    def src_layout(w):
        O, I = w.shape
        nblk = I // 32
        o4 = O // 4
        wb = np.asarray(w, np.float32).reshape(O, nblk, 32)
        out = np.zeros((128, 24576), np.float32)
        for ib in range(nblk):
            t, g = ib // 4, ib % 4
            t3, trow = t // 8, t % 8
            for s4 in range(4):
                p_src = 16 * trow + 4 * s4 + g
                seg = wb[s4 * o4:(s4 + 1) * o4, ib, :]
                base = t3 * (o4 * 32)
                out[p_src, base:base + o4 * 32] = seg.reshape(-1)
        return out

    f32 = lambda a: np.asarray(a, np.float32)
    common = dict(
        wq=feat_major(f32(L0_attn_Wq)), wk=feat_major(f32(L0_attn_Wk)),
        wv=feat_major(f32(L0_attn_Wv)), wm=feat_major(f32(L0_attn_Wm)),
        bqc=np.ascontiguousarray(f32(L0_attn_bq).reshape(KD, 128).T),
        bkc=np.ascontiguousarray(f32(L0_attn_bk).reshape(KD, 128).T),
        bmc=np.ascontiguousarray(f32(L0_attn_bm).reshape(KD, 128).T),
        bvr=f32(L0_attn_bv).reshape(1, D),
        wupT=wT_layout(f32(L0_ffn_w_up), KD),
        wgT=wT_layout(f32(L0_ffn_w_gate), KD),
        wdT=wT_layout(f32(L0_ffn_w_down), KI),
        wup_src=src_layout(f32(L0_ffn_w_up)),
        wg_src=src_layout(f32(L0_ffn_w_gate)),
        wd_src=src_layout(f32(L0_ffn_w_down)),
    )
    x = f32(L0_attn_x)
    in_maps = []
    for b in range(8):
        xb = np.ascontiguousarray(
            x[b].T.reshape(KD, 128, TC, 512).transpose(1, 2, 0, 3)
            .reshape(128, KD * S))
        m = dict(common)
        m["xin"] = xb
        in_maps.append(m)
    return in_maps


def kernel(L0_attn_x, L0_attn_Wq, L0_attn_bq, L0_attn_Wk, L0_attn_bk,
           L0_attn_Wv, L0_attn_bv, L0_attn_Wm, L0_attn_bm,
           L0_ffn_w_gate, L0_ffn_w_up, L0_ffn_w_down):
    in_maps = _prep_inputs(L0_attn_x, L0_attn_Wq, L0_attn_bq, L0_attn_Wk, L0_attn_bk,
                           L0_attn_Wv, L0_attn_bv, L0_attn_Wm, L0_attn_bm,
                           L0_ffn_w_gate, L0_ffn_w_up, L0_ffn_w_down)
    nc, gstack = build_program()
    try:
        res = run_bass_kernel_spmd(nc, in_maps, core_ids=list(range(8)))
    finally:
        try:
            gstack.close()
        except Exception:
            pass
    outs = []
    for b in range(8):
        y = res.results[b]["yout"]
        y = (y.reshape(128, TC, KD, 512).transpose(1, 3, 2, 0)
             .reshape(S, D))
        outs.append(y)
    return np.ascontiguousarray(np.stack(outs, 0)).astype(np.float32)


# version 09
# speedup vs baseline: 1.2507x; 1.0036x over previous
import numpy as np
from contextlib import ExitStack

import concourse.bass as bass
import concourse.mybir as mybir
from concourse.bass_utils import run_bass_kernel_spmd

F32 = mybir.dt.float32
I32 = mybir.dt.int32
F16 = mybir.dt.float16
BF16 = mybir.dt.bfloat16
Alu = mybir.AluOpType
AF = mybir.ActivationFunctionType
AX = mybir.AxisListType

D = 1024
S = 2048
H = 16
I3 = 3072
KD = D // 128
KI = I3 // 128
TC = S // 512
NKT = S // 128
QMAX_A = 32767.0
EPS = 1e-5
TCW = KD * 512


def TX(k, tc):
    return tc * TCW + k * 512
PHASE_LIMIT = 99


class Sem:
    def __init__(self, nc, stack, name):
        self.h = stack.enter_context(nc.semaphore(name))
        self.n = 0

    def inc(self, inst, k=1):
        inst.then_inc(self.h, k)
        self.n += k
        return self.n


def build_program():
    nc = bass.Bass("TRN2", target_bir_lowering=False)
    gstack = ExitStack()
    G = gstack.enter_context

    xin = nc.dram_tensor("xin", [128, KD * S], F32, kind="ExternalInput").ap()
    wq_h = nc.dram_tensor("wq", [128, KD * D], F32, kind="ExternalInput").ap()
    wk_h = nc.dram_tensor("wk", [128, KD * D], F32, kind="ExternalInput").ap()
    wv_h = nc.dram_tensor("wv", [128, KD * D], F32, kind="ExternalInput").ap()
    wm_h = nc.dram_tensor("wm", [128, KD * D], F32, kind="ExternalInput").ap()
    bqc = nc.dram_tensor("bqc", [128, KD], F32, kind="ExternalInput").ap()
    bkc = nc.dram_tensor("bkc", [128, KD], F32, kind="ExternalInput").ap()
    bmc = nc.dram_tensor("bmc", [128, KD], F32, kind="ExternalInput").ap()
    bvr = nc.dram_tensor("bvr", [1, D], F32, kind="ExternalInput").ap()
    wupT = nc.dram_tensor("wupT", [128, KD * I3], F32, kind="ExternalInput").ap()
    wgT = nc.dram_tensor("wgT", [128, KD * I3], F32, kind="ExternalInput").ap()
    wdT = nc.dram_tensor("wdT", [128, KI * D], F32, kind="ExternalInput").ap()
    wup_src = nc.dram_tensor("wup_src", [128, 24576], F32, kind="ExternalInput").ap()
    wg_src = nc.dram_tensor("wg_src", [128, 24576], F32, kind="ExternalInput").ap()
    wd_src = nc.dram_tensor("wd_src", [128, 24576], F32, kind="ExternalInput").ap()
    yout = nc.dram_tensor("yout", [128, KD * S], F32, kind="ExternalOutput").ap()
    ln1_sp = nc.dram_tensor("ln1_sp", [128, KD * S], F32, kind="Internal").ap()
    up_sp = nc.dram_tensor("up_sp", [128, KI * S], F16, kind="Internal").ap()
    gate_sp = nc.dram_tensor("gate_sp", [128, KI * S], F16, kind="Internal").ap()
    cc_bufs = [(nc.dram_tensor(f"cc{i}i", [1, 2], F32, kind="Internal").ap(),
                nc.dram_tensor(f"cc{i}o", [1, 2], F32, kind="Internal",
                               addr_space="Shared").ap()) for i in range(4)]

    v, sc, te, gp, sy = nc.vector, nc.scalar, nc.tensor, nc.gpsimd, nc.sync

    p = ExitStack()
    ld = Sem(nc, gstack, "ld")
    ldg = Sem(nc, gstack, "ldg")
    lds = Sem(nc, gstack, "lds")
    st = Sem(nc, gstack, "st")
    sp0 = Sem(nc, gstack, "sp0")
    sp1 = Sem(nc, gstack, "sp1")
    pa = Sem(nc, gstack, "pa")
    pb = Sem(nc, gstack, "pb")
    pc = Sem(nc, gstack, "pc")
    pd = Sem(nc, gstack, "pd")
    sv = Sem(nc, gstack, "sv")
    ss = Sem(nc, gstack, "ss")
    tt = Sem(nc, gstack, "tt")
    gg = Sem(nc, gstack, "gg")
    gw = Sem(nc, gstack, "gw")
    ccs = Sem(nc, gstack, "ccs")

    def dma(out, in_):
        return ld.inc(sy.dma_start(out=out, in_=in_), 16)

    def dma_g(out, in_):
        return ldg.inc(gp.dma_start(out=out, in_=in_), 16)

    def dma_s(out, in_):
        return lds.inc(sc.dma_start(out=out, in_=in_), 16)

    def dma_out(out, in_):
        return st.inc(sy.dma_start(out=out, in_=in_), 16)

    def barrier():
        engs = [te, v, sc, gp, sy]
        base = gg.n
        for e in engs:
            gg.inc(e.nop(), 1)
        for e in engs:
            e.wait_ge(gg.h, base + len(engs))

    def debug_exit(src_ap, nfree, is_f32):
        barrier()
        dbg = nc.sbuf_tensor("dbgbuf", [128, 2048], F32, side="right")
        dbgh = dbg.__enter__()
        nch = min(nfree, KD * S) // 2048
        for c in range(nch):
            if c > 0:
                sy.wait_ge(sv.h, sv.n)
            v.wait_ge(st.h, st.n)
            v.tensor_copy(dbgh[:], src_ap[:, c * 2048:(c + 1) * 2048])
            v.drain()
            n_ = sv.inc(v.nop())
            sy.wait_ge(sv.h, n_)
            dma_out(yout[:, c * 2048:(c + 1) * 2048], dbgh[:])
        sy.wait_ge(st.h, st.n)

    small = G(nc.sbuf_tensor("small", [128, 40], F32))
    ccst = G(nc.sbuf_tensor("ccst", [1, 8], F32))
    grow = G(nc.sbuf_tensor("grow", [1, 256], F32))
    ones_col = G(nc.sbuf_tensor("ones_col", [128, 2], BF16))
    bstage = G(nc.sbuf_tensor("bstage", [128, 3 * KD], F32))

    pAB = G(nc.psum_tensor("pAB", [128, 1024], F32))
    pCD = G(nc.psum_tensor("pCD", [128, 1024], F32))
    pA, pB = pAB[:, 0:512], pAB[:, 512:1024]
    pC, pD = pCD[:, 0:512], pCD[:, 512:1024]
    pV0 = G(nc.psum_tensor("pV0", [66, 512], F32))
    pV1 = G(nc.psum_tensor("pV1", [66, 512], F32))
    pS1 = G(nc.psum_tensor("pS1", [1, 512], F32))
    pS2 = G(nc.psum_tensor("pS2", [1, 512], F32))

    dma(bstage[:, 0:KD], bqc)
    dma(bstage[:, KD:2 * KD], bkc)
    dma(bstage[:, 2 * KD:3 * KD], bmc)
    v.memset(ones_col[:], 1.0)
    v.memset(small[:, 34:35], EPS)
    v.memset(ccst[:], 0.0)
    v.drain()
    sv.inc(v.nop())

    import ml_dtypes as _mld
    _e = np.zeros((128, 32 * 128), np.float32)
    for vv in range(32):
        for pp in range(128):
            kk = 4 * vv + pp // 32
            if kk < 128:
                _e[kk, vv * 128 + pp] = 1.0
    e_dram = nc.inline_tensor(_e.astype(_mld.bfloat16), name="e_repl").ap()
    e_stack = ExitStack()
    e_all = e_stack.enter_context(
        nc.sbuf_tensor("e_all", [128, 32 * 128], BF16, side="right"))
    dma(e_all[:], e_dram)

    attn_stack = ExitStack()
    attnT = attn_stack.enter_context(
        nc.sbuf_tensor("attnT", [128, KD * S], BF16, side="right"))
    xq = attnT

    qkv_stack = ExitStack()
    Q_ = qkv_stack.enter_context
    qT = Q_(nc.sbuf_tensor("qT", [128, KD * S], BF16))
    kT = Q_(nc.sbuf_tensor("kT", [128, KD * S], BF16))
    V66 = Q_(nc.sbuf_tensor("V66", [128, NKT * 1056], BF16))

    x_stack = ExitStack()
    X_ = x_stack.enter_context
    xbf = X_(nc.sbuf_tensor("xbf", [128, KD * S], BF16))
    wproj = X_(nc.sbuf_tensor("wproj", [128, KD * D], BF16))
    wstage = X_(nc.sbuf_tensor("wstage", [128, 2 * 1024], F32))
    bvrep = X_(nc.sbuf_tensor("bvrep", [128, D], F32))

    dma(bvrep[:], bvr.unsqueeze(1).broadcast_to([1, 128, D]))
    sy.wait_ge(ld.h, ld.n)

    def load_weight_bf16(src_ap, dest_sb, nfree, csz=1024):
        mile = []
        for c in range(nfree // csz):
            par = c % 2
            buf = wstage[:, par * 1024: par * 1024 + csz]
            eng = sy if par == 0 else sc
            sem = pa if par == 0 else pb
            if c >= 2:
                eng.wait_ge(sv.h, mile[c - 2])
            else:
                eng.wait_ge(sv.h, sv.n)
            n_ = sem.inc((sy if par == 0 else sc).dma_start(
                out=buf, in_=src_ap[:, c * csz:(c + 1) * csz]), 16)
            v.wait_ge(sem.h, n_)
            v.tensor_copy(dest_sb[:, c * csz:(c + 1) * csz], buf)
            v.drain()
            mile.append(sv.inc(v.nop()))
        return sv.n

    load_weight_bf16(xin, xbf, KD * S)

    def qk_proj(wsrc, bias_col, dest):
        v.wait_ge(tt.h, tt.n)
        wready = load_weight_bf16(wsrc, wproj, KD * D)
        te.wait_ge(sv.h, wready)
        ss_b = ss.n
        te.wait_ge(ss.h, ss_b)
        sc.wait_ge(ld.h, 5 * 16)
        tt_b = tt.n
        g = 0
        for do in range(KD):
            for tc in range(TC):
                ps = pA if g % 2 == 0 else pB
                if g >= 2:
                    te.wait_ge(ss.h, ss_b + g - 1)
                for ki in range(KD):
                    mi = te.matmul(ps[:],
                              wproj[:, ki * D + do * 128: ki * D + do * 128 + 128],
                              xbf[:, TX(ki, tc): TX(ki, tc) + 512],
                              start=(ki == 0), stop=(ki == KD - 1))
                tt.inc(mi)
                sc.wait_ge(tt.h, tt_b + g + 1)
                ss.inc(sc.activation(
                    dest[:, TX(do, tc): TX(do, tc) + 512], ps[:],
                    AF.Identity,
                    bias=bstage[:, bias_col + do: bias_col + do + 1], scale=1.0))
                g += 1

    qk_proj(wq_h, 0, qT)
    qk_proj(wk_h, KD, kT)

    v.wait_ge(tt.h, tt.n)
    wready = load_weight_bf16(wv_h, wproj, KD * D)
    gp.wait_ge(sv.h, sv.n)
    gw.inc(gp.memset(V66[:], 1.0))
    te.wait_ge(sv.h, wready)
    te.wait_ge(gw.h, gw.n)
    te.wait_ge(ss.h, ss.n)
    sv_b = sv.n
    tt_b = tt.n
    g = 0
    for tt_i in range(NKT):
        for dc in range(2):
            ps = pA if g % 2 == 0 else pB
            if g >= 2:
                te.wait_ge(sv.h, sv_b + g - 1)
            for ki in range(KD):
                mi = te.matmul(ps[:],
                          xbf[:, TX(ki, tt_i // 4) + (tt_i % 4) * 128:
                                 TX(ki, tt_i // 4) + (tt_i % 4) * 128 + 128],
                          wproj[:, ki * D + dc * 512: ki * D + dc * 512 + 512],
                          start=(ki == 0), stop=(ki == KD - 1))
            tt.inc(mi)
            v.wait_ge(tt.h, tt_b + g + 1)
            out_ap = V66[:, tt_i * 1056 + dc * 8 * 66: tt_i * 1056 + (dc * 8 + 8) * 66]
            out_ap = out_ap.rearrange("p (h c) -> p h c", c=66)[:, :, 0:64]
            ps3 = ps[:].rearrange("p (h c) -> p h c", c=64)
            bv3 = bvrep[:, dc * 512:(dc + 1) * 512].rearrange("p (h c) -> p h c", c=64)
            v.tensor_tensor(out_ap, ps3, bv3, op=Alu.add)
            v.drain()
            sv.inc(v.nop())
            g += 1

    barrier()
    if PHASE_LIMIT <= 1:
        debug_exit(qT[:], KD * S, False)
        return nc, gstack
    x_stack.close()

    a_stack = ExitStack()
    A_ = a_stack.enter_context
    probs = A_(nc.sbuf_tensor("probs", [128, 2 * NKT * 512], BF16))
    recband = A_(nc.sbuf_tensor("recband", [64, 2 * 512], F32))
    pvrec = A_(nc.sbuf_tensor("pvrec", [1, 2 * 512], F32))

    pv_norm_sv = [0, 0]
    pv_tt_par = [tt.n, tt.n]
    rec_sv_par = [sv.n, sv.n]
    ss_b4 = ss.n
    gpair = [0]
    exp_mile = []
    iters = [(h, qc) for h in range(H) for qc in range(TC)]

    def p4_scores(it):
        h, qc = iters[it]
        ko, po = h // 2, (h % 2) * 64
        par = it % 2
        pbase = par * (NKT * 512)
        kmax = 4 * (qc + 1)
        sc.wait_ge(tt.h, pv_tt_par[par])
        for kb in range(0, kmax, 2):
            j = gpair[0]
            pwide = pAB if j % 2 == 0 else pCD
            if j >= 2:
                te.wait_ge(ss.h, ss_b4 + j - 1)
            for half in range(2):
                kbb = kb + half
                kx = TX(ko, kbb // 4) + (kbb % 4) * 128
                mi = te.matmul(pwide[:, half * 512: half * 512 + 512],
                          kT[po:po + 64, kx: kx + 128],
                          qT[po:po + 64, TX(ko, qc): TX(ko, qc) + 512],
                          start=True, stop=True)
            tt.inc(mi)
            sc.wait_ge(tt.h, tt.n)
            ss.inc(sc.activation(probs[:, pbase + kb * 512: pbase + (kb + 2) * 512],
                                 pwide[:], AF.Exp, bias=0.0, scale=0.125))
            gpair[0] += 1
        exp_mile.append(ss_b4 + gpair[0])
        gp.wait_ge(ss.h, exp_mile[-1])
        for kb in [kb for kb in range(kmax) if kb * 128 + 127 > qc * 512]:
            mi_g = gp.affine_select(
                probs[:, pbase + kb * 512: pbase + (kb + 1) * 512],
                probs[:, pbase + kb * 512: pbase + (kb + 1) * 512],
                pattern=[[1, 512]], compare_op=Alu.is_ge, fill=0.0,
                base=qc * 512 - kb * 128, channel_multiplier=-1)
        return gw.inc(mi_g)

    def p4_pv(it, gw_mask):
        h, qc = iters[it]
        ko, po = h // 2, (h % 2) * 64
        par = it % 2
        pbase = par * (NKT * 512)
        kmax = 4 * (qc + 1)
        ps_pv = pV0 if par == 0 else pV1
        te.wait_ge(gw.h, gw_mask)
        te.wait_ge(sv.h, pv_norm_sv[par])
        for kb in range(kmax):
            mi = te.matmul(ps_pv[:],
                      V66[:, kb * 1056 + h * 66: kb * 1056 + h * 66 + 66],
                      probs[:, pbase + kb * 512: pbase + (kb + 1) * 512],
                      start=(kb == 0), stop=(kb == kmax - 1))
        pv_tt_par[par] = tt.inc(mi)
        rsl = pvrec[0:1, par * 512: par * 512 + 512]
        v.wait_ge(tt.h, pv_tt_par[par])
        v.reciprocal(rsl, ps_pv[64:65, :])
        v.drain()
        nrec = sv.inc(v.nop())
        gp.wait_ge(sv.h, nrec)
        gp.wait_ge(sv.h, rec_sv_par[par])
        rb = recband[:, par * 512: par * 512 + 512]
        nb = dma_g(rb, rsl.unsqueeze(1).broadcast_to([1, 64, 512]))
        v.wait_ge(ldg.h, nb)
        v.tensor_tensor(
            attnT[po:po + 64, TX(ko, qc): TX(ko, qc) + 512],
            ps_pv[0:64, :], rb, op=Alu.mult)
        v.drain()
        rec_sv_par[par] = pv_norm_sv[par] = sv.inc(v.nop())

    gw_prev = p4_scores(0)
    for it in range(len(iters)):
        gw_next = p4_scores(it + 1) if it + 1 < len(iters) else None
        p4_pv(it, gw_prev)
        gw_prev = gw_next

    barrier()
    if PHASE_LIMIT <= 2:
        debug_exit(attnT[:], KD * S, False)
        return nc, gstack
    a_stack.close()
    qkv_stack.close()

    p5_stack = ExitStack()
    P5 = p5_stack.enter_context
    wproj2 = P5(nc.sbuf_tensor("wproj2", [128, KD * D], BF16))
    wstage2 = P5(nc.sbuf_tensor("wstage2", [128, 2048], F32))
    rbuf = P5(nc.sbuf_tensor("rbuf", [128, KD * S], F32))
    xres = P5(nc.sbuf_tensor("xres", [128, 2 * TCW], F32))
    rbf = P5(nc.sbuf_tensor("rbf", [128, 512], BF16))
    rsq = P5(nc.sbuf_tensor("rsq", [128, 512], BF16))
    arep = P5(nc.sbuf_tensor("arep", [128, 512], F32))
    brep = P5(nc.sbuf_tensor("brep", [128, 512], F32))
    q32 = P5(nc.sbuf_tensor("q32", [128, TCW], F32))
    qi32 = P5(nc.sbuf_tensor("qi32", [128, TCW], I32))
    lrow = P5(nc.sbuf_tensor("lrow", [1, 1536], F32))

    def load_weight_bf16_p5(src_ap, dest_sb, nfree, csz=1024):
        mile = []
        for c in range(nfree // csz):
            par = c % 2
            buf = wstage2[:, par * 1024: par * 1024 + csz]
            eng = sy if par == 0 else sc
            sem = pa if par == 0 else pb
            if c >= 2:
                eng.wait_ge(sv.h, mile[c - 2])
            else:
                eng.wait_ge(sv.h, sv.n)
            n_ = sem.inc(eng.dma_start(out=buf, in_=src_ap[:, c * csz:(c + 1) * csz]), 16)
            v.wait_ge(sem.h, n_)
            v.tensor_copy(dest_sb[:, c * csz:(c + 1) * csz], buf)
            v.drain()
            mile.append(sv.inc(v.nop()))
        return sv.n

    load_weight_bf16_p5(wm_h, wproj2, KD * D)
    te.wait_ge(sv.h, sv.n)

    xmile = []
    for tc in range(TC):
        xeng = sy if tc % 2 == 0 else sc
        xsem = pa if tc % 2 == 0 else pb
        if tc >= 2:
            xeng.wait_ge(sv.h, xmile[tc - 2])
        else:
            xeng.wait_ge(sv.h, sv.n)
        nx = xsem.inc(xeng.dma_start(
            out=xres[:, (tc % 2) * TCW: (tc % 2) * TCW + TCW],
            in_=xin[:, tc * TCW: (tc + 1) * TCW]), 16)
        for do in range(KD):
            ps = pA if do % 2 == 0 else pB
            if do >= 2:
                te.wait_ge(sv.h, sv.n)
            for ki in range(KD):
                mi = te.matmul(ps[:],
                          wproj2[:, ki * D + do * 128: ki * D + do * 128 + 128],
                          attnT[:, TX(ki, tc): TX(ki, tc) + 512],
                          start=(ki == 0), stop=(ki == KD - 1))
            n_grp = tt.inc(mi)
            sc.wait_ge(tt.h, n_grp)
            n_ev = ss.inc(sc.activation(
                rbuf[:, TX(do, tc): TX(do, tc) + 512], ps[:],
                AF.Identity, bias=bstage[:, 2 * KD + do: 2 * KD + do + 1], scale=1.0))
            v.wait_ge(ss.h, n_ev)
            v.wait_ge(xsem.h, nx)
            v.wait_ge(tt.h, tt.n)
            rsl = rbuf[:, TX(do, tc): TX(do, tc) + 512]
            v.tensor_tensor(rsl, rsl,
                            xres[:, (tc % 2) * TCW + do * 512:
                                    (tc % 2) * TCW + do * 512 + 512], op=Alu.add)
            v.drain()
            v.tensor_copy(rbf[:], rsl)
            v.tensor_tensor(rsq[:], rsl, rsl, op=Alu.mult)
            v.drain()
            n_r = sv.inc(v.nop())
            te.wait_ge(sv.h, n_r)
            te.matmul(pS1[:], ones_col[:, 0:1], rbf[:],
                      start=(do == 0), stop=(do == KD - 1))
            mi = te.matmul(pS2[:], ones_col[:, 0:1], rsq[:],
                      start=(do == 0), stop=(do == KD - 1))
            tt.inc(mi)
        v.wait_ge(tt.h, tt.n)
        mrow = lrow[0:1, 0:512]
        urow = lrow[0:1, 512:1024]
        brow = lrow[0:1, 1024:1536]
        v.tensor_scalar(mrow, pS1[:], 1.0 / D, None, op0=Alu.mult)
        v.drain()
        v.tensor_tensor(urow, mrow, mrow, op=Alu.mult)
        v.drain()
        v.scalar_tensor_tensor(urow, pS2[:], 1.0 / D, urow,
                               op0=Alu.mult, op1=Alu.subtract)
        v.drain()
        n_v = sv.inc(v.nop())
        sc.wait_ge(sv.h, n_v)
        n_s = ss.inc(sc.activation(urow, urow, AF.Sqrt, bias=small[0:1, 34:35], scale=1.0))
        v.wait_ge(ss.h, n_s)
        v.reciprocal(urow, urow)
        v.drain()
        v.scalar_tensor_tensor(brow, mrow, -1.0, urow, op0=Alu.mult, op1=Alu.mult)
        v.drain()
        n_ab = sv.inc(v.nop())
        gp.wait_ge(sv.h, n_ab)
        dma_g(arep[:], urow.unsqueeze(1).broadcast_to([1, 128, 512]))
        nbb = dma_g(brep[:], brow.unsqueeze(1).broadcast_to([1, 128, 512]))
        v.wait_ge(ldg.h, nbb)
        win = rbuf[:, tc * TCW: (tc + 1) * TCW]
        win3 = win.rearrange("p (k f) -> p k f", f=512)
        v.tensor_tensor(win3, win3,
                        arep[:].unsqueeze(1).broadcast_to([128, KD, 512]),
                        op=Alu.mult)
        v.drain()
        v.tensor_tensor(win3, win3,
                        brep[:].unsqueeze(1).broadcast_to([128, KD, 512]),
                        op=Alu.add)
        v.drain()
        v.tensor_reduce(small[:, tc:tc + 1], win, axis=AX.X, op=Alu.max,
                        apply_absolute_value=True)
        v.drain()
        xmile.append(sv.inc(v.nop()))

    v.tensor_reduce(small[:, 8:9], small[:, 0:TC], axis=AX.X, op=Alu.max)
    v.drain()
    n_m = sv.inc(v.nop())
    sy.wait_ge(sv.h, n_m)
    ng = dma(grow[0:1, 0:128], small[:, 8:9])
    v.wait_ge(ld.h, ng)
    v.tensor_reduce(ccst[0:1, 0:1], grow[0:1, 0:128], axis=AX.X, op=Alu.max)
    v.drain()
    sv.inc(v.nop())

    def exchange(idx, nvals):
        ci, co = cc_bufs[idx]
        sy.wait_ge(sv.h, sv.n)
        nin = dma(ci[0:1, 0:2], ccst[0:1, 0:2])
        gp.wait_ge(ld.h, nin)
        ccs.inc(gp.collective_compute(
            "AllReduce", Alu.max, replica_groups=[[0, 1, 2, 3, 4, 5, 6, 7]],
            ins=[ci], outs=[co]))
        sy.wait_ge(ccs.h, ccs.n)
        nout = dma(ccst[0:1, 0:2], co[0:1, 0:2])
        v.wait_ge(ld.h, nout)

    def make_scale(src_ap, qmax, s_col, r_col):
        v.tensor_scalar(ccst[0:1, 4:5], src_ap, 1.0 / qmax, 1e-12,
                        op0=Alu.mult, op1=Alu.max)
        v.drain()
        v.reciprocal(ccst[0:1, 5:6], ccst[0:1, 4:5])
        v.drain()
        n_ = sv.inc(v.nop())
        sy.wait_ge(sv.h, n_)
        dma(small[:, s_col:s_col + 1],
            ccst[0:1, 4:5].unsqueeze(1).broadcast_to([1, 128, 1]))
        n2 = dma(small[:, r_col:r_col + 1],
                 ccst[0:1, 5:6].unsqueeze(1).broadcast_to([1, 128, 1]))
        v.wait_ge(ld.h, n2)
        sv.inc(v.nop())

    exchange(0, 1)
    make_scale(ccst[0:1, 0:1], QMAX_A, 16, 17)

    for tc in range(TC):
        rsl = rbuf[:, tc * TCW: (tc + 1) * TCW]
        v.tensor_scalar(q32[:], rsl, small[:, 17:18], None, op0=Alu.mult)
        v.drain()
        v.tensor_copy(qi32[:], q32[:])
        v.drain()
        v.tensor_scalar(xq[:, tc * TCW: (tc + 1) * TCW], qi32[:], small[:, 16:17],
                        None, op0=Alu.mult)
        v.drain()
    n_xq = sv.inc(v.nop())
    sy.wait_ge(sv.h, n_xq)
    n_ln1sp = dma_out(ln1_sp, rbuf[:])
    sy.wait_ge(st.h, n_ln1sp)

    barrier()
    if PHASE_LIMIT <= 3:
        debug_exit(rbuf[:], KD * S, True)
        return nc, gstack
    p5_stack.close()

    ffn_stack = ExitStack()
    FS = ffn_stack.enter_context
    wx_bf = FS(nc.sbuf_tensor("wx_bf", [128, KD * I3], BF16))

    def make_dq_bufs(dstack, osz, tag):
        DQ = dstack.enter_context
        return dict(
            sblk_s=DQ(nc.sbuf_tensor(f"sblk_s{tag}", [128, 768], F32)),
            sblk_r=DQ(nc.sbuf_tensor(f"sblk_r{tag}", [128, 768], F32)),
            dscr=DQ(nc.sbuf_tensor(f"dscr{tag}", [128, 768], F32)),
            shml=DQ(nc.sbuf_tensor(f"shml{tag}", [128, 3 * 768], BF16)),
            rhml=DQ(nc.sbuf_tensor(f"rhml{tag}", [128, 3 * 768], BF16)),
            wtb0=DQ(nc.sbuf_tensor(f"wtb0{tag}", [128, 3072], F32)),
            wtb1=DQ(nc.sbuf_tensor(f"wtb1{tag}", [128, 3072], F32)),
            srep0=DQ(nc.sbuf_tensor(f"srep0{tag}", [128, osz], F32)),
            srep1=(DQ(nc.sbuf_tensor(f"srep1{tag}", [128, osz], F32))
                   if osz >= 3072 else None),
            rrep0=DQ(nc.sbuf_tensor(f"rrep0{tag}", [128, osz], F32)),
            rrep1=(DQ(nc.sbuf_tensor(f"rrep1{tag}", [128, osz], F32))
                   if osz >= 3072 else None),
        )

    def dequant_weight(src_ap, wT_ap, dest_sb, n_it, osz, bufs, dest_wait_tt=None):
        sblk_s, sblk_r, dscr = bufs["sblk_s"], bufs["sblk_r"], bufs["dscr"]
        shml, rhml = bufs["shml"], bufs["rhml"]
        wtbs = [bufs["wtb0"], bufs["wtb1"]]
        dbl = bufs["srep1"] is not None
        sreps = [bufs["srep0"], bufs["srep1"] if dbl else bufs["srep0"]]
        rreps = [bufs["rrep0"], bufs["rrep1"] if dbl else bufs["rrep0"]]
        o4 = osz // 4
        mile = []
        for ch in range(8):
            par = ch % 2
            buf = wtbs[par]
            eng = sy if par == 0 else sc
            sem = pa if par == 0 else pb
            if ch >= 2:
                eng.wait_ge(sv.h, mile[ch - 2])
            else:
                eng.wait_ge(sv.h, sv.n)
            n_ = sem.inc(eng.dma_start(
                out=buf[:], in_=src_ap[:, ch * 3072:(ch + 1) * 3072]), 16)
            v.wait_ge(sem.h, n_)
            v.tensor_reduce(sblk_s[:, ch * 96:(ch + 1) * 96],
                            buf[:].rearrange("p (b c) -> p b c", c=32),
                            axis=AX.X, op=Alu.max, apply_absolute_value=True)
            v.drain()
            mile.append(sv.inc(v.nop()))
        v.tensor_scalar(sblk_s[:], sblk_s[:], 1.0 / 7.0, 1e-12,
                        op0=Alu.mult, op1=Alu.max)
        v.drain()
        v.reciprocal(sblk_r[:], sblk_s[:])
        v.drain()
        for x_f32, hml in ((sblk_s, shml), (sblk_r, rhml)):
            v.tensor_copy(hml[:, 0:768], x_f32[:])
            v.drain()
            v.tensor_tensor(dscr[:], x_f32[:], hml[:, 0:768], op=Alu.subtract)
            v.drain()
            v.tensor_copy(hml[:, 768:1536], dscr[:])
            v.drain()
            v.tensor_tensor(dscr[:], dscr[:], hml[:, 768:1536], op=Alu.subtract)
            v.drain()
            v.tensor_copy(hml[:, 1536:2304], dscr[:])
            v.drain()
        n_hml = sv.inc(v.nop())
        te.wait_ge(sv.h, n_hml)
        tmile = []
        ss_db = ss.n
        nch = osz // 512
        evac_cnt = 0
        for t in range(n_it):
            par = t % 2
            t3 = 0 if n_it == KD else t // 8
            trow = t if n_it == KD else t % 8
            srep, rrep, wtb = sreps[par if dbl else 0], rreps[par if dbl else 0], wtbs[par]
            wsem = pa if par == 0 else pb
            weng = sy if par == 0 else sc
            back = 2 if dbl else 1
            if t >= back:
                weng.wait_ge(sv.h, tmile[t - back])
                sc.wait_ge(sv.h, tmile[t - back])
            else:
                weng.wait_ge(sv.h, sv.n)
                sc.wait_ge(sv.h, sv.n)
            nw = wsem.inc(weng.dma_start(
                out=wtb[:, 0:osz], in_=wT_ap[:, t * osz:(t + 1) * osz]), 16)
            for ch in range(nch):
                o_lo = ch * 512
                for hml, rep in ((shml, srep), (rhml, rrep)):
                    psr = pC if evac_cnt % 2 == 0 else pD
                    if evac_cnt >= 2:
                        te.wait_ge(ss.h, ss_db + evac_cnt - 1)
                    specs = []
                    for xi in range(3):
                        pos = o_lo
                        while pos < o_lo + 512:
                            s4 = pos // o4
                            seg = min(o_lo + 512, (s4 + 1) * o4) - pos
                            row = 16 * trow + 4 * s4
                            specs.append((xi, pos, seg, row // 4,
                                          t3 * o4 + (pos - s4 * o4)))
                            pos += seg
                    for si, (xi, pos, seg, v32, fo) in enumerate(specs):
                        mi = te.matmul(
                            psr[:, pos - o_lo: pos - o_lo + seg],
                            e_all[:, v32 * 128: v32 * 128 + 128],
                            hml[:, xi * 768 + fo: xi * 768 + fo + seg],
                            start=(si == 0), stop=(si == len(specs) - 1))
                    tt.inc(mi)
                    sc.wait_ge(tt.h, tt.n)
                    ss.inc(sc.copy(rep[:, o_lo:o_lo + 512], psr[:]))
                    evac_cnt += 1
            v.wait_ge(wsem.h, nw)
            v.wait_ge(ss.h, ss_db + evac_cnt)
            v.tensor_tensor(wtb[:, 0:osz], wtb[:, 0:osz], rrep[:, 0:osz], op=Alu.mult)
            v.drain()
            v.tensor_copy(wtb[:, 0:osz].bitcast(I32), wtb[:, 0:osz])
            v.drain()
            if t == 0 and dest_wait_tt is not None:
                v.wait_ge(tt.h, dest_wait_tt)
            v.tensor_tensor(dest_sb[:, t * osz:(t + 1) * osz],
                            wtb[:, 0:osz].bitcast(I32), srep[:, 0:osz], op=Alu.mult)
            v.drain()
            tmile.append(sv.inc(v.nop()))

    dq_stack = ExitStack()
    dq_bufs = make_dq_bufs(dq_stack, I3, "a")

    p8_stack = ExitStack()
    P8 = p8_stack.enter_context
    fstage = P8(nc.sbuf_tensor("fstage", [128, 2 * 2 * S], F16))
    maxcols = P8(nc.sbuf_tensor("maxcols", [128, 2 * KI * TC], F32))

    def up_gate_pass(wsb, spill, maxbase):
        tt_b, ss_b, sv_b = tt.n, ss.n, sv.n
        sp_b = [sp0.n, sp1.n]
        te.wait_ge(ss.h, ss_b)
        te.wait_ge(sv.h, sv_b)
        sc.wait_ge(sp0.h, sp0.n)
        sc.wait_ge(sp1.h, sp1.n)
        sc.wait_ge(sv.h, sv_b)
        g = 0
        GSZ = 2
        NG = KI // GSZ
        for g4 in range(NG):
            par = g4 % 2
            spar = sp0 if par == 0 else sp1
            stg = fstage[:, par * GSZ * S: (par + 1) * GSZ * S]
            for oti in range(GSZ):
                ot = g4 * GSZ + oti
                for tc in range(TC):
                    ps = pA if g % 2 == 0 else pB
                    if g >= 2:
                        te.wait_ge(ss.h, ss_b + g - 1)
                    for ki in range(KD):
                        mi = te.matmul(ps[:],
                                  wsb[:, ki * I3 + ot * 128: ki * I3 + ot * 128 + 128],
                                  xq[:, TX(ki, tc): TX(ki, tc) + 512],
                                  start=(ki == 0), stop=(ki == KD - 1))
                    tt.inc(mi)
                    sc.wait_ge(tt.h, tt_b + g + 1)
                    if g4 >= 2 and oti == 0 and tc == 0:
                        sc.wait_ge(spar.h, sp_b[par] + 16 * (g4 // 2))
                        sc.wait_ge(sv.h, sv_b + (g4 - 1) * GSZ * TC)
                    ss.inc(sc.copy(stg[:, oti * S + tc * 512: oti * S + (tc + 1) * 512],
                                   ps[:]))
                    v.wait_ge(ss.h, ss_b + g + 1)
                    v.tensor_reduce(
                        maxcols[:, maxbase + ot * TC + tc: maxbase + ot * TC + tc + 1],
                        stg[:, oti * S + tc * 512: oti * S + (tc + 1) * 512],
                        axis=AX.X, op=Alu.max, apply_absolute_value=True)
                    v.drain()
                    sv.inc(v.nop())
                    g += 1
            sy.wait_ge(ss.h, ss_b + g)
            spar.inc(sy.dma_start(out=spill[:, g4 * GSZ * S: (g4 + 1) * GSZ * S],
                                  in_=stg[:]), 16)

    dequant_weight(wup_src, wupT, wx_bf, KD, I3, dq_bufs)
    if PHASE_LIMIT <= 4:
        debug_exit(wx_bf[:], KD * S, False)
        return nc, gstack
    up_gate_pass(wx_bf, up_sp, 0)
    dequant_weight(wg_src, wgT, wx_bf, KD, I3, dq_bufs, dest_wait_tt=tt.n)
    up_gate_pass(wx_bf, gate_sp, KI * TC)

    v.wait_ge(sp0.h, sp0.n)
    v.wait_ge(sp1.h, sp1.n)
    v.tensor_reduce(small[:, 20:21], maxcols[:, 0:KI * TC], axis=AX.X, op=Alu.max)
    v.tensor_reduce(small[:, 21:22], maxcols[:, KI * TC:], axis=AX.X, op=Alu.max)
    v.drain()
    n_ = sv.inc(v.nop())
    sy.wait_ge(sv.h, n_)
    dma(grow[0:1, 0:128], small[:, 20:21])
    n2 = dma(grow[0:1, 128:256], small[:, 21:22])
    v.wait_ge(ld.h, n2)
    v.tensor_reduce(ccst[0:1, 0:1], grow[0:1, 0:128], axis=AX.X, op=Alu.max)
    v.tensor_reduce(ccst[0:1, 1:2], grow[0:1, 128:256], axis=AX.X, op=Alu.max)
    v.drain()
    sv.inc(v.nop())
    exchange(1, 2)
    make_scale(ccst[0:1, 0:1], QMAX_A, 24, 25)
    make_scale(ccst[0:1, 1:2], QMAX_A, 26, 27)

    barrier()
    if PHASE_LIMIT <= 5:
        debug_exit(xq[:], KD * S, False)
        return nc, gstack
    p8_stack.close()
    dq_stack.close()
    ffn_stack.close()
    attn_stack.close()

    pass_stack = ExitStack()
    PS_ = pass_stack.enter_context
    gate2 = PS_(nc.sbuf_tensor("gate2", [128, KI * S], F16))
    ew_stack = ExitStack()
    EW = ew_stack.enter_context
    g16 = EW(nc.sbuf_tensor("g16", [128, 2 * 4 * S], F16))
    t32a = EW(nc.sbuf_tensor("t32a", [128, 2 * S], F32))
    t32b = EW(nc.sbuf_tensor("t32b", [128, S], F32))
    ti32 = EW(nc.sbuf_tensor("ti32", [128, S], I32))

    def quant3(dst_ap, src_ap, r_col, s_col, tmp_f, tmp_i):
        v.tensor_scalar(tmp_f, src_ap, small[:, r_col:r_col + 1], None, op0=Alu.mult)
        v.drain()
        v.tensor_copy(tmp_i, tmp_f)
        v.drain()
        v.tensor_scalar(dst_ap, tmp_i, small[:, s_col:s_col + 1], None, op0=Alu.mult)
        v.drain()

    def spill_load(g4, spill, pmile):
        par = g4 % 2
        buf = g16[:, par * 4 * S: par * 4 * S + 4 * S]
        eng = sy if par == 0 else sc
        sem = pa if par == 0 else pb
        if g4 >= 2:
            eng.wait_ge(sv.h, pmile[g4 - 2])
        else:
            eng.wait_ge(sv.h, sv.n)
        n_ = sem.inc(eng.dma_start(
            out=buf, in_=spill[:, g4 * 4 * S: (g4 + 1) * 4 * S]), 16)
        return buf, sem, n_

    pmile = []
    for g4 in range(KI // 4):
        gbuf, gsem, gn = spill_load(g4, gate_sp, pmile)
        v.wait_ge(gsem.h, gn)
        for oti in range(4):
            ot = g4 * 4 + oti
            gsl_in = gbuf[:, oti * S: (oti + 1) * S]
            apar = ot % 2
            ta = t32a[:, apar * S: apar * S + S]
            quant3(ta, gsl_in, 27, 26, t32b[:], ti32[:])
            n_ = sv.inc(v.nop())
            sc.wait_ge(sv.h, n_)
            n_s = ss.inc(sc.activation(t32b[:], ta, AF.Sigmoid, bias=0.0, scale=1.0))
            v.wait_ge(ss.h, n_s)
            v.tensor_scalar(t32b[:], t32b[:], 65536.0, None, op0=Alu.mult)
            v.drain()
            v.tensor_copy(ti32[:], t32b[:])
            v.drain()
            v.tensor_scalar(t32b[:], ti32[:], 65535.0, 1.0 / 65536.0,
                            op0=Alu.min, op1=Alu.mult)
            v.drain()
            v.tensor_tensor(gate2[:, ot * S:(ot + 1) * S], ta, t32b[:], op=Alu.mult)
            v.drain()
            v.tensor_reduce(small[:, 0:1], gate2[:, ot * S:(ot + 1) * S],
                            axis=AX.X, op=Alu.max, apply_absolute_value=True)
            v.drain()
            if ot == 0:
                v.tensor_copy(small[:, 1:2], small[:, 0:1])
            else:
                v.tensor_tensor(small[:, 1:2], small[:, 1:2], small[:, 0:1], op=Alu.max)
            v.drain()
        pmile.append(sv.inc(v.nop()))
    sy.wait_ge(sv.h, sv.n)
    ng = dma(grow[0:1, 0:128], small[:, 1:2])
    v.wait_ge(ld.h, ng)
    v.tensor_reduce(ccst[0:1, 0:1], grow[0:1, 0:128], axis=AX.X, op=Alu.max)
    v.drain()
    sv.inc(v.nop())
    exchange(2, 1)
    make_scale(ccst[0:1, 0:1], QMAX_A, 28, 29)

    pmile = []
    for g4 in range(KI // 4):
        gbuf, gsem, gn = spill_load(g4, up_sp, pmile)
        v.wait_ge(gsem.h, gn)
        for oti in range(4):
            ot = g4 * 4 + oti
            apar = ot % 2
            ta = t32a[:, apar * S: apar * S + S]
            quant3(ta, gbuf[:, oti * S: (oti + 1) * S], 25, 24, t32b[:], ti32[:])
            gsl = gate2[:, ot * S:(ot + 1) * S]
            v.tensor_scalar(t32b[:], gsl, small[:, 29:30], None, op0=Alu.mult)
            v.drain()
            v.tensor_copy(ti32[:], t32b[:])
            v.drain()
            v.tensor_scalar(t32b[:], ti32[:], small[:, 28:29], None, op0=Alu.mult)
            v.drain()
            v.tensor_tensor(gsl, ta, t32b[:], op=Alu.mult)
            v.drain()
            v.tensor_reduce(small[:, 0:1], gsl, axis=AX.X, op=Alu.max,
                            apply_absolute_value=True)
            v.drain()
            if ot == 0:
                v.tensor_copy(small[:, 2:3], small[:, 0:1])
            else:
                v.tensor_tensor(small[:, 2:3], small[:, 2:3], small[:, 0:1], op=Alu.max)
            v.drain()
        pmile.append(sv.inc(v.nop()))
    sy.wait_ge(sv.h, sv.n)
    ng = dma(grow[0:1, 0:128], small[:, 2:3])
    v.wait_ge(ld.h, ng)
    v.tensor_reduce(ccst[0:1, 0:1], grow[0:1, 0:128], axis=AX.X, op=Alu.max)
    v.drain()
    sv.inc(v.nop())
    exchange(3, 1)
    make_scale(ccst[0:1, 0:1], QMAX_A, 30, 31)

    for ot in range(KI):
        gsl = gate2[:, ot * S:(ot + 1) * S]
        quant3(gsl.bitcast(BF16), gsl, 31, 30, t32a[:, 0:S], ti32[:])
    sv.inc(v.nop())
    oq_all = gate2

    barrier()
    if PHASE_LIMIT <= 6:
        debug_exit(gate2[:].bitcast(BF16), KD * S, False)
        return nc, gstack
    ew_stack.close()

    wd_bf = PS_(nc.sbuf_tensor("wd_bf", [128, KI * D], BF16))
    dq2_stack = ExitStack()
    dq2_bufs = make_dq_bufs(dq2_stack, D, "b")
    dequant_weight(wd_src, wdT, wd_bf, KI, D, dq2_bufs)
    barrier()
    dq2_stack.close()

    out_stack = ExitStack()
    O_ = out_stack.enter_context
    ln1b = O_(nc.sbuf_tensor("ln1b", [128, TCW], F32))
    r2 = O_(nc.sbuf_tensor("r2", [128, KD * 512], F32))
    rbf2 = O_(nc.sbuf_tensor("rbf2", [128, 512], BF16))
    rsq2 = O_(nc.sbuf_tensor("rsq2", [128, 512], BF16))
    a2 = O_(nc.sbuf_tensor("a2", [128, 512], F32))
    b2 = O_(nc.sbuf_tensor("b2", [128, 512], F32))
    lrow2 = O_(nc.sbuf_tensor("lrow2", [1, 1536], F32))

    lmile = []
    for tc in range(TC):
        v.wait_ge(st.h, st.n)
        lpar = 0
        lsl = ln1b[:]
        lsem = pa
        sy.wait_ge(sv.h, sv.n)
        nl1 = lsem.inc(sy.dma_start(
            out=lsl, in_=ln1_sp[:, tc * TCW: (tc + 1) * TCW]), 16)
        for do in range(KD):
            ps = pA if do % 2 == 0 else pB
            if do >= 2:
                te.wait_ge(sv.h, sv.n)
            for ot in range(KI):
                mi = te.matmul(ps[:],
                          wd_bf[:, ot * D + do * 128: ot * D + do * 128 + 128],
                          oq_all[:, ot * S + tc * 512: ot * S + tc * 512 + 512]
                          .bitcast(BF16),
                          start=(ot == 0), stop=(ot == KI - 1))
            n_grp = tt.inc(mi)
            v.wait_ge(tt.h, n_grp)
            v.wait_ge(lsem.h, nl1)
            rsl = r2[:, do * 512:(do + 1) * 512]
            v.tensor_tensor(rsl, ps[:],
                            ln1b[:, do * 512: do * 512 + 512], op=Alu.add)
            v.drain()
            v.tensor_copy(rbf2[:], rsl)
            v.tensor_tensor(rsq2[:], rsl, rsl, op=Alu.mult)
            v.drain()
            n_r = sv.inc(v.nop())
            te.wait_ge(sv.h, n_r)
            te.matmul(pS1[:], ones_col[:, 0:1], rbf2[:],
                      start=(do == 0), stop=(do == KD - 1))
            mi = te.matmul(pS2[:], ones_col[:, 0:1], rsq2[:],
                      start=(do == 0), stop=(do == KD - 1))
            tt.inc(mi)
        v.wait_ge(tt.h, tt.n)
        mrow = lrow2[0:1, 0:512]
        urow = lrow2[0:1, 512:1024]
        brow = lrow2[0:1, 1024:1536]
        v.tensor_scalar(mrow, pS1[:], 1.0 / D, None, op0=Alu.mult)
        v.drain()
        v.tensor_tensor(urow, mrow, mrow, op=Alu.mult)
        v.drain()
        v.scalar_tensor_tensor(urow, pS2[:], 1.0 / D, urow,
                               op0=Alu.mult, op1=Alu.subtract)
        v.drain()
        n_v = sv.inc(v.nop())
        sc.wait_ge(sv.h, n_v)
        n_s = ss.inc(sc.activation(urow, urow, AF.Sqrt, bias=small[0:1, 34:35], scale=1.0))
        v.wait_ge(ss.h, n_s)
        v.reciprocal(urow, urow)
        v.drain()
        v.scalar_tensor_tensor(brow, mrow, -1.0, urow, op0=Alu.mult, op1=Alu.mult)
        v.drain()
        n_ab = sv.inc(v.nop())
        gp.wait_ge(sv.h, n_ab)
        dma_g(a2[:], urow.unsqueeze(1).broadcast_to([1, 128, 512]))
        nbb = dma_g(b2[:], brow.unsqueeze(1).broadcast_to([1, 128, 512]))
        v.wait_ge(ldg.h, nbb)
        r23 = r2[:].rearrange("p (k f) -> p k f", f=512)
        v.tensor_tensor(r23, r23, a2[:].unsqueeze(1).broadcast_to([128, KD, 512]),
                        op=Alu.mult)
        v.drain()
        v.tensor_tensor(r23, r23, b2[:].unsqueeze(1).broadcast_to([128, KD, 512]),
                        op=Alu.add)
        v.drain()
        n_fin = sv.inc(v.nop())
        lmile.append(n_fin)
        sy.wait_ge(sv.h, n_fin)
        dma_out(yout[:, tc * TCW: (tc + 1) * TCW], r2[:])

    sy.wait_ge(st.h, st.n)
    sy.wait_ge(lds.h, lds.n)
    sy.wait_ge(ldg.h, ldg.n)
    sy.wait_ge(ld.h, ld.n)
    sy.wait_ge(sp0.h, sp0.n)
    sy.wait_ge(sp1.h, sp1.n)

    out_stack.close()
    pass_stack.close()
    return nc, gstack


def _prep_inputs(L0_attn_x, L0_attn_Wq, L0_attn_bq, L0_attn_Wk, L0_attn_bk,
                 L0_attn_Wv, L0_attn_bv, L0_attn_Wm, L0_attn_bm,
                 L0_ffn_w_gate, L0_ffn_w_up, L0_ffn_w_down):
    def feat_major(w):
        din, n = w.shape
        return np.ascontiguousarray(
            w.reshape(KD, 128, n).transpose(1, 0, 2).reshape(128, KD * n))

    def wT_layout(w, nk):
        O, I = w.shape
        wt = np.ascontiguousarray(np.asarray(w).T)
        return np.ascontiguousarray(
            wt.reshape(nk, 128, O).transpose(1, 0, 2).reshape(128, nk * O))

    def src_layout(w):
        O, I = w.shape
        nblk = I // 32
        o4 = O // 4
        wb = np.asarray(w, np.float32).reshape(O, nblk, 32)
        out = np.zeros((128, 24576), np.float32)
        for ib in range(nblk):
            t, g = ib // 4, ib % 4
            t3, trow = t // 8, t % 8
            for s4 in range(4):
                p_src = 16 * trow + 4 * s4 + g
                seg = wb[s4 * o4:(s4 + 1) * o4, ib, :]
                base = t3 * (o4 * 32)
                out[p_src, base:base + o4 * 32] = seg.reshape(-1)
        return out

    f32 = lambda a: np.asarray(a, np.float32)
    common = dict(
        wq=feat_major(f32(L0_attn_Wq)), wk=feat_major(f32(L0_attn_Wk)),
        wv=feat_major(f32(L0_attn_Wv)), wm=feat_major(f32(L0_attn_Wm)),
        bqc=np.ascontiguousarray(f32(L0_attn_bq).reshape(KD, 128).T),
        bkc=np.ascontiguousarray(f32(L0_attn_bk).reshape(KD, 128).T),
        bmc=np.ascontiguousarray(f32(L0_attn_bm).reshape(KD, 128).T),
        bvr=f32(L0_attn_bv).reshape(1, D),
        wupT=wT_layout(f32(L0_ffn_w_up), KD),
        wgT=wT_layout(f32(L0_ffn_w_gate), KD),
        wdT=wT_layout(f32(L0_ffn_w_down), KI),
        wup_src=src_layout(f32(L0_ffn_w_up)),
        wg_src=src_layout(f32(L0_ffn_w_gate)),
        wd_src=src_layout(f32(L0_ffn_w_down)),
    )
    x = f32(L0_attn_x)
    in_maps = []
    for b in range(8):
        xb = np.ascontiguousarray(
            x[b].T.reshape(KD, 128, TC, 512).transpose(1, 2, 0, 3)
            .reshape(128, KD * S))
        m = dict(common)
        m["xin"] = xb
        in_maps.append(m)
    return in_maps


def kernel(L0_attn_x, L0_attn_Wq, L0_attn_bq, L0_attn_Wk, L0_attn_bk,
           L0_attn_Wv, L0_attn_bv, L0_attn_Wm, L0_attn_bm,
           L0_ffn_w_gate, L0_ffn_w_up, L0_ffn_w_down):
    in_maps = _prep_inputs(L0_attn_x, L0_attn_Wq, L0_attn_bq, L0_attn_Wk, L0_attn_bk,
                           L0_attn_Wv, L0_attn_bv, L0_attn_Wm, L0_attn_bm,
                           L0_ffn_w_gate, L0_ffn_w_up, L0_ffn_w_down)
    nc, gstack = build_program()
    try:
        res = run_bass_kernel_spmd(nc, in_maps, core_ids=list(range(8)))
    finally:
        try:
            gstack.close()
        except Exception:
            pass
    outs = []
    for b in range(8):
        y = res.results[b]["yout"]
        y = (y.reshape(128, TC, KD, 512).transpose(1, 3, 2, 0)
             .reshape(S, D))
        outs.append(y)
    return np.ascontiguousarray(np.stack(outs, 0)).astype(np.float32)
